# version 04
# baseline (speedup 1.0000x reference)
import numpy as np
import ml_dtypes
from contextlib import ExitStack

import concourse.bass as bass
import concourse.mybir as mybir
from concourse.bass_utils import run_bass_kernel_spmd

F32 = mybir.dt.float32
BF16 = mybir.dt.bfloat16
AF = mybir.ActivationFunctionType
ALU = mybir.AluOpType
AX = mybir.AxisListType

NCORES = 8
ROWT = 4096
OWN = 2048
DM = 1024
DC = 8


class Sched:
    ENGINES = ("sync", "tensor", "vector", "scalar", "gpsimd")

    def __init__(self, nc, stack):
        self.nc = nc
        self.stack = stack
        self._log = {e: [] for e in self.ENGINES}
        self.dma_sems = set()
        self.dma_wmax = {}
        self.dma_barrier = {}
        self.ops = {e: [] for e in self.ENGINES}
        self.sems = {}
        self.cnt = {}
        self.wdone = {e: {} for e in self.ENGINES}

    def sem(self, name, dma=False):
        if name in self.sems:
            return name
        s = self.stack.enter_context(self.nc.semaphore(name))
        self.sems[name] = s
        self.cnt[name] = 0
        if dma:
            self.dma_sems.add(name)
        return name

    def barrier(self):
        marks = []
        if "mm" in self.sems:
            marks.append(self.op("tensor", lambda e: e.drain(), inc="mm"))
        for aname, eng in [("@vector", "vector"), ("@scalar", "scalar"),
                           ("@gpsimd", "gpsimd")]:
            if aname in self.sems and self.cnt[aname] > 0:
                marks.append((aname, self.cnt[aname], eng))
        for s in sorted(self.dma_sems):
            if self.cnt[s] > 0:
                marks.append((s, self.cnt[s], None))
        toks = []
        for e in self.ENGINES:
            toks.append(self.op(e, lambda en: en.nop(), wait=marks, inc=None))
        return toks

    AUTO = ("vector", "scalar", "gpsimd")

    def op(self, eng, build, wait=(), inc=None, inc_by=1):
        if eng in self.AUTO:
            aname = "@" + eng
            if aname not in self.sems:
                self.sem(aname)
            prev = self.cnt[aname]
            wait = list(wait) + ([(aname, prev, None)] if prev > 0 else [])
            inc, inc_by = aname, 1
        waits = []
        logwaits = []
        for tok in wait:
            if tok is None:
                continue
            sname, v, peng = tok
            if sname in self.dma_sems:
                v = self.cnt[sname]
                self.dma_wmax[sname] = max(self.dma_wmax.get(sname, 0), v)
            elif peng == eng:
                continue
            if v <= 0 or self.wdone[eng].get(sname, -1) >= v:
                continue
            self.wdone[eng][sname] = v
            waits.append((self.sems[sname], v))
            logwaits.append((sname, v))
        tok = None
        semh = None
        if inc is not None:
            self.cnt[inc] += inc_by
            tok = (inc, self.cnt[inc], eng)
            semh = self.sems[inc]

        def f(engine, waits=waits, build=build, semh=semh, inc_by=inc_by):
            for (s, v) in waits:
                engine.wait_ge(s, v)
            ins = build(engine)
            if semh is not None:
                ins.then_inc(semh, inc_by)

        self.ops[eng].append(f)
        self._log[eng].append((logwaits, None if inc is None else (inc, inc_by)))
        return tok

    def dma(self, eng, out, in_, wait=(), inc=None, slow=False):
        wait = list(wait)
        if inc in self.dma_sems:
            w = self.dma_wmax.get(inc, 0)
            if w > self.dma_barrier.get(inc, 0):
                self.dma_barrier[inc] = w
                wait.append((inc, w, None))
        if slow:
            return self.op(eng, lambda e, o=out, i=in_: e.dma_start(
                out=o, in_=i, allow_slow_non_contiguous=True),
                wait=wait, inc=inc, inc_by=16)
        return self.op(eng, lambda e, o=out, i=in_: e.dma_start(out=o, in_=i),
                       wait=wait, inc=inc, inc_by=16)

    def raw_wait(self, eng, sname, v):
        def f(engine, s=self.sems[sname], v=v):
            engine.wait_ge(s, v)
            engine.nop()
        self.ops[eng].append(f)
        self._log[eng].append(([(sname, v)], None))

    def check(self):
        import collections
        meta = {e: [] for e in self.ENGINES}
        qs = {e: collections.deque(self._log[e]) for e in self.ENGINES}
        val = collections.defaultdict(int)
        progress = True
        while progress:
            progress = False
            for e in self.ENGINES:
                while qs[e]:
                    waits, inc = qs[e][0]
                    if any(val[s] < v for (s, v) in waits):
                        break
                    if inc is not None:
                        val[inc[0]] += inc[1]
                    qs[e].popleft()
                    progress = True
        stuck = {e: len(qs[e]) for e in self.ENGINES if qs[e]}
        if stuck:
            for e in self.ENGINES:
                if qs[e]:
                    waits, inc = qs[e][0]
                    bad = [(s, v, val[s]) for (s, v) in waits if val[s] < v]
                    print(f"DEADLOCK {e}: waiting {bad}")
            raise RuntimeError(f"deadlock, remaining ops {stuck}")

    def finalize(self):
        self.check()
        blk = self.stack.enter_context(self.nc.Block())
        for e in self.ENGINES:
            lst = self.ops[e]
            if not lst:
                continue
            def runner(engine, lst=lst):
                for f in lst:
                    f(engine)
            getattr(blk, e)(runner)


_NAME_N = [0]


def _uniq():
    _NAME_N[0] += 1
    return "_u%d" % _NAME_N[0]


def fm_view(dram_ap):
    return dram_ap.rearrange("(c p) t -> p c t", p=128)


def emit_ln_fm(k, src_getter, n_tok, dst_put, ps_stat, ps_b, sq, rows,
               ones_col, ones_row1, ready_toks, tag, affine=None, eps_col=None,
               extra_dst=None):
    S, V, T, G = "scalar", "vector", "tensor", "gpsimd"
    nt = n_tok
    pm = ps_stat[0][0:1, 0:nt]
    pq = ps_stat[1][0:1, 0:nt]
    sq_toks = [None, None]
    mm_last = None
    for c in range(DC):
        xc = src_getter(c)
        t_sq = k.op(S, lambda e, s=sq[c % 2], x=xc, n=nt:
                    e.square(out=s[:, 0:n], in_=x),
                    wait=list(ready_toks) + [sq_toks[c % 2]], inc=tag + "s")
        k.op(T, lambda e, x=xc, st=(c == 0), sp=(c == DC - 1):
             e.matmul(pm, ones_col, x, start=st, stop=sp),
             wait=list(ready_toks), inc=None)
        mm_last = k.op(T, lambda e, s=sq[c % 2], st=(c == 0), sp=(c == DC - 1), n=nt:
                       e.matmul(pq, ones_col, s[:, 0:n], start=st, stop=sp),
                       wait=[t_sq], inc=tag + "t")
        sq_toks[c % 2] = mm_last
    r_mu = rows[0:1, 0:nt]
    r_rstd = rows[0:1, nt:2 * nt]
    r_tmp = rows[0:1, 2 * nt:3 * nt]
    r_mrs = rows[0:1, 3 * nt:4 * nt]
    r_mu2 = rows[0:1, 4 * nt:5 * nt]
    tv1 = k.op(V, lambda e: e.tensor_scalar_mul(r_mu, pm, 1.0 / DM),
               wait=[mm_last], inc=tag + "v")
    k.op(V, lambda e: e.tensor_scalar_mul(r_tmp, pq, 1.0 / DM), wait=[], inc=None)
    ts1 = k.op(S, lambda e: e.square(out=r_mu2, in_=r_mu), wait=[tv1], inc=tag + "s")
    tv2 = k.op(V, lambda e: e.tensor_sub(r_tmp, r_tmp, r_mu2), wait=[ts1], inc=tag + "v")
    ts2 = k.op(S, lambda e: e.activation(out=r_tmp, in_=r_tmp, func=AF.Ln,
                                         bias=eps_col), wait=[tv2], inc=tag + "s")
    ts3 = k.op(S, lambda e: e.activation(out=r_rstd, in_=r_tmp, func=AF.Exp, scale=-0.5),
               wait=[], inc=tag + "s")
    tv3 = k.op(V, lambda e: e.tensor_mul(r_mrs, r_mu, r_rstd), wait=[ts3], inc=tag + "v")
    rb = ps_b[0][:, 0:nt]
    mb = ps_b[1][:, 0:nt]
    tb1 = k.op(T, lambda e: e.matmul(rb, ones_row1, r_rstd, start=True, stop=True),
               wait=[ts3] + list(ready_toks), inc=tag + "t")
    tb2 = k.op(T, lambda e: e.matmul(mb, ones_row1, r_mrs, start=True, stop=True),
               wait=[tv3], inc=tag + "t")
    out_toks = []
    for c in range(DC):
        xc = src_getter(c)
        k.op(V, lambda e, x=xc, s=sq[c % 2], n=nt: e.tensor_mul(s[:, 0:n], x, rb),
             wait=[tb1], inc=None)
        if affine is None and extra_dst is None:
            to = k.op(V, lambda e, s=sq[c % 2], o=dst_put(c), n=nt:
                      e.tensor_sub(o, s[:, 0:n], mb), wait=[tb2], inc=tag + "v")
        elif affine is None:
            tf = k.op(V, lambda e, s=sq[c % 2], d=extra_dst(c), n=nt:
                      e.tensor_sub(d, s[:, 0:n], mb), wait=[tb2], inc=tag + "v")
            to = k.op(S, lambda e, d=extra_dst(c), o=dst_put(c):
                      e.copy(out=o, in_=d), wait=[tf], inc=tag + "s")
        else:
            lg_col, lb_col = affine(c)
            tf = k.op(V, lambda e, s=sq[c % 2], n=nt:
                      e.tensor_sub(s[:, 0:n], s[:, 0:n], mb), wait=[tb2], inc=tag + "v")
            to = k.op(V, lambda e, s=sq[c % 2], o=dst_put(c), lg=lg_col, lb=lb_col, n=nt:
                      e.tensor_scalar(out=o, in0=s[:, 0:n], scalar1=lg, scalar2=lb,
                                      op0=ALU.mult, op1=ALU.add),
                      wait=[tf], inc=tag + "v")
        out_toks.append(to)
    return out_toks


def emit_moe(nc, k, stack, xo, rwT, eb, w1, w3, w2, sw1, sw3, sw2, eye, y,
             cwscr):
    if True:
        sfx = _uniq()
        sb = lambda n, sh, dt: stack.enter_context(nc.sbuf_tensor(n + sfx, sh, dt))
        psal = lambda n, sh: stack.enter_context(nc.psum_tensor(n + sfx, sh, F32))
        hbf = sb("hbf", [128, DC, OWN], BF16)
        eo = sb("eo", [128, 16, 1024], BF16)
        macc = sb("macc", [128, DC, 1024], F32)
        w2bf = sb("w2bf", [128, 16, DM], BF16)
        w2st = sb("w2st", [128, 2, DM], F32)
        w1st = sb("w1st", [128, DC, 256], F32)
        w3st = sb("w3st", [128, DC, 256], F32)
        w1bf = sb("w1bf", [128, DC, 256], BF16)
        w3bf = sb("w3bf", [128, DC, 256], BF16)
        xt = sb("xt", [128, DC, 256], F32)
        sq = [sb("sq0", [128, 1024], F32), sb("sq1", [128, 1024], F32)]
        sg = sb("sg", [128, 1024], F32)
        tt_s = sb("tt_s", [128, 1024], F32)
        cwb_sb = sb("cwb_sb", [128, 1024], F32)
        cw_sb = sb("cw_sb", [128, 8], F32)
        m1c = sb("m1c", [128, 6], F32)
        eq1 = sb("eq1", [128, 8], F32)
        eq2 = sb("eq2", [128, 8], F32)
        cwrow = sb("cwrow", [1, 1024], F32)
        cwtmp = sb("cwtmp", [8, 128], F32)
        rows = sb("rows", [1, 6 * 256], F32)
        tksc = sb("tksc", [1, 512], F32)
        rwTs = sb("rwTs", [128, DC, 8], F32)
        rwTb = sb("rwTb", [128, DC, 8], BF16)
        ebs = sb("ebs", [1, 8], F32)
        ebb = sb("ebb", [1, 8], BF16)
        eye_sb = sb("eye_sb", [128, 128], F32)
        ones_col = sb("ones_col", [128, 1], F32)
        ones_row1 = sb("ones_row1", [1, 128], F32)
        ones_rbf = sb("ones_rbf", [1, 128], BF16)
        epscol = sb("epscol", [1, 1], F32)

        PGa = psal("PGa", [128, 512])
        PGb = psal("PGb", [128, 512])
        PUa = psal("PUa", [128, 512])
        PUb = psal("PUb", [128, 512])
        PCa = psal("PCa", [128, 512])
        PCb = psal("PCb", [128, 512])
        PWa = psal("PWa", [128, 512])
        PWb = psal("PWb", [128, 512])

        for s in ["lns", "lnt", "lnv", "lng",
                  "mm", "ev", "es", "eg", "cwv", "cws"]:
            k.sem(s)
        for s in ["ld", "wl0", "wl1", "w3l0", "w3l1", "st", "dtr"]:
            k.sem(s, dma=True)

        S, V, T, G, SY = "scalar", "vector", "tensor", "gpsimd", "sync"

        k.op(G, lambda e: e.memset(ones_col[:], 1.0), inc="eg")
        k.op(G, lambda e: e.memset(epscol[:], 1e-5), inc="eg")
        k.op(G, lambda e: e.memset(ones_row1[:], 1.0), inc=None)
        t_ones3 = k.op(G, lambda e: e.memset(ones_rbf[:], 1.0), inc="eg")
        t_ones = t_eps = t_ones3
        t_rw = k.dma(SY, rwTs[:], rwT.rearrange("(c p) e -> p c e", p=128), inc="ld")
        t_eb = k.dma(SY, ebs[:], eb.rearrange("(o e) -> o e", o=1), inc="ld")
        t_eye = k.dma(SY, eye_sb[:], eye, inc="ld")
        t_rwb = k.op(S, lambda e: e.copy(out=rwTb[:], in_=rwTs[:]), wait=[t_rw], inc="lns")
        t_ebb = k.op(S, lambda e: e.copy(out=ebb[:], in_=ebs[:]), wait=[t_eb], inc="lns")

        ln_done = []
        for sti in range(8):
            t0 = sti * 256
            t_x = k.dma(SY, xt[:], fm_view(xo)[:, :, t0:t0 + 256],
                        wait=[ln_done[-1] if ln_done else None], inc="ld")
            toks = emit_ln_fm(
                k, lambda c: xt[:, c, :], 256,
                lambda c, t0=t0: hbf[:, c, t0:t0 + 256],
                (PGa, PGb), (PUa, PUb), sq, rows, ones_col[:], ones_row1[:],
                [t_x, t_ones, t_eps], tag="ln", eps_col=epscol[0:1, 0:1])
            ln_done.extend(toks)
        ln_all = ln_done[-1]

        prev_pw_read = None
        dtr_last = None
        for q in range(16):
            for c in range(DC):
                k.op(T, lambda e, c=c, q=q: e.matmul(
                    PWa[:, 0:8], hbf[:, c, q * 128:q * 128 + 128], rwTb[:, c, :],
                    start=(c == 0), stop=False),
                    wait=[ln_all, t_rwb, prev_pw_read], inc=None)
            tl = k.op(T, lambda e: e.matmul(PWa[:, 0:8], ones_rbf[:], ebb[:],
                                            start=False, stop=True),
                      wait=[t_ebb, t_ones3], inc="mm")
            v1 = k.op(V, lambda e: e.reduce_max(m1c[:, 0:1], PWa[:, 0:8], axis=AX.X),
                      wait=[tl], inc="cwv")
            v2 = k.op(V, lambda e: e.tensor_scalar(out=eq1[:], in0=PWa[:, 0:8],
                                                   scalar1=m1c[:, 0:1], scalar2=None,
                                                   op0=ALU.is_equal), wait=[], inc="cwv")
            k.op(V, lambda e: e.scalar_tensor_tensor(
                out=sq[0][:, 0:8], in0=eq1[:], scalar=-1e30, in1=PWa[:, 0:8],
                op0=ALU.mult, op1=ALU.add), wait=[], inc=None)
            v4 = k.op(V, lambda e: e.reduce_max(m1c[:, 1:2], sq[0][:, 0:8], axis=AX.X),
                      wait=[], inc="cwv")
            v5 = k.op(V, lambda e: e.tensor_sub(m1c[:, 2:3], m1c[:, 1:2], m1c[:, 0:1]),
                      wait=[], inc=None)
            s1 = k.op(S, lambda e: e.activation(out=m1c[:, 4:5], in_=m1c[:, 2:3],
                                                func=AF.Sigmoid), wait=[v5], inc="cws")
            v6 = k.op(V, lambda e: e.tensor_scalar(out=m1c[:, 3:4], in0=m1c[:, 4:5],
                                                   scalar1=-1.0, scalar2=1.0,
                                                   op0=ALU.mult, op1=ALU.add),
                      wait=[s1], inc="cwv")
            v7 = k.op(V, lambda e: e.tensor_scalar(out=eq2[:], in0=PWa[:, 0:8],
                                                   scalar1=m1c[:, 1:2], scalar2=None,
                                                   op0=ALU.is_equal), wait=[], inc="cwv")
            prev_pw_read = v7
            k.op(V, lambda e: e.tensor_scalar(out=cw_sb[:], in0=eq1[:],
                                              scalar1=m1c[:, 3:4], scalar2=None,
                                              op0=ALU.mult), wait=[], inc=None)
            v9 = k.op(V, lambda e: e.scalar_tensor_tensor(
                out=cw_sb[:], in0=eq2[:], scalar=m1c[:, 4:5], in1=cw_sb[:],
                op0=ALU.mult, op1=ALU.add), wait=[], inc="cwv")
            tt1 = k.op(T, lambda e: e.transpose(PCa[0:8, 0:128], cw_sb[:], eye_sb[:]),
                       wait=[v9, t_eye, dtr_last], inc="mm")
            tt2 = k.op(S, lambda e: e.copy(out=cwtmp[:], in_=PCa[0:8, 0:128]),
                       wait=[tt1, dtr_last], inc="cws")
            dtr_last = k.dma(SY, cwscr[:, q * 128:q * 128 + 128], cwtmp[:],
                             wait=[tt2], inc="dtr")

        prev_ty = None
        last_w2mm = None
        prev_cwmm = None
        prev_st = None
        for tti in range(2):
            ts0 = tti * 1024
            t_ms = k.op(G, lambda e: e.memset(macc[:], 0.0),
                        wait=[prev_ty], inc="eg")
            for ei in range(9):
                shared = (ei == 8)
                w1_src = sw1 if shared else w1[ei]
                w3_src = sw3 if shared else w3[ei]
                w2_src = sw2 if shared else w2[ei]
                if not shared:
                    t_cw = k.dma(SY, cwrow[:], cwscr[ei:ei + 1, ts0:ts0 + 1024],
                                 wait=[dtr_last, prev_cwmm], inc="dtr")
                    tc1a = k.op(T, lambda e: e.matmul(
                        PCa[:, :], ones_row1[:], cwrow[0:1, 0:512],
                        start=True, stop=True), wait=[t_cw, last_w2mm], inc=None)
                    tc1b = k.op(T, lambda e: e.matmul(
                        PCb[:, :], ones_row1[:], cwrow[0:1, 512:1024],
                        start=True, stop=True), wait=[], inc="mm")
                    prev_cwmm = tc1b
                    tc2 = k.op(S, lambda e: e.copy(out=cwb_sb[:, 0:512], in_=PCa[:, :]),
                               wait=[tc1b], inc="cws")
                    tc2 = k.op(S, lambda e: e.copy(out=cwb_sb[:, 512:1024], in_=PCb[:, :]),
                               wait=[], inc="cws")
                else:
                    tc2 = None
                w2b_toks = []
                for hc in range(16):
                    t_l = k.dma(SY, w2st[:, hc % 2, :],
                                w2_src[hc * 128:hc * 128 + 128, :],
                                wait=[w2b_toks[-2] if len(w2b_toks) >= 2 else None,
                                      last_w2mm], inc="wl0" if hc % 2 == 0 else "wl1")
                    t_c = k.op(G, lambda e, hc=hc: e.tensor_copy(
                        w2bf[:, hc, :], w2st[:, hc % 2, :]),
                        wait=[t_l, last_w2mm], inc="eg")
                    w2b_toks.append(t_c)
                eo_toks = []
                prev_sg = prev_tt = prev_te = None
                prev_g_mm = prev_u_mm = None
                for hs in range(8):
                    cols = slice(hs * 256, hs * 256 + 256)
                    t_l1 = k.dma(SY, w1st[:],
                                 w1_src.rearrange("(c p) h -> p c h", p=128)[:, :, cols],
                                 wait=[prev_g_mm, last_w2mm],
                                 inc="wl0" if hs % 2 == 0 else "wl1")
                    t_l3 = k.dma(SY, w3st[:],
                                 w3_src.rearrange("(c p) h -> p c h", p=128)[:, :, cols],
                                 wait=[prev_u_mm, last_w2mm],
                                 inc="w3l0" if hs % 2 == 0 else "w3l1")
                    t_c1 = k.op(G, lambda e: e.tensor_copy(w1bf[:], w1st[:]),
                                wait=[t_l1, prev_g_mm], inc="eg")
                    t_c3 = k.op(G, lambda e: e.tensor_copy(w3bf[:], w3st[:]),
                                wait=[t_l3, prev_u_mm], inc="eg")
                    for hh in range(2):
                        hc = hs * 2 + hh
                        hcol = slice(hh * 128, hh * 128 + 128)
                        for c in range(DC):
                            k.op(T, lambda e, c=c, hcol=hcol, ts0=ts0: e.matmul(
                                PGa[:, :], w1bf[:, c, hcol], hbf[:, c, ts0:ts0 + 512],
                                start=(c == 0), stop=(c == DC - 1)),
                                wait=[t_c1, ln_all, prev_sg], inc=None)
                            prev_g_mm = k.op(T, lambda e, c=c, hcol=hcol, ts0=ts0: e.matmul(
                                PGb[:, :], w1bf[:, c, hcol],
                                hbf[:, c, ts0 + 512:ts0 + 1024],
                                start=(c == 0), stop=(c == DC - 1)),
                                wait=[], inc="mm" if c == DC - 1 else None)
                        for c in range(DC):
                            k.op(T, lambda e, c=c, hcol=hcol, ts0=ts0: e.matmul(
                                PUa[:, :], w3bf[:, c, hcol], hbf[:, c, ts0:ts0 + 512],
                                start=(c == 0), stop=(c == DC - 1)),
                                wait=[t_c3, prev_tt], inc=None)
                            prev_u_mm = k.op(T, lambda e, c=c, hcol=hcol, ts0=ts0: e.matmul(
                                PUb[:, :], w3bf[:, c, hcol],
                                hbf[:, c, ts0 + 512:ts0 + 1024],
                                start=(c == 0), stop=(c == DC - 1)),
                                wait=[], inc="mm" if c == DC - 1 else None)
                        k.op(S, lambda e: e.activation(out=sg[:, 0:512], in_=PGa[:, :],
                                                       func=AF.Silu),
                             wait=[prev_g_mm, prev_tt], inc="es")
                        t_sg = k.op(S, lambda e: e.activation(
                            out=sg[:, 512:1024], in_=PGb[:, :], func=AF.Silu),
                            wait=[], inc="es")
                        prev_sg = t_sg
                        k.op(V, lambda e: e.tensor_mul(tt_s[:, 0:512], sg[:, 0:512],
                                                       PUa[:, :]),
                             wait=[t_sg, prev_u_mm, prev_te], inc="ev")
                        t_t = k.op(V, lambda e: e.tensor_mul(
                            tt_s[:, 512:1024], sg[:, 512:1024], PUb[:, :]),
                            wait=[], inc="ev")
                        prev_tt = t_t
                        if shared:
                            t_e = k.op(G, lambda e, hc=hc: e.tensor_copy(
                                eo[:, hc, :], tt_s[:]), wait=[t_t], inc="eg")
                        else:
                            t_e = k.op(G, lambda e, hc=hc: e.tensor_mul(
                                eo[:, hc, :], tt_s[:], cwb_sb[:]),
                                wait=[t_t, tc2], inc="eg")
                        eo_toks.append(t_e)
                        prev_te = t_e
                prev_macc = {0: None, 1: None}
                for tsub in range(2):
                    tcol = slice(tsub * 512, tsub * 512 + 512)
                    pwh = (PWa if tsub == 0 else PWb)[:, :]
                    for dc in range(DC):
                        tm = None
                        for hc in range(16):
                            tm = k.op(T, lambda e, hc=hc, dc=dc, tcol=tcol, p=pwh: e.matmul(
                                p, w2bf[:, hc, dc * 128:dc * 128 + 128], eo[:, hc, tcol],
                                start=(hc == 0), stop=(hc == 15)),
                                wait=[w2b_toks[hc], eo_toks[hc]] +
                                     ([prev_macc[tsub]] if hc == 0 else []),
                                inc="mm" if hc == 15 else None)
                        last_w2mm = tm
                        prev_macc[tsub] = k.op(V, lambda e, dc=dc, tcol=tcol, p=pwh:
                            e.tensor_add(macc[:, dc, tcol], macc[:, dc, tcol], p),
                            wait=[tm, t_ms], inc="ev")
            lastv = k.op(V, lambda e: e.nop(), inc="ev")
            for sti in range(4):
                c0 = sti * 256
                t_x = k.dma(SY, xt[:], fm_view(xo)[:, :, ts0 + c0:ts0 + c0 + 256],
                            wait=[lastv, prev_st], inc="ld")
                t_y = k.op(V, lambda e, c0=c0: e.scalar_tensor_tensor(
                    out=xt[:], in0=macc[:, :, c0:c0 + 256], scalar=0.5, in1=xt[:],
                    op0=ALU.mult, op1=ALU.add), wait=[t_x], inc="ev")
                prev_st = k.dma(SY, fm_view(y)[:, :, ts0 + c0:ts0 + c0 + 256], xt[:],
                                wait=[t_y], inc="st")
                prev_ty = t_y


def build_moe():
    nc = bass.Bass("TRN2", target_bir_lowering=False, debug=False)
    xo = nc.dram_tensor("xo", [DM, OWN], F32, kind="ExternalInput").ap()
    rwT = nc.dram_tensor("rwT", [DM, 8], F32, kind="ExternalInput").ap()
    eb = nc.dram_tensor("eb", [8], F32, kind="ExternalInput").ap()
    w1 = nc.dram_tensor("w1", [8, DM, 2048], F32, kind="ExternalInput").ap()
    w3 = nc.dram_tensor("w3", [8, DM, 2048], F32, kind="ExternalInput").ap()
    w2 = nc.dram_tensor("w2", [8, 2048, DM], F32, kind="ExternalInput").ap()
    sw1 = nc.dram_tensor("sw1", [DM, 2048], F32, kind="ExternalInput").ap()
    sw3 = nc.dram_tensor("sw3", [DM, 2048], F32, kind="ExternalInput").ap()
    sw2 = nc.dram_tensor("sw2", [2048, DM], F32, kind="ExternalInput").ap()
    eye = nc.dram_tensor("eye", [128, 128], F32, kind="ExternalInput").ap()
    y = nc.dram_tensor("y", [DM, OWN], F32, kind="ExternalOutput").ap()
    cwscr = nc.dram_tensor("cwscr", [8, OWN], F32, kind="Internal").ap()
    stack = ExitStack()
    with stack:
        k = Sched(nc, stack)
        with ExitStack() as ps:
            emit_moe(nc, k, ps, xo, rwT, eb, w1, w3, w2, sw1, sw3, sw2, eye, y,
                     cwscr)
        k.raw_wait("sync", "st", k.cnt["st"])
        k.finalize()
    return nc


def _shards_fm(x):
    out = []
    for core in range(NCORES):
        r, h = core // 2, core % 2
        out.append(np.ascontiguousarray(x[r, h * OWN:(h + 1) * OWN, :].T))
    return out


def _assemble_fm(shards):
    x = np.empty((4, ROWT, DM), np.float32)
    for core in range(NCORES):
        r, h = core // 2, core % 2
        x[r, h * OWN:(h + 1) * OWN, :] = shards[core].T
    return x


_EYE = np.eye(128, dtype=np.float32)


def run_moe_launch(nc_moe, x, W):
    shards = _shards_fm(x)
    base = {
        "rwT": np.ascontiguousarray(np.asarray(W["L1_ffn_router_w"]).T),
        "eb": np.asarray(W["L1_ffn_experts_bias"]),
        "w1": np.asarray(W["L1_ffn_w1"]), "w3": np.asarray(W["L1_ffn_w3"]),
        "w2": np.asarray(W["L1_ffn_w2"]),
        "sw1": np.asarray(W["L1_ffn_sw1"])[0], "sw3": np.asarray(W["L1_ffn_sw3"])[0],
        "sw2": np.asarray(W["L1_ffn_sw2"])[0], "eye": _EYE,
    }
    in_maps = [dict(base, xo=shards[c]) for c in range(NCORES)]
    res = run_bass_kernel_spmd(nc_moe, in_maps, core_ids=list(range(NCORES)))
    return _assemble_fm([np.asarray(res.results[c]["y"]) for c in range(NCORES)])


def emit_l1attn(nc, k, stack, xo, xr, Wq, Wk, Wv, Wo, vecs, eye, y,
                row_read=None):
    if True:
        sfx = _uniq()
        sb = lambda n, sh, dt: stack.enter_context(nc.sbuf_tensor(n + sfx, sh, dt))
        psal = lambda n, sh: stack.enter_context(nc.psum_tensor(n + sfx, sh, F32))
        vtm = sb("vtm", [128, 32, DM], BF16)
        wb1 = sb("wb1", [128, DC, DM], BF16)
        wb2 = sb("wb2", [128, DC, DM], BF16)
        wst = sb("wst", [128, DC, 256], F32)
        hkb = sb("hkb", [128, DC, 512], BF16)
        xt = sb("xt", [128, DC, 256], F32)
        osb = sb("osb", [128, DC, 512], BF16)
        qs_sb = sb("qs_sb", [128, 512], F32)
        sq = [sb("sq0", [128, 256], F32), sb("sq1", [128, 256], F32)]
        tqb = sb("tqb", [128, OWN], F32)
        tkcol = sb("tkcol", [128, 32], F32)
        vscol = sb("vscol", [128, 8], F32)
        dt_s = sb("dt_s", [128, 512], F32)
        ab_s = sb("ab_s", [128, 512], F32)
        et_s = sb("et_s", [128, 512], F32)
        mt_s = sb("mt_s", [128, 512], F32)
        atb = [sb("atb0", [128, 512], BF16), sb("atb1", [128, 512], BF16)]
        rows = sb("rows", [1, 6 * 256], F32)
        tksc = sb("tksc", [1, 512], F32)
        grows = sb("grows", [1, 8 * DM], F32)
        browb = sb("browb", [1, 4 * DM], BF16)
        eye_sb = sb("eye_sb", [128, 128], F32)
        ones_col = sb("ones_col", [128, 1], F32)
        ones_row1 = sb("ones_row1", [1, 128], F32)
        ones_rbf = sb("ones_rbf", [1, 512], BF16)
        ones_cbf = sb("ones_cbf", [128, 1], BF16)
        epscol = sb("epscol", [1, 1], F32)

        P = [psal(f"P{i}", [128, 512]) for i in range(8)]

        for s in ["lnt", "mm"]:
            k.sem(s)
        for s in ["ld", "wl0", "wl1", "st"]:
            k.sem(s, dma=True)
        S, V, T, G, SY = "scalar", "vector", "tensor", "gpsimd", "sync"
        if row_read is None:
            def row_read(dst, t0, n, wait):
                return k.dma(SY, dst, fm_view(xr)[:, :, t0:t0 + n],
                             wait=wait, inc="ld")

        k.op(G, lambda e: e.memset(ones_col[:], 1.0))
        k.op(G, lambda e: e.memset(ones_row1[:], 1.0))
        k.op(G, lambda e: e.memset(ones_rbf[:], 1.0))
        k.op(G, lambda e: e.memset(ones_cbf[:], 1.0))
        t_const = k.op(G, lambda e: e.memset(epscol[:], 1e-5))
        t_eye = k.dma(SY, eye_sb[:], eye, inc="ld")

        r_t = grows[0:1, 7 * DM:8 * DM]
        c_toks = []
        gam_toks = []
        prev_vl = None
        for pi in range(3):
            t_vl = k.dma(SY, grows[0:1, 0:4 * DM],
                         vecs[4 * pi:4 * pi + 4].rearrange("(o a) d -> o (a d)", o=1),
                         wait=[prev_vl], inc="ld")
            g_r = grows[0:1, 0:DM]
            b_r = grows[0:1, DM:2 * DM]
            m_r = grows[0:1, 2 * DM:3 * DM]
            v_r = grows[0:1, 3 * DM:4 * DM]
            gam = grows[0:1, (4 + pi) * DM:(5 + pi) * DM]
            t1 = k.op(S, lambda e, v_r=v_r: e.activation(
                out=r_t, in_=v_r, func=AF.Ln, bias=epscol[0:1, 0:1]),
                wait=[t_vl, t_const])
            t2 = k.op(S, lambda e: e.activation(out=r_t, in_=r_t, func=AF.Exp,
                                                scale=-0.5))
            t3 = k.op(V, lambda e, g_r=g_r, gam=gam: e.tensor_mul(gam, g_r, r_t),
                      wait=[t2])
            gam_toks.append(t3)
            t4 = k.op(V, lambda e, m_r=m_r, gam=gam: e.tensor_mul(r_t, m_r, gam))
            t5 = k.op(V, lambda e, b_r=b_r: e.tensor_sub(r_t, b_r, r_t))
            t6 = k.op(V, lambda e: e.tensor_scalar(out=r_t, in0=r_t, scalar1=1.0,
                                                   scalar2=-1.0, op0=ALU.mult,
                                                   op1=ALU.add))
            tc = k.op(S, lambda e, pi=pi: e.copy(
                out=browb[0:1, pi * DM:(pi + 1) * DM], in_=r_t), wait=[t6])
            c_toks.append(tc)
            prev_vl = tc
        t_bo = k.dma(SY, grows[0:1, 0:DM],
                     vecs[12:13].rearrange("(o a) d -> o (a d)", o=1),
                     wait=[prev_vl], inc="ld")
        t_bo = k.op(S, lambda e: e.copy(out=browb[0:1, 3 * DM:4 * DM],
                                        in_=grows[0:1, 0:DM]), wait=[t_bo])

        def fold_w(wsrc, wdst, pi, waitfor):
            last = waitfor
            if pi is not None:
                gam = grows[0:1, (4 + pi) * DM:(5 + pi) * DM]
                gb1 = k.op(T, lambda e, gam=gam: e.matmul(
                    P[6][:, :], ones_row1[:], gam[0:1, 0:512], start=True, stop=True),
                    wait=[gam_toks[pi]] + waitfor, inc="mm")
                gb2 = k.op(T, lambda e, gam=gam: e.matmul(
                    P[7][:, :], ones_row1[:], gam[0:1, 512:1024], start=True, stop=True),
                    wait=[], inc="mm")
            toks = []
            tf = None
            for qtr in range(4):
                cols = slice(qtr * 256, qtr * 256 + 256)
                t_l = k.dma(SY, wst[:],
                            wsrc.rearrange("(c p) d -> p c d", p=128)[:, :, cols],
                            wait=[tf] + waitfor, inc="wl%d" % (qtr % 2))
                for c in range(DC):
                    if pi is None:
                        tf = k.op(V, lambda e, c=c, cols=cols: e.tensor_copy(
                            wdst[:, c, cols], wst[:, c, :]), wait=[t_l])
                    else:
                        pp = P[6] if qtr < 2 else P[7]
                        po = (qtr % 2) * 256
                        tf = k.op(V, lambda e, c=c, cols=cols, pp=pp, po=po:
                                  e.tensor_mul(wdst[:, c, cols], wst[:, c, :],
                                               pp[:, po:po + 256]),
                                  wait=[t_l, gb2])
                toks.append(tf)
            return toks[-1]

        t_wk = fold_w(Wk, wb1, 1, [])
        t_wv = fold_w(Wv, wb2, 2, [t_wk])

        def emit_ln_tile(reader, t0, waits):
            out_toks = []
            t_prev = None
            for sub in range(2):
                o = sub * 256
                t_x = reader(xt[:], t0 + o, 256, waits + [t_prev])
                toks = emit_ln_fm(
                    k, lambda c: xt[:, c, :], 256,
                    lambda c, o=o: hkb[:, c, o:o + 256],
                    (P[2], P[3]), (P[4], P[5]), sq, rows, ones_col[:], ones_row1[:],
                    [t_x, t_const], tag="ln", eps_col=epscol[0:1, 0:1])
                out_toks.extend(toks)
                t_prev = toks[-1]
            return out_toks[-1]

        vtm_toks = [None] * 32
        prev_sig = None
        prev_ln = None
        tq_tok = None
        tkc_toks = []
        tqb_toks = []
        for ti in range(12):
            qphase = ti >= 8
            t0 = (ti - 8) * 512 if qphase else ti * 512
            if qphase:
                rd = lambda dst, tt0, n, wait: k.dma(
                    SY, dst, fm_view(xo)[:, :, tt0:tt0 + n], wait=wait, inc="ld")
            else:
                rd = row_read
            t_ln = emit_ln_tile(rd, t0, [prev_ln])
            if not qphase:
                for j in range(4):
                    ktc = ti * 4 + j
                    tsl = slice(j * 128, j * 128 + 128)
                    for half in range(2):
                        ph = P[6] if half == 0 else P[7]
                        dcols = slice(half * 512, half * 512 + 512)
                        for c in range(DC):
                            k.op(T, lambda e, c=c, tsl=tsl, dcols=dcols, ph=ph:
                                 e.matmul(ph[:, :], hkb[:, c, tsl], wb2[:, c, dcols],
                                          start=(c == 0), stop=False),
                                 wait=[t_ln, t_wv, prev_sig], inc=None)
                        tm = k.op(T, lambda e, half=half, ph=ph: e.matmul(
                            ph[:, :], ones_rbf[0:1, 0:128],
                            browb[0:1, 2 * DM + half * 512:2 * DM + half * 512 + 512],
                            start=False, stop=True),
                            wait=[c_toks[2]], inc="mm")
                        prev_sig = k.op(S, lambda e, ktc=ktc, dcols=dcols, ph=ph:
                                        e.activation(out=vtm[:, ktc, dcols],
                                                     in_=ph[:, :],
                                                     func=AF.Sigmoid, scale=4.0),
                                        wait=[tm])
                    vtm_toks[ktc] = prev_sig
            wb = wb1
            crow_off = 0 if qphase else DM
            w_tok = t_wq if qphase else t_wk
            tsum = None
            for dc in range(DC):
                ph = P[6] if dc % 2 == 0 else P[7]
                for c in range(DC):
                    k.op(T, lambda e, c=c, dc=dc, wb=wb, ph=ph: e.matmul(
                        ph[:, :], wb[:, c, dc * 128:dc * 128 + 128], hkb[:, c, :],
                        start=(c == 0), stop=False),
                        wait=[t_ln, w_tok, prev_sig], inc=None)
                tm = k.op(T, lambda e, dc=dc, ph=ph, crow_off=crow_off: e.matmul(
                    ph[:, :],
                    browb[0:1, crow_off + dc * 128:crow_off + dc * 128 + 128],
                    ones_rbf[:], start=False, stop=True),
                    wait=[c_toks[0] if qphase else c_toks[1]], inc="mm")
                prev_sig = k.op(S, lambda e, ph=ph: e.activation(
                    out=qs_sb[:], in_=ph[:, :], func=AF.Sigmoid, scale=4.0),
                    wait=[tm])
                tsum = k.op(T, lambda e, st=(dc == 0), sp=(dc == DC - 1): e.matmul(
                    P[2][0:1, 0:512], ones_col[:], qs_sb[:], start=st, stop=sp),
                    wait=[prev_sig, prev_ln], inc="mm")
            tq_tok = k.op(V, lambda e: e.tensor_scalar(
                out=tksc[:], in0=P[2][0:1, 0:512],
                scalar1=-1.0 / DM, scalar2=1.0, op0=ALU.mult, op1=ALU.add),
                wait=[tsum])
            if qphase:
                qt = ti - 8
                tb = k.op(T, lambda e: e.matmul(
                    P[7][:, :], ones_row1[:], tksc[:], start=True, stop=True),
                    wait=[tq_tok, prev_sig, t_eye], inc="mm")
                tq_tok = k.op(S, lambda e, qt=qt: e.copy(
                    out=tqb[:, qt * 512:qt * 512 + 512], in_=P[7][:, :]), wait=[tb])
                tqb_toks.append(tq_tok)
            else:
                for j in range(4):
                    tt1 = k.op(T, lambda e, j=j: e.matmul(
                        P[7][0:128, 0:1], tksc[0:1, j * 128:j * 128 + 128],
                        ones_row1[0:1, 0:1], start=True, stop=True),
                        wait=[tq_tok, prev_sig] +
                             ([tkc_toks[-1]] if tkc_toks else []), inc="mm")
                    tq_tok = k.op(S, lambda e, ti=ti, j=j: e.copy(
                        out=tkcol[:, ti * 4 + j:ti * 4 + j + 1],
                        in_=P[7][0:128, 0:1]), wait=[tt1])
                    tkc_toks.append(tq_tok)
            prev_ln = tq_tok
            if ti == 7:
                t_wq = fold_w(Wq, wb1, 0, [prev_sig, tq_tok])

        vs_toks = []
        prev_vs = tq_tok
        for half in range(2):
            tm = None
            for ktc in range(32):
                tm = k.op(T, lambda e, ktc=ktc, half=half, st=(ktc == 0),
                          sp=(ktc == 31): e.matmul(
                    P[6][0:1, 0:512], ones_cbf[:],
                    vtm[:, ktc, half * 512:half * 512 + 512],
                    start=st, stop=sp),
                    wait=[vtm_toks[ktc]] + ([prev_vs] if ktc == 0 else []), inc="mm")
            tr = k.op(V, lambda e: e.tensor_scalar_mul(
                rows[0:1, 0:512], P[6][0:1, 0:512], 0.9), wait=[tm])
            for j in range(4):
                tt1 = k.op(T, lambda e, j=j: e.matmul(
                    P[7][0:128, 0:1], rows[0:1, j * 128:j * 128 + 128],
                    ones_row1[0:1, 0:1], start=True, stop=True),
                    wait=[tr, prev_vs], inc="mm")
                prev_vs = k.op(S, lambda e, j=j, half=half: e.copy(
                    out=vscol[:, half * 4 + j:half * 4 + j + 1],
                    in_=P[7][0:128, 0:1]), wait=[tt1])
                vs_toks.append(prev_vs)
        t_vs = vs_toks[-1]

        t_wo = fold_w(Wo, wb2, None, [vtm_toks[31], prev_sig, t_vs])

        prev_st = None
        prev_add = None
        prev_osb = None
        at_rd = [None, None]
        for qt in range(4):
            qsl = slice(qt * 512, qt * 512 + 512)
            av_last = None
            for ktc in range(32):
                bi = ktc % 2
                tv1 = k.op(V, lambda e, qsl=qsl, ktc=ktc: e.tensor_scalar(
                    out=dt_s[:], in0=tqb[:, qsl], scalar1=tkcol[:, ktc:ktc + 1],
                    scalar2=None, op0=ALU.subtract),
                    wait=[tkc_toks[ktc], tqb_toks[qt], at_rd[bi]])
                tg1 = k.op(S, lambda e: e.activation(out=ab_s[:], in_=dt_s[:],
                                                     func=AF.Abs), wait=[tv1])
                ts1 = k.op(S, lambda e: e.activation(out=et_s[:], in_=ab_s[:],
                                                     func=AF.Exp, scale=-2.0),
                           wait=[tg1])
                tv2 = k.op(V, lambda e: e.tensor_scalar(
                    out=mt_s[:], in0=dt_s[:], scalar1=0.0, scalar2=None,
                    op0=ALU.is_lt), wait=[tv1])
                tg2 = k.op(V, lambda e: e.tensor_scalar(
                    out=mt_s[:], in0=mt_s[:], scalar1=2.0, scalar2=-1.0,
                    op0=ALU.mult, op1=ALU.add), wait=[tv2])
                tv3 = k.op(V, lambda e, bi=bi: e.tensor_mul(atb[bi][:], mt_s[:],
                                                            et_s[:]),
                           wait=[tg2, ts1])
                for dc in range(DC):
                    av_last = k.op(T, lambda e, dc=dc, ktc=ktc, bi=bi,
                                   st=(ktc == 0), sp=(ktc == 31): e.matmul(
                        P[dc][:, :], vtm[:, ktc, dc * 128:dc * 128 + 128],
                        atb[bi][:], start=st, stop=sp),
                        wait=[tv3, vtm_toks[ktc], t_wo] +
                             ([prev_osb] if ktc == 0 else []),
                        inc="mm" if dc == DC - 1 else None)
                at_rd[bi] = av_last
            for dc in range(DC):
                prev_osb = k.op(V, lambda e, dc=dc: e.tensor_scalar(
                    out=osb[:, dc, :], in0=P[dc][:, :], scalar1=0.9,
                    scalar2=vscol[:, dc:dc + 1], op0=ALU.mult, op1=ALU.add),
                    wait=[av_last, t_vs])
            t0 = qt * 512
            for sub in range(2):
                o = sub * 256
                t_x = k.dma(SY, xt[:], fm_view(xo)[:, :, t0 + o:t0 + o + 256],
                            wait=[prev_st, prev_add, prev_ln], inc="ld")
                for dc in range(DC):
                    ph = P[dc % 2]
                    for c in range(DC):
                        k.op(T, lambda e, c=c, dc=dc, ph=ph, o=o: e.matmul(
                            ph[:, 0:256], wb2[:, c, dc * 128:dc * 128 + 128],
                            osb[:, c, o:o + 256], start=(c == 0), stop=False),
                            wait=[prev_osb, t_wo] + ([prev_add] if c == 0 else []),
                            inc=None)
                    tm = k.op(T, lambda e, dc=dc, ph=ph: e.matmul(
                        ph[:, 0:256],
                        browb[0:1, 3 * DM + dc * 128:3 * DM + dc * 128 + 128],
                        ones_rbf[0:1, 0:256], start=False, stop=True),
                        wait=[t_bo], inc="mm")
                    prev_add = k.op(V, lambda e, dc=dc, ph=ph: e.tensor_add(
                        xt[:, dc, :], xt[:, dc, :], ph[:, 0:256]), wait=[tm, t_x])
                prev_st = k.dma(SY, fm_view(y)[:, :, t0 + o:t0 + o + 256], xt[:],
                                wait=[prev_add], inc="st")




def build_l1attn():
    nc = bass.Bass("TRN2", target_bir_lowering=False, debug=False)
    xo = nc.dram_tensor("xo", [DM, OWN], F32, kind="ExternalInput").ap()
    xr = nc.dram_tensor("xr", [DM, ROWT], F32, kind="ExternalInput").ap()
    Wq = nc.dram_tensor("Wq", [DM, DM], F32, kind="ExternalInput").ap()
    Wk = nc.dram_tensor("Wk", [DM, DM], F32, kind="ExternalInput").ap()
    Wv = nc.dram_tensor("Wv", [DM, DM], F32, kind="ExternalInput").ap()
    Wo = nc.dram_tensor("Wo", [DM, DM], F32, kind="ExternalInput").ap()
    vecs = nc.dram_tensor("vecs", [13, DM], F32, kind="ExternalInput").ap()
    eye = nc.dram_tensor("eye", [128, 128], F32, kind="ExternalInput").ap()
    y = nc.dram_tensor("y", [DM, OWN], F32, kind="ExternalOutput").ap()
    stack = ExitStack()
    with stack:
        k = Sched(nc, stack)
        with ExitStack() as ps:
            emit_l1attn(nc, k, ps, xo, xr, Wq, Wk, Wv, Wo, vecs, eye, y)
        k.raw_wait("sync", "st", k.cnt["st"])
        k.finalize()
    return nc


def run_l1attn_launch(nc_a, x, W):
    shards = _shards_fm(x)
    vecs = np.stack([
        W["L1_attn_gq"], W["L1_attn_bq"], W["L1_attn_mq"], W["L1_attn_vq"],
        W["L1_attn_gk"], W["L1_attn_bk"], W["L1_attn_mk"], W["L1_attn_vk"],
        W["L1_attn_gv"], W["L1_attn_bv"], W["L1_attn_mv"], W["L1_attn_vv"],
        W["L1_attn_bo"]]).astype(np.float32)
    base = {
        "Wq": np.asarray(W["L1_attn_Wq"]), "Wk": np.asarray(W["L1_attn_Wk"]),
        "Wv": np.asarray(W["L1_attn_Wv"]), "Wo": np.asarray(W["L1_attn_Wo"]),
        "vecs": vecs, "eye": _EYE,
    }
    in_maps = []
    for c in range(NCORES):
        r = c // 2
        xrow = np.ascontiguousarray(x[r].T)
        in_maps.append(dict(base, xo=shards[c], xr=xrow))
    res = run_bass_kernel_spmd(nc_a, in_maps, core_ids=list(range(NCORES)))
    return _assemble_fm([np.asarray(res.results[c]["y"]) for c in range(NCORES)])


def emit_l0attn(nc, k, stack, xo, xr, Wq, Wk, Wv, Wo, brows, tcol, eyeb, y,
                row_read=None):
    if True:
        sfx = _uniq()
        sb = lambda n, sh, dt: stack.enter_context(nc.sbuf_tensor(n + sfx, sh, dt))
        qbf = sb("qbf", [128, DC, OWN], BF16)
        osb = sb("osb", [128, DC, OWN], BF16)
        wbuf = sb("wbuf", [128, DC, DM], BF16)
        wst = sb("wst", [128, DC, 256], F32)
        hkb = sb("hkb", [128, DC, 256], BF16)
        hf = sb("hf", [128, DC, 256], F32)
        xt = sb("xt", [128, DC, 256], F32)
        summ = sb("summ", [128, DC, 64], F32)
        summ_b = sb("summ_b", [128, DC, 64], BF16)
        kbf = sb("kbf", [128, DC, 64], BF16)
        vtm = sb("vtm", [64, DM], BF16)
        sq = [sb("sq0", [128, 256], F32), sb("sq1", [128, 256], F32)]
        rows = sb("rows", [1, 6 * 256], F32)
        browb = sb("browb", [1, 4 * DM], BF16)
        brf = sb("brf", [1, 4 * DM], F32)
        i64b = sb("i64b", [128, 64], F32)
        i64r = sb("i64r", [1, 64], F32)
        tcs = sb("tcs", [128, 16], F32)
        mask01 = sb("mask01", [128, 64], F32)
        ms_s = sb("ms_s", [128, 64], F32)
        att_f = sb("att_f", [128, 64], F32)
        att_b = sb("att_b", [128, 64], BF16)
        attT = sb("attT", [64, 128], BF16)
        mcol = sb("mcol", [128, 4], F32)
        eye_b = sb("eye_b", [128, 128], BF16)
        eye_f = sb("eye_f", [128, 128], F32)
        ones_col = sb("ones_col", [128, 1], F32)
        ones_row1 = sb("ones_row1", [1, 128], F32)
        ones_rbf = sb("ones_rbf", [1, 512], BF16)
        epscol = sb("epscol", [1, 1], F32)

        P = [stack.enter_context(nc.psum_tensor(f"P{i}{sfx}", [128, 512], F32))
             for i in range(6)]
        PTb = stack.enter_context(nc.psum_tensor("PTb" + sfx, [64, 512], BF16))
        PO = stack.enter_context(nc.psum_tensor("PO" + sfx, [64, 512], F32))

        for s in ["lnt", "mm"]:
            k.sem(s)
        for s in ["ld", "wl0", "wl1", "st"]:
            k.sem(s, dma=True)
        S, V, T, G, SY = "scalar", "vector", "tensor", "gpsimd", "sync"
        if row_read is None:
            def row_read(dst, t0, n, wait, slow=False):
                return k.dma(SY, dst, fm_view(xr)[:, :, t0:t0 + n],
                             wait=wait, inc="ld", slow=slow)

        k.op(G, lambda e: e.memset(ones_col[:], 1.0))
        k.op(G, lambda e: e.memset(ones_row1[:], 1.0))
        k.op(G, lambda e: e.memset(ones_rbf[:], 1.0))
        t_const = k.op(G, lambda e: e.memset(epscol[:], 1e-5))
        t_eye = k.dma(SY, eye_f[:], eyeb, inc="ld")
        t_eyeb = k.op(S, lambda e: e.copy(out=eye_b[:], in_=eye_f[:]), wait=[t_eye])
        t_tc = k.dma(SY, tcs[:], tcol, inc="ld")
        t_br = k.dma(SY, brf[:], brows.rearrange("(o a) d -> o (a d)", o=1), inc="ld")
        t_brb = k.op(S, lambda e: e.copy(out=browb[:], in_=brf[:]), wait=[t_br])
        t_io = k.op(G, lambda e: e.iota(i64r[:], pattern=[[64, 64]], base=0,
                                        channel_multiplier=0, allow_small_or_imprecise_dtypes=True))
        t_iob = k.op(T, lambda e: e.matmul(P[4][:, 0:64], ones_row1[:], i64r[:],
                                           start=True, stop=True),
                     wait=[t_io], inc="mm")
        t_iobs = k.op(S, lambda e: e.copy(out=i64b[:], in_=P[4][:, 0:64]),
                      wait=[t_iob])

        def cast_w(wsrc, waitfor):
            tf = None
            for qtr in range(4):
                cols = slice(qtr * 256, qtr * 256 + 256)
                t_l = k.dma(SY, wst[:],
                            wsrc.rearrange("(c p) d -> p c d", p=128)[:, :, cols],
                            wait=[tf] + waitfor, inc="wl%d" % (qtr % 2))
                for c in range(DC):
                    tf = k.op(V, lambda e, c=c, cols=cols: e.tensor_copy(
                        wbuf[:, c, cols], wst[:, c, :]), wait=[t_l])
            return tf

        t_sm0 = k.op(G, lambda e: e.memset(summ[:], 0.0))
        prev_ln = None
        sum_toks = []
        t_x0 = row_read(xt[:, :, 0:1], 0, 1, [], slow=True)
        toks0 = emit_ln_fm(
            k, lambda c: xt[:, c, 0:1], 1,
            lambda c: hkb[:, c, 0:1],
            (P[0], P[1]), (P[2], P[3]), sq, rows, ones_col[:], ones_row1[:],
            [t_x0, t_const, t_sm0], tag="ln", eps_col=epscol[0:1, 0:1],
            extra_dst=lambda c: summ[:, c, 0:1])
        prev_ln = toks0[-1]
        sum_toks.append(prev_ln)
        for ti in range(16):
            t0 = 1 + ti * 256
            ntk = 256 if ti < 15 else 192
            ng = ntk // 64
            t_x = row_read(xt[:, :, 0:ntk], t0, ntk, [prev_ln])
            toks = emit_ln_fm(
                k, lambda c, ntk=ntk: xt[:, c, 0:ntk], ntk,
                lambda c, ntk=ntk: hkb[:, c, 0:ntk],
                (P[0], P[1]), (P[2], P[3]), sq, rows, ones_col[:], ones_row1[:],
                [t_x, t_const], tag="ln", eps_col=epscol[0:1, 0:1],
                extra_dst=lambda c, ntk=ntk: hf[:, c, 0:ntk])
            g0 = 1 + ti * 4
            for c in range(DC):
                tr = k.op(V, lambda e, c=c, g0=g0, ng=ng, ntk=ntk: e.reduce_sum(
                    summ[:, c, g0:g0 + ng],
                    hf[:, c, 0:ntk].rearrange("p (g j) -> p g j", j=64),
                    axis=AX.X), wait=[toks[c], t_sm0])
                sum_toks.append(tr)
            prev_ln = sum_toks[-1]
        t_means = k.op(V, lambda e: e.tensor_scalar_mul(
            summ[:, :, 1:64], summ[:, :, 1:64], 1.0 / 64.0), wait=[prev_ln])
        t_sumb = k.op(S, lambda e: e.copy(out=summ_b[:], in_=summ[:]),
                      wait=[t_means])

        t_wk = cast_w(Wk, [])
        prev_sig = None
        for dc in range(DC):
            ph = P[4] if dc % 2 == 0 else P[5]
            for c in range(DC):
                k.op(T, lambda e, c=c, dc=dc, ph=ph: e.matmul(
                    ph[:, 0:64], wbuf[:, c, dc * 128:dc * 128 + 128],
                    summ_b[:, c, :], start=(c == 0), stop=False),
                    wait=[t_sumb, t_wk, prev_sig, t_iobs], inc=None)
            tm = k.op(T, lambda e, dc=dc, ph=ph: e.matmul(
                ph[:, 0:64], browb[0:1, DM + dc * 128:DM + dc * 128 + 128],
                ones_rbf[0:1, 0:64], start=False, stop=True),
                wait=[t_brb], inc="mm")
            prev_sig = k.op(S, lambda e, dc=dc, ph=ph: e.copy(
                out=kbf[:, dc, :], in_=ph[:, 0:64]), wait=[tm])
        t_wv = cast_w(Wv, [prev_sig])
        vt_toks = []
        for half in range(2):
            ph = P[4] if half == 0 else P[5]
            dcols = slice(half * 512, half * 512 + 512)
            for c in range(DC):
                k.op(T, lambda e, c=c, dcols=dcols, ph=ph: e.matmul(
                    ph[0:64, :], summ_b[:, c, :], wbuf[:, c, dcols],
                    start=(c == 0), stop=False),
                    wait=[t_wv, prev_sig], inc=None)
            tm = k.op(T, lambda e, half=half, ph=ph: e.matmul(
                ph[0:64, :], ones_rbf[0:1, 0:64],
                browb[0:1, 2 * DM + half * 512:2 * DM + half * 512 + 512],
                start=False, stop=True), wait=[t_brb], inc="mm")
            prev_sig = k.op(S, lambda e, dcols=dcols, ph=ph: e.copy(
                out=vtm[:, dcols], in_=ph[0:64, :]), wait=[tm])
            vt_toks.append(prev_sig)
        t_vtm = vt_toks[-1]

        t_wq = cast_w(Wq, [t_vtm])
        prev_q = {0: None, 1: None}
        q_toks = []
        for ti in range(8):
            t0 = ti * 256
            t_x = k.dma(SY, xt[:, :, 0:256], fm_view(xo)[:, :, t0:t0 + 256],
                        wait=[prev_ln, prev_q[0], prev_q[1]], inc="ld")
            toks = emit_ln_fm(
                k, lambda c: xt[:, c, 0:256], 256,
                lambda c: hkb[:, c, 0:256],
                (P[0], P[1]), (P[2], P[3]), sq, rows, ones_col[:], ones_row1[:],
                [t_x, t_const], tag="ln", eps_col=epscol[0:1, 0:1])
            t_ln = toks[-1]
            prev_ln = t_ln
            for dc in range(DC):
                bi = dc % 2
                ph = P[4] if bi == 0 else P[5]
                for c in range(DC):
                    k.op(T, lambda e, c=c, dc=dc, ph=ph: e.matmul(
                        ph[:, 0:256], wbuf[:, c, dc * 128:dc * 128 + 128],
                        hkb[:, c, 0:256], start=(c == 0), stop=False),
                        wait=[t_ln, t_wq, prev_q[bi]], inc=None)
                tm = k.op(T, lambda e, dc=dc, ph=ph: e.matmul(
                    ph[:, 0:256], browb[0:1, dc * 128:dc * 128 + 128],
                    ones_rbf[0:1, 0:256], start=False, stop=True),
                    wait=[t_brb], inc="mm")
                prev_q[bi] = k.op(S, lambda e, dc=dc, t0=t0, ph=ph: e.copy(
                    out=qbf[:, dc, t0:t0 + 256], in_=ph[:, 0:256]), wait=[tm])
            q_toks.append(prev_q[0])
            q_toks.append(prev_q[1])
        t_qall = q_toks[-1]

        prev_sc = {0: None, 1: None}
        prev_tp = None
        prev_av = None
        osb_toks = []
        for qc in range(16):
            qsl = slice(qc * 128, qc * 128 + 128)
            t_mk = k.op(V, lambda e, qc=qc: e.tensor_scalar(
                out=mask01[:], in0=i64b[:], scalar1=tcs[:, qc:qc + 1],
                scalar2=None, op0=ALU.is_le), wait=[t_iobs, t_tc, prev_av])
            for h in range(16):
                bi = h % 2
                ph = P[4] if bi == 0 else P[5]
                hc, hp = h // 2, 64 * (h % 2)
                tsc = k.op(T, lambda e, ph=ph, hc=hc, hp=hp, qsl=qsl: e.matmul(
                    ph[:, 0:64], qbf[hp:hp + 64, hc, qsl], kbf[hp:hp + 64, hc, :],
                    start=True, stop=True),
                    wait=[t_qall, prev_sig, prev_sc[bi]], inc="mm")
                tv1 = k.op(V, lambda e, ph=ph: e.scalar_tensor_tensor(
                    out=ms_s[:], in0=mask01[:], scalar=30000.0, in1=ph[:, 0:64],
                    op0=ALU.mult, op1=ALU.add), wait=[tsc, t_mk])
                prev_sc[bi] = tv1
                tv2 = k.op(V, lambda e: e.reduce_max(
                    mcol[:, 0:1], ms_s[:], axis=AX.X, negate=True), wait=[])
                tv3 = k.op(V, lambda e: e.tensor_scalar_mul(
                    mcol[:, 1:2], mcol[:, 0:1], 0.125), wait=[])
                ts1 = k.op(S, lambda e: e.activation(
                    out=att_f[:], in_=ms_s[:], func=AF.Exp, scale=0.125,
                    bias=mcol[:, 1:2]), wait=[tv3])
                tv4 = k.op(V, lambda e: e.reduce_sum(
                    mcol[:, 2:3], att_f[:], axis=AX.X), wait=[ts1])
                tv5 = k.op(V, lambda e: e.reciprocal(
                    mcol[:, 3:4], mcol[:, 2:3]), wait=[])
                tv6 = k.op(V, lambda e: e.tensor_scalar(
                    out=att_b[:], in0=att_f[:], scalar1=mcol[:, 3:4],
                    scalar2=None, op0=ALU.mult), wait=[tv5])
                ttp = k.op(T, lambda e: e.transpose(
                    PTb[0:64, 0:128], att_b[:], eye_b[:]),
                    wait=[tv6, t_eyeb, prev_tp], inc="mm")
                tcp = k.op(S, lambda e: e.copy(out=attT[:], in_=PTb[0:64, 0:128]),
                           wait=[ttp])
                prev_tp = tcp
                tav = k.op(T, lambda e, h=h: e.matmul(
                    PO[0:64, 0:128], vtm[:, h * 64:h * 64 + 64].rearrange(
                        "c d -> c d") if False else vtm[0:64, h * 64:h * 64 + 64],
                    attT[:], start=True, stop=True),
                    wait=[tcp, t_vtm, prev_av], inc="mm")
                prev_av = k.op(S, lambda e, hc=hc, hp=hp, qsl=qsl: e.copy(
                    out=osb[hp:hp + 64, hc, qsl], in_=PO[0:64, 0:128]),
                    wait=[tav])
                osb_toks.append(prev_av)
        t_oall = osb_toks[-1]

        t_wo = cast_w(Wo, [t_oall])
        prev_st = None
        prev_add = None
        for ti in range(8):
            t0 = ti * 256
            t_x = k.dma(SY, xt[:, :, 0:256], fm_view(xo)[:, :, t0:t0 + 256],
                        wait=[prev_st, prev_add, prev_ln], inc="ld")
            for dc in range(DC):
                bi = dc % 2
                ph = P[4] if bi == 0 else P[5]
                for c in range(DC):
                    k.op(T, lambda e, c=c, dc=dc, ph=ph, t0=t0: e.matmul(
                        ph[:, 0:256], wbuf[:, c, dc * 128:dc * 128 + 128],
                        osb[:, c, t0:t0 + 256], start=(c == 0), stop=False),
                        wait=[t_oall, t_wo] + ([prev_add] if c == 0 else []),
                        inc=None)
                tm = k.op(T, lambda e, dc=dc, ph=ph: e.matmul(
                    ph[:, 0:256], browb[0:1, 3 * DM + dc * 128:3 * DM + dc * 128 + 128],
                    ones_rbf[0:1, 0:256], start=False, stop=True),
                    wait=[t_brb], inc="mm")
                prev_add = k.op(V, lambda e, dc=dc, ph=ph: e.tensor_add(
                    xt[:, dc, :], xt[:, dc, :], ph[:, 0:256]), wait=[tm, t_x])
            prev_st = k.dma(SY, fm_view(y)[:, :, t0:t0 + 256], xt[:, :, 0:256],
                            wait=[prev_add], inc="st")


def build_l0attn():
    nc = bass.Bass("TRN2", target_bir_lowering=False, debug=False)
    xo = nc.dram_tensor("xo", [DM, OWN], F32, kind="ExternalInput").ap()
    xr = nc.dram_tensor("xr", [DM, ROWT], F32, kind="ExternalInput").ap()
    Wq = nc.dram_tensor("Wq", [DM, DM], F32, kind="ExternalInput").ap()
    Wk = nc.dram_tensor("Wk", [DM, DM], F32, kind="ExternalInput").ap()
    Wv = nc.dram_tensor("Wv", [DM, DM], F32, kind="ExternalInput").ap()
    Wo = nc.dram_tensor("Wo", [DM, DM], F32, kind="ExternalInput").ap()
    brows = nc.dram_tensor("brows", [4, DM], F32, kind="ExternalInput").ap()
    tcol = nc.dram_tensor("tcol", [128, 16], F32, kind="ExternalInput").ap()
    eyeb = nc.dram_tensor("eyeb", [128, 128], F32, kind="ExternalInput").ap()
    y = nc.dram_tensor("y", [DM, OWN], F32, kind="ExternalOutput").ap()
    stack = ExitStack()
    with stack:
        k = Sched(nc, stack)
        with ExitStack() as ps:
            emit_l0attn(nc, k, ps, xo, xr, Wq, Wk, Wv, Wo, brows, tcol, eyeb, y)
        k.raw_wait("sync", "st", k.cnt["st"])
        k.finalize()
    return nc


def run_l0attn_launch(nc_a, x, W):
    shards = _shards_fm(x)
    brows = np.stack([
        np.asarray(W["L0_attn_bq"]), np.asarray(W["L0_attn_bk"]),
        np.asarray(W["L0_attn_bv"]), np.asarray(W["L0_attn_bo"])]).astype(np.float32)
    base = {
        "Wq": np.asarray(W["L0_attn_Wq"]), "Wk": np.asarray(W["L0_attn_Wk"]),
        "Wv": np.asarray(W["L0_attn_Wv"]), "Wo": np.asarray(W["L0_attn_Wo"]),
        "brows": brows, "eyeb": _EYE,
    }
    in_maps = []
    for c in range(NCORES):
        r, h = c // 2, c % 2
        s = h * OWN
        tcolv = (s + np.arange(OWN).reshape(16, 128).T).astype(np.float32)
        in_maps.append(dict(base, xo=shards[c],
                            xr=np.ascontiguousarray(x[r].T),
                            tcol=np.ascontiguousarray(tcolv)))
    res = run_bass_kernel_spmd(nc_a, in_maps, core_ids=list(range(NCORES)))
    return _assemble_fm([np.asarray(res.results[c]["y"]) for c in range(NCORES)])


def emit_l0ffn(nc, k, stack, xc, wqkv, wo, w1, w2, lrows, b1r, qkcol, lgbi,
               Cq, Sq, Ck, Sk, Psw, E8i, E8Ti, eyef, mbk, hscol, y):
    if True:
        sfx = _uniq()
        sb = lambda n, sh, dt: stack.enter_context(nc.sbuf_tensor(n + sfx, sh, dt))
        h1buf = sb("h1buf", [128, DC, 3073], BF16)
        h0r = sb("h0r", [128, DC, 513], BF16)
        xmid = sb("xmid", [128, DC, 513], F32)
        hln = sb("hln", [128, DC, 512], BF16)
        eo = sb("eo", [128, 22, 512], BF16)
        wqkvb = sb("wqkvb", [128, DC, 1536], BF16)
        wob = sb("wob", [128, 4, DM], BF16)
        w1as = sb("w1as", [128, DC, 128], F32)
        w1ab = sb("w1ab", [128, DC, 128], BF16)
        w1gb = sb("w1gb", [128, DC, 128], BF16)
        w2st = sb("w2st", [128, 512], F32)
        w2sb = sb("w2sb", [128, 512], BF16)
        qh = sb("qh", [128, 512], F32)
        t1s = sb("t1s", [128, 512], F32)
        qrot = sb("qrot", [128, 4, 512], BF16)
        krot = sb("krot", [128, 4, 1024], BF16)
        vtm = sb("vtm", [128, 8, 512], BF16)
        attf = sb("attf", [128, 1024], F32)
        attT = sb("attT", [128, 128], BF16)
        o_bf = sb("o_bf", [128, 4, 512], BF16)
        sq = [sb("sq0", [128, 256], F32), sb("sq1", [128, 256], F32)]
        gg = sb("gg", [128, 512], F32)
        rss = sb("rss", [8, 512], F32)
        rbs = sb("rbs", [128, 512], F32)
        mbs = sb("mbs", [128, 4, 1024], BF16)
        E8s = sb("E8s", [128, 4, 8], F32)
        E8Ts = sb("E8Ts", [8, 4, 128], F32)
        psws = sb("psws", [128, 128], F32)
        eyes = sb("eyes", [128, 128], F32)
        lgb = sb("lgb", [128, DC, 4], F32)
        b2b = sb("b2b", [1, 2 * DM], BF16)
        b1sb = sb("b1sb", [1, 256], BF16)
        qkc = sb("qkc", [128, 4], F32)
        hss = sb("hss", [128, 2], F32)
        mcol = sb("mcol", [128, 4], F32)
        rows = sb("rows", [1, 5 * 256], F32)
        ones_col = sb("ones_col", [128, 1], F32)
        ones_row1 = sb("ones_row1", [1, 128], F32)
        ones_rbf = sb("ones_rbf", [1, 512], BF16)
        epscol = sb("epscol", [1, 1], F32)

        P = [stack.enter_context(nc.psum_tensor(f"P{i}{sfx}", [128, 512], F32))
             for i in range(8)]

        for s in ["lnt", "mm"]:
            k.sem(s)
        for s in ["ld", "wl0", "wl1", "bl", "tb", "st"]:
            k.sem(s, dma=True)
        S, V, T, G, SY = "scalar", "vector", "tensor", "gpsimd", "sync"

        k.op(G, lambda e: e.memset(ones_col[:], 1.0))
        k.op(G, lambda e: e.memset(ones_row1[:], 1.0))
        k.op(G, lambda e: e.memset(ones_rbf[:], 1.0))
        t_const = k.op(G, lambda e: e.memset(epscol[:], 1e-5))
        t_cs = []
        for (dst, src_) in [(mbs, mbk), (E8s, E8i), (E8Ts, E8Ti),
                            (psws, Psw), (eyes, eyef), (hss, hscol)]:
            t_cs.append(k.dma(SY, dst[:], src_, inc="ld"))
        t_lr = k.dma(SY, attf[0:1, 0:DM],
                     lrows[0, 2:3].rearrange("(o a) d -> o (a d)", o=1), inc="ld")
        t_b2a = k.op(S, lambda e: e.copy(out=b2b[0:1, 0:DM], in_=attf[0:1, :]),
                     wait=[t_lr])
        t_lr2 = k.dma(SY, attf[0:1, 0:DM],
                      lrows[1, 2:3].rearrange("(o a) d -> o (a d)", o=1),
                      wait=[t_b2a], inc="ld")
        t_b2b = k.op(S, lambda e: e.copy(out=b2b[0:1, DM:2 * DM], in_=attf[0:1, :]),
                     wait=[t_lr2, t_b2a])
        t_lg = k.dma(SY, lgb[:], lgbi, inc="ld")
        t_lb = t_lg
        t_qk = k.dma(SY, qkc[:], qkcol.rearrange("l a p -> p (l a)"), inc="ld",
                     slow=True)
        t_q8a = k.op(S, lambda e: e.mul(out=qkc[:, 0:1], in_=qkc[:, 0:1], mul=8.0),
                     wait=[t_qk])
        t_q8b = k.op(S, lambda e: e.mul(out=qkc[:, 2:3], in_=qkc[:, 2:3], mul=8.0),
                     wait=[t_qk])
        t_con = k.op(V, lambda e: e.nop(),
                     wait=t_cs + [t_b2b, t_q8a, t_q8b, t_const, t_lg, t_lb])

        st = {"qkvmm": None, "attmm": None, "womm": None, "w2mm": None,
              "w2mmlast": None,
              "ssmm": None, "ffnmm": None, "ystore": None,
              "xmid_tok": None, "hln_tok": None, "h1_tok": None,
              "h0_tok": None, "qr": None, "kr": {}, "vt": {},
              "o_tok": None, "eo_toks": [None] * 22, "roll": None,
              "sig": None, "evac": [None] * 8, "attread": None,
              "tabq": None, "tabk": None, "wtok": None, "b1tok": None,
              "w2cast": None, "w1cast": None, "lnlast": None}

        def load_layer_weights(l, waits):
            tf = st["w1cast"]
            for sl in range(12):
                cols = slice(sl * 128, sl * 128 + 128)
                t_l = k.dma(SY, w1as[:],
                            wqkv[l].rearrange("(c p) d -> p c d", p=128)[:, :, cols],
                            wait=[tf] + waits, inc="wl%d" % (sl % 2))
                for c in range(DC):
                    tf = k.op(V, lambda e, c=c, cols=cols: e.tensor_copy(
                        wqkvb[:, c, cols], w1as[:, c, :]), wait=[t_l])
            for sl in range(8):
                cols = slice(sl * 128, sl * 128 + 128)
                t_l = k.dma(SY, w1as[:, 0:4, :],
                            wo[l].rearrange("(c p) d -> p c d", p=128)[:, :, cols],
                            wait=[tf] + waits, inc="wl%d" % (sl % 2))
                for c in range(4):
                    tf = k.op(V, lambda e, c=c, cols=cols: e.tensor_copy(
                        wob[:, c, cols], w1as[:, c, :]), wait=[t_l])
            st["w1cast"] = tf
            return tf

        def hget(l, c, w0, shifted):
            off = -1 if shifted else 0
            if l == 0:
                return h0r[:, c, 1 + off:513 + off]
            b = w0 - 1024 + off
            return h1buf[:, c, b:b + 512]

        def hget_tile(l, c, w0, shifted, t0, n):
            a = hget(l, c, w0, shifted)
            return a[:, t0:t0 + n]

        def emit_h0_from_x(w0):
            out = []
            prev = None
            for sub in range(4):
                o = sub * 128
                t_x = k.dma(SY, w1as[:], fm_view(xc)[:, :, w0 + o:w0 + o + 128],
                            wait=[prev, st["roll"], st["lnlast"],
                                  st["w1cast"]], inc="ld")
                toks = emit_ln_fm(
                    k, lambda c: w1as[:, c, :], 128,
                    lambda c, o=o: h0r[:, c, 1 + o:1 + o + 128],
                    (P[0], P[1]), (P[2], P[3]), sq, rows,
                    ones_col[:], ones_row1[:],
                    [t_x, t_con, st["qkvmm"]], tag="ln",
                    eps_col=epscol[0:1, 0:1])
                out.extend(toks)
                prev = toks[-1]
            st["lnlast"] = prev
            return out

        def emit_qkv(l, w, w0):
            wt = st["wtok"]
            for which in range(2):
                tabsrc = (Cq, Sq) if which == 0 else (Ck, Sk)
                t_t1 = k.dma(SY, attf[:, 0:512], tabsrc[0][:, w0:w0 + 512],
                             wait=[st["tabq"], st["attread"]], inc="tb")
                t_t2 = k.dma(SY, attf[:, 512:1024], tabsrc[1][:, w0:w0 + 512],
                             wait=[], inc="tb")
                qp_toks = []
                sq_toks = []
                for oc in range(4):
                    col0 = which * 512 + oc * 128
                    tm = None
                    for c in range(DC):
                        tm = k.op(T, lambda e, c=c, oc=oc, col0=col0, l=l, w0=w0:
                                  e.matmul(P[4 + oc][:, :],
                                           wqkvb[:, c, col0:col0 + 128],
                                           hget(l, c, w0, c >= 4),
                                           start=(c == 0), stop=(c == DC - 1)),
                                  wait=[st["h0_tok"], wt, st["sig"],
                                        st["attread"], st["evac"][7],
                                        st.get("geread"), st["h1_tok"],
                                        st.get("yread"), st["xmid_tok"],
                                        st["hln_tok"]] if c == 0 else [],
                                  inc="mm" if c == DC - 1 else None)
                    qp_toks.append(tm)
                    t_s = k.op(S, lambda e, oc=oc: e.square(out=t1s[:],
                                                            in_=P[4 + oc][:, :]),
                               wait=[tm], inc=None)
                    t_ss = k.op(T, lambda e, oc=oc: e.matmul(
                        P[0][0:8, :], E8s[:, oc, :], t1s[:],
                        start=(oc == 0), stop=(oc == 3)),
                        wait=[t_s, t_con, st["lnlast"]],
                        inc="mm" if oc == 3 else None)
                    sq_toks.append(t_s)
                t_ssl = t_ss
                t_r1 = k.op(S, lambda e: e.sqrt(out=rss[:], in_=P[0][0:8, :]),
                            wait=[t_ssl])
                t_r2 = k.op(V, lambda e: e.tensor_scalar_max(rss[:], rss[:], 1e-12),
                            wait=[t_r1])
                t_r3 = k.op(V, lambda e: e.reciprocal(rss[:], rss[:]), wait=[t_r2])
                qcolix = which if l == 0 else 2 + which
                prev_m2 = None
                for oc in range(4):
                    t_rb = k.op(T, lambda e, oc=oc: e.matmul(
                        P[1][:, :], E8Ts[0:8, oc, :], rss[:],
                        start=True, stop=True),
                        wait=[t_r3, st["lnlast"], prev_m2], inc="mm")
                    t_rc = k.op(S, lambda e: e.copy(out=rbs[:], in_=P[1][:, :]),
                                wait=[t_rb])
                    t_qh = k.op(V, lambda e, oc=oc, qcolix=qcolix:
                                e.scalar_tensor_tensor(
                                    out=qh[:], in0=P[4 + oc][:, :],
                                    scalar=qkc[:, qcolix:qcolix + 1], in1=rbs[:],
                                    op0=ALU.mult, op1=ALU.mult),
                                wait=[t_rc, st["ystore"]], inc=None)
                    t_pw = k.op(T, lambda e: e.matmul(
                        P[1][:, :], psws[:], qh[:], start=True, stop=True),
                        wait=[t_qh, t_rc], inc="mm")
                    t_m1 = k.op(V, lambda e: e.tensor_mul(t1s[:], qh[:],
                                                          attf[:, 0:512]),
                                wait=[t_t1, t_t2, t_pw], inc=None)
                    t_m2 = k.op(V, lambda e: e.tensor_mul(gg[:], P[1][:, :],
                                                          attf[:, 512:1024]),
                                wait=[t_pw], inc=None)
                    prev_m2 = t_m2
                    if which == 0:
                        t_o = k.op(V, lambda e, oc=oc: e.tensor_add(
                            qrot[:, oc, :], t1s[:], gg[:]), wait=[], inc=None)
                        st["qr"] = t_o
                    else:
                        half = slice((w % 2) * 512, (w % 2) * 512 + 512)
                        t_o = k.op(V, lambda e, oc=oc, half=half: e.tensor_add(
                            krot[:, oc, half], t1s[:], gg[:]), wait=[], inc=None)
                        st["kr"][(w, oc)] = t_o
                st["tabq"] = t_o
                st["sig"] = t_o
            for j in range(4):
                slot = (w % 2) * 4 + j
                tsl = slice(j * 128, j * 128 + 128)
                ph = P[4 + j]
                tm = None
                for c in range(DC):
                    tm = k.op(T, lambda e, c=c, tsl=tsl, l=l, w0=w0, ph=ph:
                              e.matmul(ph[:, :],
                                       hget(l, c, w0, c >= 4)[:, tsl],
                                       wqkvb[:, c, 1024:1536],
                                       start=(c == 0), stop=(c == DC - 1)),
                              wait=[st["sig"], st["h0_tok"]] if c == 0 else [],
                              inc="mm" if c == DC - 1 else None)
                t_v = k.op(S, lambda e, slot=slot, ph=ph: e.copy(
                    out=vtm[:, slot, :], in_=ph[:, :]), wait=[tm])
                st["vt"][(w, j)] = t_v
                st["sig"] = t_v
            st["qkvmm"] = tm

        def emit_attn(l, w, w0, first_scored):
            use4 = (w == 4)
            for h in range(8):
                hc, hp = h // 2, 64 * (h % 2)
                for qc in range(4):
                    qsl = slice(qc * 128, qc * 128 + 128)
                    prevhalf = slice(((w - 1) % 2) * 512, ((w - 1) % 2) * 512 + 512)
                    curhalf = slice((w % 2) * 512, (w % 2) * 512 + 512)
                    t_s1 = k.op(T, lambda e, hp=hp, hc=hc, qsl=qsl, ph=prevhalf:
                                e.matmul(P[6][:, :], qrot[hp:hp + 64, hc, qsl],
                                         krot[hp:hp + 64, hc, ph],
                                         start=True, stop=True),
                                wait=[st["qr"], st["kr"].get((w - 1, hc), None),
                                      st["kr"].get((w, hc), None),
                                      st["attread"], st["sig"]], inc="mm")
                    t_s2 = k.op(T, lambda e, hp=hp, hc=hc, qsl=qsl, ch=curhalf:
                                e.matmul(P[7][:, :], qrot[hp:hp + 64, hc, qsl],
                                         krot[hp:hp + 64, hc, ch],
                                         start=True, stop=True),
                                wait=[], inc="mm")
                    tv1 = k.op(V, lambda e, qc=qc: e.scalar_tensor_tensor(
                        out=attf[:, 0:512], in0=mbs[:, qc, 0:512], scalar=1.0,
                        in1=P[6][:, :], op0=ALU.mult, op1=ALU.add),
                        wait=[t_s1, t_con])
                    if use4:
                        tv1 = k.op(V, lambda e: e.tensor_scalar(
                            out=attf[:, 0:512], in0=attf[:, 0:512],
                            scalar1=hss[:, 1:2], scalar2=None, op0=ALU.add),
                            wait=[])
                    tv2 = k.op(V, lambda e, qc=qc: e.scalar_tensor_tensor(
                        out=attf[:, 512:1024], in0=mbs[:, qc, 512:1024],
                        scalar=1.0, in1=P[7][:, :],
                        op0=ALU.mult, op1=ALU.add), wait=[t_s2])
                    st["attread"] = tv2
                    tv3 = k.op(V, lambda e: e.reduce_max(
                        mcol[:, 0:1], attf[:], axis=AX.X, negate=True), wait=[])
                    ts1 = k.op(S, lambda e: e.activation(
                        out=attf[:], in_=attf[:], func=AF.Exp,
                        bias=mcol[:, 0:1]), wait=[tv3])
                    tv4 = k.op(V, lambda e: e.reduce_sum(
                        mcol[:, 1:2], attf[:], axis=AX.X), wait=[ts1])
                    tv5 = k.op(V, lambda e: e.reciprocal(
                        mcol[:, 2:3], mcol[:, 1:2]), wait=[])
                    tv6 = k.op(V, lambda e: e.tensor_scalar(
                        out=attf[:], in0=attf[:], scalar1=mcol[:, 2:3],
                        scalar2=None, op0=ALU.mult), wait=[tv5])
                    tav = None
                    prev_cp = None
                    t_tp = None
                    for ktc in range(8):
                        slot = (((w - 1) % 2) * 4 + ktc) if ktc < 4 else \
                               ((w % 2) * 4 + (ktc - 4))
                        t_tp = k.op(T, lambda e, ktc=ktc: e.transpose(
                            P[5][:, 0:128], attf[:, ktc * 128:ktc * 128 + 128],
                            eyes[:]), wait=[tv6, t_con, prev_cp], inc="mm")
                        t_cp = k.op(S, lambda e: e.copy(out=attT[:],
                                                        in_=P[5][:, 0:128]),
                                    wait=[t_tp, tav])
                        tav = k.op(T, lambda e, ktc=ktc, slot=slot, h=h:
                                   e.matmul(P[4][0:64, 0:128],
                                            vtm[:, slot, h * 64:h * 64 + 64],
                                            attT[:],
                                            start=(ktc == 0), stop=(ktc == 7)),
                                   wait=[t_cp, st["vt"].get((w - 1, slot % 4), None)
                                         if ktc < 4 else
                                         st["vt"].get((w, slot % 4), None)],
                                   inc="mm")
                        prev_cp = t_cp
                    t_ob = k.op(S, lambda e, hp=hp, hc=hc, qsl=qsl: e.copy(
                        out=o_bf[hp:hp + 64, hc, qsl], in_=P[4][0:64, 0:128]),
                        wait=[tav])
                    st["o_tok"] = t_ob
                    st["sig"] = t_ob
                    st["attread"] = t_tp

        def emit_wo(l, w, w0):
            prev_bank = {0: None, 1: None}
            for dc in range(DC):
                ph = P[6] if dc % 2 == 0 else P[7]
                tm = None
                for c in range(4):
                    tm = k.op(T, lambda e, c=c, dc=dc, ph=ph: e.matmul(
                        ph[:, :], wob[:, c, dc * 128:dc * 128 + 128],
                        o_bf[:, c, :], start=(c == 0), stop=(c == 3)),
                        wait=[st["o_tok"], st["wtok"], st["evac"][dc],
                              prev_bank[dc % 2]] if c == 0 else [],
                        inc="mm" if c == 3 else None)
                t_e = k.op(V, lambda e, dc=dc, ph=ph, l=l, w0=w0: e.tensor_add(
                    xmid[:, dc, 1:513], hget(l, dc, w0, False), ph[:, :]),
                    wait=[tm, st["xmid_tok"], st["hln_tok"]])
                st["evac"][dc] = t_e
                prev_bank[dc % 2] = t_e
            st["womm"] = tm
            st["xmid_w"] = st["evac"][:]

        def emit_ffn(l, w, w0, sink):
            lntoks = []
            prevt = None
            for sub in range(2):
                o = sub * 256
                toks = emit_ln_fm(
                    k, lambda c, o=o: xmid[:, c, (1 + o if c < 4 else o):
                                           (1 + o if c < 4 else o) + 256], 256,
                    lambda c, o=o: hln[:, c, o:o + 256],
                    (P[0], P[1]), (P[2], P[3]), sq, rows,
                    ones_col[:], ones_row1[:],
                    [st["evac"][7], t_con, st["ffnmm"], prevt, st["lnlast"]],
                    tag="ln", eps_col=epscol[0:1, 0:1],
                    affine=lambda c: (lgb[:, c, l:l + 1], lgb[:, c, 2 + l:3 + l]))
                lntoks.extend(toks)
                prevt = toks[-1]
            st["hln_tok"] = lntoks[-1]
            st["lnlast"] = lntoks[-1]
            prev_w1 = st["w1cast"]
            prev_b1 = st["b1tok"]
            for hc in range(22):
                sz = 128 if hc < 21 else 42
                t_la = k.dma(SY, w1as[:, :, 0:sz],
                             w1[l].rearrange("(c p) d -> p c d", p=128)
                             [:, :, hc * 128:hc * 128 + sz],
                             wait=[prev_w1, st["lnlast"]], inc="wl0")
                t_ca = k.op(V, lambda e, sz=sz: e.tensor_copy(
                    w1ab[:, :, 0:sz], w1as[:, :, 0:sz]), wait=[t_la])
                t_lg_ = k.dma(SY, w1as[:, :, 0:sz],
                              w1[l].rearrange("(c p) d -> p c d", p=128)
                              [:, :, 2730 + hc * 128:2730 + hc * 128 + sz],
                              wait=[t_ca], inc="wl1")
                t_cg = k.op(V, lambda e, sz=sz: e.tensor_copy(
                    w1gb[:, :, 0:sz], w1as[:, :, 0:sz]), wait=[t_lg_])
                prev_w1 = t_cg
                st["w1cast"] = t_cg
                t_b = k.dma(SY, rows[0:1, 0:sz],
                            b1r[l:l + 1, hc * 128:hc * 128 + sz],
                            wait=[prev_b1, st["lnlast"]], inc="bl")
                t_b2 = k.dma(SY, rows[0:1, 128:128 + sz],
                             b1r[l:l + 1, 2730 + hc * 128:2730 + hc * 128 + sz],
                             wait=[], inc="bl")
                t_bc = k.op(S, lambda e: e.copy(out=b1sb[:], in_=rows[0:1, 0:256]),
                            wait=[t_b, t_b2])
                prev_b1 = t_bc
                st["b1tok"] = t_bc
                tma = tmg = None
                for c in range(DC):
                    tma = k.op(T, lambda e, c=c, sz=sz: e.matmul(
                        P[4][0:sz, :], w1ab[:, c, 0:sz], hln[:, c, :],
                        start=(c == 0), stop=False),
                        wait=[t_ca, st["hln_tok"], st["geread"]
                              if "geread" in st else None], inc=None)
                tma = k.op(T, lambda e, sz=sz: e.matmul(
                    P[4][0:sz, :], b1sb[0:1, 0:sz], ones_rbf[:],
                    start=False, stop=True), wait=[t_bc], inc="mm")
                for c in range(DC):
                    tmg = k.op(T, lambda e, c=c, sz=sz: e.matmul(
                        P[6][0:sz, :], w1gb[:, c, 0:sz], hln[:, c, :],
                        start=(c == 0), stop=False),
                        wait=[t_cg] if c == 0 else [], inc=None)
                tmg = k.op(T, lambda e, sz=sz: e.matmul(
                    P[6][0:sz, :], b1sb[0:1, 128:128 + sz], ones_rbf[:],
                    start=False, stop=True), wait=[t_bc], inc="mm")
                t_gl = k.op(S, lambda e, sz=sz: e.activation(
                    out=gg[0:sz, :], in_=P[6][0:sz, :], func=AF.Gelu),
                    wait=[tmg])
                t_eo = k.op(V, lambda e, hc=hc, sz=sz: e.tensor_mul(
                    eo[0:sz, hc, :], P[4][0:sz, :], gg[0:sz, :]),
                    wait=[t_gl, tma])
                st["eo_toks"][hc] = t_eo
                st["geread"] = t_eo
            prev_w2 = st["w2cast"]
            w2mms = [None] * 8
            for hc in range(22):
                sz = 128 if hc < 21 else 42
                for half in range(2):
                    dcl = slice(half * 512, half * 512 + 512)
                    t_l2 = k.dma(SY, w2st[0:sz, :],
                                 w2[l][hc * 128:hc * 128 + sz, dcl],
                                 wait=[prev_w2, st.get("w2mmlast")],
                                 inc="wl0" if half == 0 else "wl1")
                    t_c2 = k.op(V, lambda e, sz=sz: e.tensor_copy(
                        w2sb[0:sz, :], w2st[0:sz, :]), wait=[t_l2])
                    prev_w2 = t_c2
                    st["w2cast"] = t_c2
                    for dc4 in range(4):
                        dc = half * 4 + dc4
                        w2mms[dc] = k.op(T, lambda e, hc=hc, dc=dc, dc4=dc4, sz=sz:
                                         e.matmul(
                            P[dc][:, :], w2sb[0:sz, dc4 * 128:dc4 * 128 + 128],
                            eo[0:sz, hc, :], start=(hc == 0), stop=False),
                            wait=[t_c2, st["eo_toks"][hc]] +
                                 ([st["yread"] if "yread" in st else None,
                                   st["h1_tok"]] if hc == 0 else []),
                            inc="mm" if dc4 == 3 else None)
                        if dc4 == 3:
                            st["w2mmlast"] = w2mms[dc]
            for dc in range(DC):
                w2mms[dc] = k.op(T, lambda e, dc=dc, l=l: e.matmul(
                    P[dc][:, :], b2b[0:1, l * DM + dc * 128:l * DM + dc * 128 + 128],
                    ones_rbf[:], start=False, stop=True),
                    wait=[t_b2b], inc="mm")
            st["ffnmm"] = w2mms[7]
            if sink == "h1":
                for dc in range(DC):
                    if w == 1:
                        t_h = k.op(V, lambda e, dc=dc: e.tensor_add(
                            h1buf[:, dc, 0:1], xmid[:, dc, 512:513],
                            P[dc][:, 511:512]), wait=[w2mms[dc]])
                    else:
                        b = 512 * (w - 2) + 1
                        t_h = k.op(V, lambda e, dc=dc, b=b: e.tensor_add(
                            h1buf[:, dc, b:b + 512], xmid[:, dc, 1:513],
                            P[dc][:, :]), wait=[w2mms[dc]])
                    st["h1_tok"] = t_h
                st["w2mm"] = w2mms[7]
            else:
                for dc in range(DC):
                    t_t = k.op(V, lambda e, dc=dc: e.tensor_add(
                        t1s[:], P[dc][:, :], xmid[:, dc, 1:513]),
                        wait=[w2mms[dc], st["ystore"]])
                    t_x = k.dma(SY, qh[:], fm_view(xc)[:, dc, w0:w0 + 512],
                                wait=[st.get("yread"), st["ystore"]],
                                inc="ld")
                    t_y = k.op(V, lambda e: e.scalar_tensor_tensor(
                        out=qh[:], in0=t1s[:], scalar=0.5, in1=qh[:],
                        op0=ALU.mult, op1=ALU.add), wait=[t_x, t_t])
                    st["yread"] = t_y
                    st["ystore"] = k.dma(
                        SY, fm_view(y)[:, dc, w0 - 2049:w0 - 2049 + 512], qh[:],
                        wait=[t_y], inc="st")
                st["w2mm"] = w2mms[7]

        def emit_layer(l, h_src, sink):
            wfirst = 0 if l == 0 else 2
            sfirst = wfirst + 1
            ffirst = sfirst if sink == "h1" else sfirst + 1
            st["wtok"] = load_layer_weights(l, [st["qkvmm"], st["womm"]])
            t_z = k.op(G, lambda e: e.memset(xmid[:, :, 0:1], 0.0),
                       wait=[st["ffnmm"], st["hln_tok"], t_con,
                             st["xmid_tok"], st["evac"][7]])
            st["xmid_tok"] = t_z
            if l == 0:
                t_z0 = k.op(G, lambda e: e.memset(h0r[:, :, 0:1], 0.0),
                            wait=[st["qkvmm"], t_con])
                st["roll"] = t_z0
            for w in range(wfirst, 8):
                w0 = 1 + 512 * w
                if h_src == "x":
                    toks = emit_h0_from_x(w0)
                    st["h0_tok"] = toks[-1]
                else:
                    st["h0_tok"] = st["h1_tok"]
                emit_qkv(l, w, w0)
                if w >= sfirst:
                    emit_attn(l, w, w0, sfirst)
                    emit_wo(l, w, w0)
                    if w >= ffirst:
                        emit_ffn(l, w, w0, sink)
                    t_r = k.op(V, lambda e: e.tensor_copy(
                        xmid[:, :, 0:1], xmid[:, :, 512:513]),
                        wait=[st["evac"][7], st["hln_tok"], st["ffnmm"]])
                    if w + 1 == 4:
                        t_r = k.op(V, lambda e: e.tensor_scalar(
                            out=xmid[:, :, 0:1], in0=xmid[:, :, 0:1],
                            scalar1=hss[:, 0:1], scalar2=None, op0=ALU.mult),
                            wait=[t_r])
                    st["xmid_tok"] = t_r
                if h_src == "x":
                    st["roll"] = k.op(V, lambda e: e.tensor_copy(
                        h0r[:, :, 0:1], h0r[:, :, 512:513]),
                        wait=[st["qkvmm"], st["h0_tok"]])

        emit_layer(0, "x", "h1")
        emit_layer(1, "h1", "y")


def build_l0ffn():
    nc = bass.Bass("TRN2", target_bir_lowering=False, debug=False)
    xc = nc.dram_tensor("xc", [DM, 4097], F32, kind="ExternalInput").ap()
    wqkv = nc.dram_tensor("wqkv", [2, DM, 1536], F32, kind="ExternalInput").ap()
    wo = nc.dram_tensor("wo", [2, 512, DM], F32, kind="ExternalInput").ap()
    w1 = nc.dram_tensor("w1", [2, DM, 5460], F32, kind="ExternalInput").ap()
    w2 = nc.dram_tensor("w2", [2, 2730, DM], F32, kind="ExternalInput").ap()
    lrows = nc.dram_tensor("lrows", [2, 3, DM], F32, kind="ExternalInput").ap()
    b1r = nc.dram_tensor("b1r", [2, 5460], F32, kind="ExternalInput").ap()
    qkcol = nc.dram_tensor("qkcol", [2, 2, 128], F32, kind="ExternalInput").ap()
    lgbi = nc.dram_tensor("lgbi", [128, DC, 4], F32, kind="ExternalInput").ap()
    Cq = nc.dram_tensor("Cq", [128, 4097], F32, kind="ExternalInput").ap()
    Sq = nc.dram_tensor("Sq", [128, 4097], F32, kind="ExternalInput").ap()
    Ck = nc.dram_tensor("Ck", [128, 4097], F32, kind="ExternalInput").ap()
    Sk = nc.dram_tensor("Sk", [128, 4097], F32, kind="ExternalInput").ap()
    Psw = nc.dram_tensor("Psw", [128, 128], F32, kind="ExternalInput").ap()
    E8i = nc.dram_tensor("E8i", [128, 4, 8], F32, kind="ExternalInput").ap()
    E8Ti = nc.dram_tensor("E8Ti", [8, 4, 128], F32, kind="ExternalInput").ap()
    eyef = nc.dram_tensor("eyef", [128, 128], F32, kind="ExternalInput").ap()
    mbk = nc.dram_tensor("mbk", [128, 4, 1024], BF16, kind="ExternalInput").ap()
    hscol = nc.dram_tensor("hscol", [128, 2], F32, kind="ExternalInput").ap()
    y = nc.dram_tensor("y", [DM, OWN], F32, kind="ExternalOutput").ap()
    stack = ExitStack()
    with stack:
        k = Sched(nc, stack)
        with ExitStack() as ps:
            emit_l0ffn(nc, k, ps, xc, wqkv, wo, w1, w2, lrows, b1r, qkcol, lgbi,
                       Cq, Sq, Ck, Sk, Psw, E8i, E8Ti, eyef, mbk, hscol, y)
        k.raw_wait("sync", "st", k.cnt["st"])
        k.finalize()
    return nc


def _l0ffn_host_consts(core):
    r, h = core // 2, core % 2
    s = h * OWN
    g = np.arange(-2049, 2048, dtype=np.float64) + s
    gc = np.clip(g, 0, None)
    half = np.arange(0, 64, 2, dtype=np.float64)
    inv_freq = 1.0 / 10000.0 ** (half / 64.0)
    freqs = gc[:, None] * inv_freq[None, :]
    base = (half + 0.4 * 64.0) / (1.4 * 64.0)
    power = (gc - 4096 // 2) / float(512 // 2)
    scale = base[None, :] ** power[:, None]
    cos = np.cos(freqs)
    sin = np.sin(freqs)
    cq = (cos * scale).astype(np.float32)
    sq_ = (sin * scale).astype(np.float32)
    ck = (cos / scale).astype(np.float32)
    sk = (sin / scale).astype(np.float32)

    def tab(a):
        full = np.concatenate([a, a], axis=1)
        return np.ascontiguousarray(np.tile(full.T, (2, 1)).astype(np.float32))

    Psw = np.zeros((128, 128), np.float32)
    for m in range(128):
        if m % 64 < 32:
            Psw[m + 32, m] = -1.0
        else:
            Psw[m - 32, m] = 1.0
    E8 = np.zeros((128, 4, 8), np.float32)
    E8T = np.zeros((8, 4, 128), np.float32)
    for oc in range(4):
        E8[0:64, oc, 2 * oc] = 1.0
        E8[64:128, oc, 2 * oc + 1] = 1.0
        E8T[2 * oc, oc, 0:64] = 1.0
        E8T[2 * oc + 1, oc, 64:128] = 1.0
    mb = np.full((128, 4, 1024), -30000.0, np.float32)
    for j in range(4):
        for p in range(128):
            a = 128 * j + p
            mb[p, j, a:a + 513] = 0.0
    hs = np.zeros((128, 2), np.float32)
    hs[:, 0] = 1.0 if h == 1 else 0.0
    hs[:, 1] = 0.0 if h == 1 else -30000.0
    return {
        "Cq": tab(cq), "Sq": tab(sq_), "Ck": tab(ck), "Sk": tab(sk),
        "Psw": Psw, "E8i": E8, "E8Ti": E8T, "eyef": _EYE,
        "mbk": mb.astype(ml_dtypes.bfloat16),
        "hscol": hs,
    }


def run_l0ffn_launch(nc_f, x, W):
    wqkv = np.asarray(W["L0_ffn_wqkv"])
    wov = np.asarray(W["L0_ffn_wo"])
    w1v = np.asarray(W["L0_ffn_w1"])
    w2v = np.asarray(W["L0_ffn_w2"])
    lrows = np.stack([np.asarray(W["L0_ffn_ln_g"]), np.asarray(W["L0_ffn_ln_b"]),
                      np.asarray(W["L0_ffn_b2"])], axis=1).astype(np.float32)
    lgbi = np.stack([np.asarray(W["L0_ffn_ln_g"])[0].reshape(DC, 128).T,
                     np.asarray(W["L0_ffn_ln_g"])[1].reshape(DC, 128).T,
                     np.asarray(W["L0_ffn_ln_b"])[0].reshape(DC, 128).T,
                     np.asarray(W["L0_ffn_ln_b"])[1].reshape(DC, 128).T],
                    axis=2).astype(np.float32)
    b1v = np.asarray(W["L0_ffn_b1"]).astype(np.float32)
    qs = np.asarray(W["L0_ffn_q_scale"])
    ks = np.asarray(W["L0_ffn_k_scale"])
    qkcol = np.stack([np.stack([np.tile(qs[l], 2), np.tile(ks[l], 2)])
                      for l in range(2)]).astype(np.float32)
    base = {"wqkv": wqkv, "wo": wov, "w1": w1v, "w2": w2v,
            "lrows": lrows, "b1r": b1v, "qkcol": qkcol,
            "lgbi": np.ascontiguousarray(lgbi)}
    in_maps = []
    for c in range(NCORES):
        r, h = c // 2, c % 2
        s = h * OWN
        xrT = x[r].T
        ctx = np.zeros((DM, 4097), np.float32)
        lo = s - 2049
        src_lo = max(lo, 0)
        ctx[:, src_lo - lo:] = xrT[:, src_lo:s + OWN]
        m = dict(base, xc=ctx)
        m.update(_l0ffn_host_consts(c))
        in_maps.append(m)
    res = run_bass_kernel_spmd(nc_f, in_maps, core_ids=list(range(NCORES)))
    return _assemble_fm([np.asarray(res.results[c]["y"]) for c in range(NCORES)])


RG_PAIRS = [[0, 1], [2, 3], [4, 5], [6, 7]]


def build_fused(rg=None):
    rg = rg or RG_PAIRS
    nc = bass.Bass("TRN2", target_bir_lowering=False, debug=False)
    dt = nc.dram_tensor
    xc0 = dt("xc0", [DM, 4097], F32, kind="ExternalInput").ap()
    f_wqkv = dt("f_wqkv", [2, DM, 1536], F32, kind="ExternalInput").ap()
    f_wo = dt("f_wo", [2, 512, DM], F32, kind="ExternalInput").ap()
    f_w1 = dt("f_w1", [2, DM, 5460], F32, kind="ExternalInput").ap()
    f_w2 = dt("f_w2", [2, 2730, DM], F32, kind="ExternalInput").ap()
    f_lrows = dt("f_lrows", [2, 3, DM], F32, kind="ExternalInput").ap()
    f_b1r = dt("f_b1r", [2, 5460], F32, kind="ExternalInput").ap()
    f_qkcol = dt("f_qkcol", [2, 2, 128], F32, kind="ExternalInput").ap()
    f_lgbi = dt("f_lgbi", [128, DC, 4], F32, kind="ExternalInput").ap()
    f_Cq = dt("f_Cq", [128, 4097], F32, kind="ExternalInput").ap()
    f_Sq = dt("f_Sq", [128, 4097], F32, kind="ExternalInput").ap()
    f_Ck = dt("f_Ck", [128, 4097], F32, kind="ExternalInput").ap()
    f_Sk = dt("f_Sk", [128, 4097], F32, kind="ExternalInput").ap()
    f_Psw = dt("f_Psw", [128, 128], F32, kind="ExternalInput").ap()
    f_E8i = dt("f_E8i", [128, 4, 8], F32, kind="ExternalInput").ap()
    f_E8Ti = dt("f_E8Ti", [8, 4, 128], F32, kind="ExternalInput").ap()
    f_eyef = dt("f_eyef", [128, 128], F32, kind="ExternalInput").ap()
    f_mbk = dt("f_mbk", [128, 4, 1024], BF16, kind="ExternalInput").ap()
    f_hscol = dt("f_hscol", [128, 2], F32, kind="ExternalInput").ap()
    a_Wq = dt("a_Wq", [DM, DM], F32, kind="ExternalInput").ap()
    a_Wk = dt("a_Wk", [DM, DM], F32, kind="ExternalInput").ap()
    a_Wv = dt("a_Wv", [DM, DM], F32, kind="ExternalInput").ap()
    a_Wo = dt("a_Wo", [DM, DM], F32, kind="ExternalInput").ap()
    a_brows = dt("a_brows", [4, DM], F32, kind="ExternalInput").ap()
    a_tcol = dt("a_tcol", [128, 16], F32, kind="ExternalInput").ap()
    m_rwT = dt("m_rwT", [DM, 8], F32, kind="ExternalInput").ap()
    m_eb = dt("m_eb", [8], F32, kind="ExternalInput").ap()
    m_w1 = dt("m_w1", [8, DM, 2048], F32, kind="ExternalInput").ap()
    m_w3 = dt("m_w3", [8, DM, 2048], F32, kind="ExternalInput").ap()
    m_w2 = dt("m_w2", [8, 2048, DM], F32, kind="ExternalInput").ap()
    m_sw1 = dt("m_sw1", [DM, 2048], F32, kind="ExternalInput").ap()
    m_sw3 = dt("m_sw3", [DM, 2048], F32, kind="ExternalInput").ap()
    m_sw2 = dt("m_sw2", [2048, DM], F32, kind="ExternalInput").ap()
    t_Wq = dt("t_Wq", [DM, DM], F32, kind="ExternalInput").ap()
    t_Wk = dt("t_Wk", [DM, DM], F32, kind="ExternalInput").ap()
    t_Wv = dt("t_Wv", [DM, DM], F32, kind="ExternalInput").ap()
    t_Wo = dt("t_Wo", [DM, DM], F32, kind="ExternalInput").ap()
    t_vecs = dt("t_vecs", [13, DM], F32, kind="ExternalInput").ap()
    yout = dt("yout", [DM, OWN], F32, kind="ExternalOutput").ap()
    xb = [dt(f"xb{i}", [DM, OWN], F32).ap() for i in range(1, 6)]
    ag2 = dt("ag2", [2 * DM, OWN], F32).ap()
    ag4 = dt("ag4", [2 * DM, OWN], F32).ap()
    ctxb = dt("ctxb", [DM, 4097], F32).ap()
    cwscr = dt("cwscr", [8, OWN], F32).ap()

    stack = ExitStack()
    with stack:
        k = Sched(nc, stack)
        SY, V, G = "sync", "vector", "gpsimd"
        k.sem("ccA")
        k.sem("ccB")

        def mk_rowread(ag_ap, cctok):
            lo = ag_ap[0:DM, :].rearrange("(c p) t -> p c t", p=128)
            hi = ag_ap[DM:2 * DM, :].rearrange("(c p) t -> p c t", p=128)

            def rr(dst, t0, n, wait, slow=False):
                t1 = min(n, max(0, OWN - t0))
                tok = None
                if t1 > 0:
                    tok = k.dma(SY, dst[:, :, 0:t1], lo[:, :, t0:t0 + t1],
                                wait=list(wait) + [cctok], inc="ld",
                                slow=slow or t1 < 8)
                if t1 < n:
                    tok = k.dma(SY, dst[:, :, t1:n],
                                hi[:, :, t0 + t1 - OWN:t0 + n - OWN],
                                wait=list(wait) + [cctok], inc="ld",
                                slow=slow or (n - t1) < 8)
                return tok
            return rr

        with ExitStack() as ps:
            emit_l0ffn(nc, k, ps, xc0, f_wqkv, f_wo, f_w1, f_w2, f_lrows,
                       f_b1r, f_qkcol, f_lgbi, f_Cq, f_Sq, f_Ck, f_Sk, f_Psw,
                       f_E8i, f_E8Ti, f_eyef, f_mbk, f_hscol, xb[0])
        k.barrier()
        tcc = k.op(G, lambda e: e.collective_compute(
            "AllGather", ALU.bypass, replica_groups=rg,
            ins=[xb[0].opt()], outs=[ag2.opt()]), inc="ccA")
        with ExitStack() as ps:
            emit_l0attn(nc, k, ps, xb[0], xb[0], a_Wq, a_Wk, a_Wv, a_Wo,
                        a_brows, a_tcol, f_eyef, xb[1],
                        row_read=mk_rowread(ag2, tcc))
        k.barrier()
        with ExitStack() as ps:
            cb = ps.enter_context(nc.sbuf_tensor("cb", [128, DC, 512], F32))
            hs2 = ps.enter_context(nc.sbuf_tensor("hs2", [128, 2], F32))
            t_h = k.dma(SY, hs2[:], f_hscol, inc="ld")
            t_z = k.op(G, lambda e: e.memset(cb[:, :, 0:1], 0.0))
            t_z2 = k.dma(SY, fm_view(ctxb)[:, :, 0:1], cb[:, :, 0:1],
                         wait=[t_z], inc="st", slow=True)
            prev = t_z2
            aglo = ag2[0:DM, :].rearrange("(c p) t -> p c t", p=128)
            for j in range(4):
                sl = slice(j * 512, j * 512 + 512)
                t_l = k.dma(SY, cb[:], aglo[:, :, sl], wait=[prev, tcc],
                            inc="ld")
                t_m = k.op(V, lambda e: e.tensor_scalar(
                    out=cb[:], in0=cb[:], scalar1=hs2[:, 0:1], scalar2=None,
                    op0=ALU.mult), wait=[t_l, t_h])
                prev = k.dma(SY, fm_view(ctxb)[:, :, 1 + j * 512:1 + j * 512 + 512],
                             cb[:], wait=[t_m], inc="st")
            for j in range(4):
                sl = slice(j * 512, j * 512 + 512)
                t_l = k.dma(SY, cb[:], fm_view(xb[1])[:, :, sl], wait=[prev],
                            inc="ld")
                prev = k.dma(SY, fm_view(ctxb)[:, :, 2049 + j * 512:
                                               2049 + j * 512 + 512],
                             cb[:], wait=[t_l], inc="st")
        k.barrier()
        with ExitStack() as ps:
            emit_l0ffn(nc, k, ps, ctxb, f_wqkv, f_wo, f_w1, f_w2, f_lrows,
                       f_b1r, f_qkcol, f_lgbi, f_Cq, f_Sq, f_Ck, f_Sk, f_Psw,
                       f_E8i, f_E8Ti, f_eyef, f_mbk, f_hscol, xb[2])
        k.barrier()
        with ExitStack() as ps:
            emit_moe(nc, k, ps, xb[2], m_rwT, m_eb, m_w1, m_w3, m_w2, m_sw1,
                     m_sw3, m_sw2, f_eyef, xb[3], cwscr)
        k.barrier()
        tcc2 = k.op(G, lambda e: e.collective_compute(
            "AllGather", ALU.bypass, replica_groups=rg,
            ins=[xb[3].opt()], outs=[ag4.opt()]), inc="ccB")
        with ExitStack() as ps:
            emit_l1attn(nc, k, ps, xb[3], xb[3], t_Wq, t_Wk, t_Wv, t_Wo,
                        t_vecs, f_eyef, xb[4], row_read=mk_rowread(ag4, tcc2))
        k.barrier()
        with ExitStack() as ps:
            emit_moe(nc, k, ps, xb[4], m_rwT, m_eb, m_w1, m_w3, m_w2, m_sw1,
                     m_sw3, m_sw2, f_eyef, yout, cwscr)
        k.raw_wait("sync", "st", k.cnt["st"])
        k.finalize()
    return nc


def run_fused(nc_z, x, W):
    shards = _shards_fm(x)
    base = {
        "f_wqkv": np.asarray(W["L0_ffn_wqkv"]), "f_wo": np.asarray(W["L0_ffn_wo"]),
        "f_w1": np.asarray(W["L0_ffn_w1"]), "f_w2": np.asarray(W["L0_ffn_w2"]),
        "f_b1r": np.asarray(W["L0_ffn_b1"]).astype(np.float32),
        "a_Wq": np.asarray(W["L0_attn_Wq"]), "a_Wk": np.asarray(W["L0_attn_Wk"]),
        "a_Wv": np.asarray(W["L0_attn_Wv"]), "a_Wo": np.asarray(W["L0_attn_Wo"]),
        "m_rwT": np.ascontiguousarray(np.asarray(W["L1_ffn_router_w"]).T),
        "m_eb": np.asarray(W["L1_ffn_experts_bias"]),
        "m_w1": np.asarray(W["L1_ffn_w1"]), "m_w3": np.asarray(W["L1_ffn_w3"]),
        "m_w2": np.asarray(W["L1_ffn_w2"]),
        "m_sw1": np.asarray(W["L1_ffn_sw1"])[0],
        "m_sw3": np.asarray(W["L1_ffn_sw3"])[0],
        "m_sw2": np.asarray(W["L1_ffn_sw2"])[0],
        "t_Wq": np.asarray(W["L1_attn_Wq"]), "t_Wk": np.asarray(W["L1_attn_Wk"]),
        "t_Wv": np.asarray(W["L1_attn_Wv"]), "t_Wo": np.asarray(W["L1_attn_Wo"]),
    }
    base["f_lrows"] = np.stack([np.asarray(W["L0_ffn_ln_g"]),
                                np.asarray(W["L0_ffn_ln_b"]),
                                np.asarray(W["L0_ffn_b2"])], axis=1).astype(np.float32)
    qs, ks = np.asarray(W["L0_ffn_q_scale"]), np.asarray(W["L0_ffn_k_scale"])
    base["f_qkcol"] = np.stack([np.stack([np.tile(qs[l], 2), np.tile(ks[l], 2)])
                                for l in range(2)]).astype(np.float32)
    base["f_lgbi"] = np.ascontiguousarray(np.stack(
        [np.asarray(W["L0_ffn_ln_g"])[0].reshape(DC, 128).T,
         np.asarray(W["L0_ffn_ln_g"])[1].reshape(DC, 128).T,
         np.asarray(W["L0_ffn_ln_b"])[0].reshape(DC, 128).T,
         np.asarray(W["L0_ffn_ln_b"])[1].reshape(DC, 128).T], axis=2)).astype(np.float32)
    base["a_brows"] = np.stack([np.asarray(W["L0_attn_bq"]),
                                np.asarray(W["L0_attn_bk"]),
                                np.asarray(W["L0_attn_bv"]),
                                np.asarray(W["L0_attn_bo"])]).astype(np.float32)
    base["t_vecs"] = np.stack([
        W["L1_attn_gq"], W["L1_attn_bq"], W["L1_attn_mq"], W["L1_attn_vq"],
        W["L1_attn_gk"], W["L1_attn_bk"], W["L1_attn_mk"], W["L1_attn_vk"],
        W["L1_attn_gv"], W["L1_attn_bv"], W["L1_attn_mv"], W["L1_attn_vv"],
        W["L1_attn_bo"]]).astype(np.float32)
    in_maps = []
    for c in range(NCORES):
        r, h = c // 2, c % 2
        s = h * OWN
        xrT = x[r].T
        ctx = np.zeros((DM, 4097), np.float32)
        lo = s - 2049
        src_lo = max(lo, 0)
        ctx[:, src_lo - lo:] = xrT[:, src_lo:s + OWN]
        m = dict(base, xc0=ctx)
        cst = _l0ffn_host_consts(c)
        m.update({"f_" + kk: vv for kk, vv in cst.items()})
        tcolv = (s + np.arange(OWN).reshape(16, 128).T).astype(np.float32)
        m["a_tcol"] = np.ascontiguousarray(tcolv)
        in_maps.append(m)
    res = run_bass_kernel_spmd(nc_z, in_maps, core_ids=list(range(NCORES)))
    return _assemble_fm([np.asarray(res.results[c]["yout"]) for c in range(NCORES)])


_NC_CACHE = {}


def _get_nc(name, builder):
    if name not in _NC_CACHE:
        _NC_CACHE[name] = builder()
    return _NC_CACHE[name]


def kernel(**inputs):
    x = np.asarray(inputs["L0_attn_x"], np.float32)
    nc_z = _get_nc("fused", build_fused)
    return run_fused(nc_z, x, inputs)


# version 05
# speedup vs baseline: 1.2217x; 1.2217x over previous
import numpy as np
import ml_dtypes
from contextlib import ExitStack

import concourse.bass as bass
import concourse.mybir as mybir
from concourse.bass_utils import run_bass_kernel_spmd

F32 = mybir.dt.float32
BF16 = mybir.dt.bfloat16
AF = mybir.ActivationFunctionType
ALU = mybir.AluOpType
AX = mybir.AxisListType

NCORES = 8
ROWT = 4096
OWN = 2048
DM = 1024
DC = 8


class Sched:
    ENGINES = ("sync", "tensor", "vector", "scalar", "gpsimd")

    def __init__(self, nc, stack):
        self.nc = nc
        self.stack = stack
        self._log = {e: [] for e in self.ENGINES}
        self.dma_sems = set()
        self.dma_wmax = {}
        self.dma_barrier = {}
        self.ops = {e: [] for e in self.ENGINES}
        self.sems = {}
        self.cnt = {}
        self.wdone = {e: {} for e in self.ENGINES}

    def sem(self, name, dma=False):
        if name in self.sems:
            return name
        s = self.stack.enter_context(self.nc.semaphore(name))
        self.sems[name] = s
        self.cnt[name] = 0
        if dma:
            self.dma_sems.add(name)
        return name

    def barrier(self):
        marks = []
        if "mm" in self.sems:
            marks.append(self.op("tensor", lambda e: e.drain(), inc="mm"))
        for aname, eng in [("@vector", "vector"), ("@scalar", "scalar"),
                           ("@gpsimd", "gpsimd")]:
            if aname in self.sems and self.cnt[aname] > 0:
                marks.append((aname, self.cnt[aname], eng))
        for s in sorted(self.dma_sems):
            if self.cnt[s] > 0:
                marks.append((s, self.cnt[s], None))
        toks = []
        for e in self.ENGINES:
            toks.append(self.op(e, lambda en: en.nop(), wait=marks, inc=None))
        return toks

    AUTO = ("vector", "scalar", "gpsimd")

    def op(self, eng, build, wait=(), inc=None, inc_by=1):
        if eng in self.AUTO:
            aname = "@" + eng
            if aname not in self.sems:
                self.sem(aname)
            prev = self.cnt[aname]
            wait = list(wait) + ([(aname, prev, None)] if prev > 0 else [])
            inc, inc_by = aname, 1
        waits = []
        logwaits = []
        for tok in wait:
            if tok is None:
                continue
            sname, v, peng = tok
            if sname in self.dma_sems:
                v = self.cnt[sname]
                self.dma_wmax[sname] = max(self.dma_wmax.get(sname, 0), v)
            elif peng == eng:
                continue
            if v <= 0 or self.wdone[eng].get(sname, -1) >= v:
                continue
            self.wdone[eng][sname] = v
            waits.append((self.sems[sname], v))
            logwaits.append((sname, v))
        tok = None
        semh = None
        if inc is not None:
            self.cnt[inc] += inc_by
            tok = (inc, self.cnt[inc], eng)
            semh = self.sems[inc]

        def f(engine, waits=waits, build=build, semh=semh, inc_by=inc_by):
            for (s, v) in waits:
                engine.wait_ge(s, v)
            ins = build(engine)
            if semh is not None:
                ins.then_inc(semh, inc_by)

        self.ops[eng].append(f)
        self._log[eng].append((logwaits, None if inc is None else (inc, inc_by)))
        return tok

    def dma(self, eng, out, in_, wait=(), inc=None, slow=False):
        wait = list(wait)
        if inc in self.dma_sems:
            w = self.dma_wmax.get(inc, 0)
            if w > self.dma_barrier.get(inc, 0):
                self.dma_barrier[inc] = w
                wait.append((inc, w, None))
        if slow:
            return self.op(eng, lambda e, o=out, i=in_: e.dma_start(
                out=o, in_=i, allow_slow_non_contiguous=True),
                wait=wait, inc=inc, inc_by=16)
        return self.op(eng, lambda e, o=out, i=in_: e.dma_start(out=o, in_=i),
                       wait=wait, inc=inc, inc_by=16)

    def raw_wait(self, eng, sname, v):
        def f(engine, s=self.sems[sname], v=v):
            engine.wait_ge(s, v)
            engine.nop()
        self.ops[eng].append(f)
        self._log[eng].append(([(sname, v)], None))

    def check(self):
        import collections
        meta = {e: [] for e in self.ENGINES}
        qs = {e: collections.deque(self._log[e]) for e in self.ENGINES}
        val = collections.defaultdict(int)
        progress = True
        while progress:
            progress = False
            for e in self.ENGINES:
                while qs[e]:
                    waits, inc = qs[e][0]
                    if any(val[s] < v for (s, v) in waits):
                        break
                    if inc is not None:
                        val[inc[0]] += inc[1]
                    qs[e].popleft()
                    progress = True
        stuck = {e: len(qs[e]) for e in self.ENGINES if qs[e]}
        if stuck:
            for e in self.ENGINES:
                if qs[e]:
                    waits, inc = qs[e][0]
                    bad = [(s, v, val[s]) for (s, v) in waits if val[s] < v]
                    print(f"DEADLOCK {e}: waiting {bad}")
            raise RuntimeError(f"deadlock, remaining ops {stuck}")

    def finalize(self):
        self.check()
        blk = self.stack.enter_context(self.nc.Block())
        for e in self.ENGINES:
            lst = self.ops[e]
            if not lst:
                continue
            def runner(engine, lst=lst):
                for f in lst:
                    f(engine)
            getattr(blk, e)(runner)


_NAME_N = [0]


def _uniq():
    _NAME_N[0] += 1
    return "_u%d" % _NAME_N[0]


def fm_view(dram_ap):
    return dram_ap.rearrange("(c p) t -> p c t", p=128)


def emit_ln_fm(k, src_getter, n_tok, dst_put, ps_stat, ps_b, sq, rows,
               ones_col, ones_row1, ready_toks, tag, affine=None, eps_col=None,
               extra_dst=None):
    S, V, T, G = "scalar", "vector", "tensor", "gpsimd"
    nt = n_tok
    pm = ps_stat[0][0:1, 0:nt]
    pq = ps_stat[1][0:1, 0:nt]
    sq_toks = [None, None]
    mm_last = None
    for c in range(DC):
        xc = src_getter(c)
        t_sq = k.op(S, lambda e, s=sq[c % 2], x=xc, n=nt:
                    e.square(out=s[:, 0:n], in_=x),
                    wait=list(ready_toks) + [sq_toks[c % 2]], inc=tag + "s")
        k.op(T, lambda e, x=xc, st=(c == 0), sp=(c == DC - 1):
             e.matmul(pm, ones_col, x, start=st, stop=sp),
             wait=list(ready_toks), inc=None)
        mm_last = k.op(T, lambda e, s=sq[c % 2], st=(c == 0), sp=(c == DC - 1), n=nt:
                       e.matmul(pq, ones_col, s[:, 0:n], start=st, stop=sp),
                       wait=[t_sq], inc=tag + "t")
        sq_toks[c % 2] = mm_last
    r_mu = rows[0:1, 0:nt]
    r_rstd = rows[0:1, nt:2 * nt]
    r_tmp = rows[0:1, 2 * nt:3 * nt]
    r_mrs = rows[0:1, 3 * nt:4 * nt]
    r_mu2 = rows[0:1, 4 * nt:5 * nt]
    tv1 = k.op(V, lambda e: e.tensor_scalar_mul(r_mu, pm, 1.0 / DM),
               wait=[mm_last], inc=tag + "v")
    k.op(V, lambda e: e.tensor_scalar_mul(r_tmp, pq, 1.0 / DM), wait=[], inc=None)
    ts1 = k.op(S, lambda e: e.square(out=r_mu2, in_=r_mu), wait=[tv1], inc=tag + "s")
    tv2 = k.op(V, lambda e: e.tensor_sub(r_tmp, r_tmp, r_mu2), wait=[ts1], inc=tag + "v")
    ts2 = k.op(S, lambda e: e.activation(out=r_tmp, in_=r_tmp, func=AF.Ln,
                                         bias=eps_col), wait=[tv2], inc=tag + "s")
    ts3 = k.op(S, lambda e: e.activation(out=r_rstd, in_=r_tmp, func=AF.Exp, scale=-0.5),
               wait=[], inc=tag + "s")
    tv3 = k.op(V, lambda e: e.tensor_mul(r_mrs, r_mu, r_rstd), wait=[ts3], inc=tag + "v")
    rb = ps_b[0][:, 0:nt]
    mb = ps_b[1][:, 0:nt]
    tb1 = k.op(T, lambda e: e.matmul(rb, ones_row1, r_rstd, start=True, stop=True),
               wait=[ts3] + list(ready_toks), inc=tag + "t")
    tb2 = k.op(T, lambda e: e.matmul(mb, ones_row1, r_mrs, start=True, stop=True),
               wait=[tv3], inc=tag + "t")
    out_toks = []
    for c in range(DC):
        xc = src_getter(c)
        k.op(V, lambda e, x=xc, s=sq[c % 2], n=nt: e.tensor_mul(s[:, 0:n], x, rb),
             wait=[tb1], inc=None)
        if affine is None and extra_dst is None:
            to = k.op(V, lambda e, s=sq[c % 2], o=dst_put(c), n=nt:
                      e.tensor_sub(o, s[:, 0:n], mb), wait=[tb2], inc=tag + "v")
        elif affine is None:
            tf = k.op(V, lambda e, s=sq[c % 2], d=extra_dst(c), n=nt:
                      e.tensor_sub(d, s[:, 0:n], mb), wait=[tb2], inc=tag + "v")
            to = k.op(S, lambda e, d=extra_dst(c), o=dst_put(c):
                      e.copy(out=o, in_=d), wait=[tf], inc=tag + "s")
        else:
            lg_col, lb_col = affine(c)
            tf = k.op(V, lambda e, s=sq[c % 2], n=nt:
                      e.tensor_sub(s[:, 0:n], s[:, 0:n], mb), wait=[tb2], inc=tag + "v")
            to = k.op(V, lambda e, s=sq[c % 2], o=dst_put(c), lg=lg_col, lb=lb_col, n=nt:
                      e.tensor_scalar(out=o, in0=s[:, 0:n], scalar1=lg, scalar2=lb,
                                      op0=ALU.mult, op1=ALU.add),
                      wait=[tf], inc=tag + "v")
        out_toks.append(to)
    return out_toks


def emit_moe(nc, k, stack, xo, rwT, eb, w1, w3, w2, sw1, sw3, sw2, eye, y,
             cwscr, w1_b=None, w3_b=None, w2_b=None, sw1_b=None, sw3_b=None,
             sw2_b=None, ctok=None):
    if True:
        sfx = _uniq()
        sb = lambda n, sh, dt: stack.enter_context(nc.sbuf_tensor(n + sfx, sh, dt))
        psal = lambda n, sh: stack.enter_context(nc.psum_tensor(n + sfx, sh, F32))
        hbf = sb("hbf", [128, DC, OWN], BF16)
        eo = sb("eo", [128, 16, 1024], BF16)
        macc = sb("macc", [128, DC, 1024], F32)
        w2bf = sb("w2bf", [128, 16, DM], BF16)
        w1bf = [sb("w1bf0", [128, DC, 256], BF16), sb("w1bf1", [128, DC, 256], BF16)]
        w3bf = [sb("w3bf0", [128, DC, 256], BF16), sb("w3bf1", [128, DC, 256], BF16)]
        xt = sb("xt", [128, DC, 256], F32)
        sq = [sb("sq0", [128, 1024], F32), sb("sq1", [128, 1024], F32)]
        sg = sb("sg", [128, 1024], F32)
        tt_s = sb("tt_s", [128, 1024], F32)
        cwb_sb = sb("cwb_sb", [128, 1024], F32)
        cw_sb = sb("cw_sb", [128, 8], F32)
        m1c = sb("m1c", [128, 6], F32)
        eq1 = sb("eq1", [128, 8], F32)
        eq2 = sb("eq2", [128, 8], F32)
        cwrow = sb("cwrow", [1, 1024], F32)
        cwtmp = sb("cwtmp", [8, 128], F32)
        rows = sb("rows", [1, 6 * 256], F32)
        tksc = sb("tksc", [1, 512], F32)
        rwTs = sb("rwTs", [128, DC, 8], F32)
        rwTb = sb("rwTb", [128, DC, 8], BF16)
        ebs = sb("ebs", [1, 8], F32)
        ebb = sb("ebb", [1, 8], BF16)
        eye_sb = sb("eye_sb", [128, 128], F32)
        ones_col = sb("ones_col", [128, 1], F32)
        ones_row1 = sb("ones_row1", [1, 128], F32)
        ones_rbf = sb("ones_rbf", [1, 128], BF16)
        epscol = sb("epscol", [1, 1], F32)

        PGa = psal("PGa", [128, 512])
        PGb = psal("PGb", [128, 512])
        PUa = psal("PUa", [128, 512])
        PUb = psal("PUb", [128, 512])
        PCa = psal("PCa", [128, 512])
        PCb = psal("PCb", [128, 512])
        PWa = psal("PWa", [128, 512])
        PWb = psal("PWb", [128, 512])

        for s in ["lns", "lnt", "lnv", "lng",
                  "mm", "ev", "es", "eg", "cwv", "cws"]:
            k.sem(s)
        for s in ["ld", "wl0", "wl1", "w3l0", "w3l1", "st", "dtr"]:
            k.sem(s, dma=True)

        S, V, T, G, SY = "scalar", "vector", "tensor", "gpsimd", "sync"

        k.op(G, lambda e: e.memset(ones_col[:], 1.0), inc="eg")
        k.op(G, lambda e: e.memset(epscol[:], 1e-5), inc="eg")
        k.op(G, lambda e: e.memset(ones_row1[:], 1.0), inc=None)
        t_ones3 = k.op(G, lambda e: e.memset(ones_rbf[:], 1.0), inc="eg")
        t_ones = t_eps = t_ones3
        t_rw = k.dma(SY, rwTs[:], rwT.rearrange("(c p) e -> p c e", p=128), inc="ld")
        t_eb = k.dma(SY, ebs[:], eb.rearrange("(o e) -> o e", o=1), inc="ld")
        t_eye = k.dma(SY, eye_sb[:], eye, inc="ld")
        t_rwb = k.op(S, lambda e: e.copy(out=rwTb[:], in_=rwTs[:]), wait=[t_rw], inc="lns")
        t_ebb = k.op(S, lambda e: e.copy(out=ebb[:], in_=ebs[:]), wait=[t_eb], inc="lns")

        ln_done = []
        for sti in range(8):
            t0 = sti * 256
            t_x = k.dma(SY, xt[:], fm_view(xo)[:, :, t0:t0 + 256],
                        wait=[ln_done[-1] if ln_done else None], inc="ld")
            toks = emit_ln_fm(
                k, lambda c: xt[:, c, :], 256,
                lambda c, t0=t0: hbf[:, c, t0:t0 + 256],
                (PGa, PGb), (PUa, PUb), sq, rows, ones_col[:], ones_row1[:],
                [t_x, t_ones, t_eps], tag="ln", eps_col=epscol[0:1, 0:1])
            ln_done.extend(toks)
        ln_all = ln_done[-1]

        prev_pw_read = None
        dtr_last = None
        for q in range(16):
            for c in range(DC):
                k.op(T, lambda e, c=c, q=q: e.matmul(
                    PWa[:, 0:8], hbf[:, c, q * 128:q * 128 + 128], rwTb[:, c, :],
                    start=(c == 0), stop=False),
                    wait=[ln_all, t_rwb, prev_pw_read], inc=None)
            tl = k.op(T, lambda e: e.matmul(PWa[:, 0:8], ones_rbf[:], ebb[:],
                                            start=False, stop=True),
                      wait=[t_ebb, t_ones3], inc="mm")
            v1 = k.op(V, lambda e: e.reduce_max(m1c[:, 0:1], PWa[:, 0:8], axis=AX.X),
                      wait=[tl], inc="cwv")
            v2 = k.op(V, lambda e: e.tensor_scalar(out=eq1[:], in0=PWa[:, 0:8],
                                                   scalar1=m1c[:, 0:1], scalar2=None,
                                                   op0=ALU.is_equal), wait=[], inc="cwv")
            k.op(V, lambda e: e.scalar_tensor_tensor(
                out=sq[0][:, 0:8], in0=eq1[:], scalar=-1e30, in1=PWa[:, 0:8],
                op0=ALU.mult, op1=ALU.add), wait=[], inc=None)
            v4 = k.op(V, lambda e: e.reduce_max(m1c[:, 1:2], sq[0][:, 0:8], axis=AX.X),
                      wait=[], inc="cwv")
            v5 = k.op(V, lambda e: e.tensor_sub(m1c[:, 2:3], m1c[:, 1:2], m1c[:, 0:1]),
                      wait=[], inc=None)
            s1 = k.op(S, lambda e: e.activation(out=m1c[:, 4:5], in_=m1c[:, 2:3],
                                                func=AF.Sigmoid), wait=[v5], inc="cws")
            v6 = k.op(V, lambda e: e.tensor_scalar(out=m1c[:, 3:4], in0=m1c[:, 4:5],
                                                   scalar1=-1.0, scalar2=1.0,
                                                   op0=ALU.mult, op1=ALU.add),
                      wait=[s1], inc="cwv")
            v7 = k.op(V, lambda e: e.tensor_scalar(out=eq2[:], in0=PWa[:, 0:8],
                                                   scalar1=m1c[:, 1:2], scalar2=None,
                                                   op0=ALU.is_equal), wait=[], inc="cwv")
            prev_pw_read = v7
            k.op(V, lambda e: e.tensor_scalar(out=cw_sb[:], in0=eq1[:],
                                              scalar1=m1c[:, 3:4], scalar2=None,
                                              op0=ALU.mult), wait=[], inc=None)
            v9 = k.op(V, lambda e: e.scalar_tensor_tensor(
                out=cw_sb[:], in0=eq2[:], scalar=m1c[:, 4:5], in1=cw_sb[:],
                op0=ALU.mult, op1=ALU.add), wait=[], inc="cwv")
            tt1 = k.op(T, lambda e: e.transpose(PCa[0:8, 0:128], cw_sb[:], eye_sb[:]),
                       wait=[v9, t_eye, dtr_last], inc="mm")
            tt2 = k.op(S, lambda e: e.copy(out=cwtmp[:], in_=PCa[0:8, 0:128]),
                       wait=[tt1, dtr_last], inc="cws")
            dtr_last = k.dma(SY, cwscr[:, q * 128:q * 128 + 128], cwtmp[:],
                             wait=[tt2], inc="dtr")

        prev_ty = None
        last_w2mm = None
        prev_cwmm = None
        prev_st = None
        for tti in range(2):
            ts0 = tti * 1024
            t_ms = k.op(G, lambda e: e.memset(macc[:], 0.0),
                        wait=[prev_ty], inc="eg")
            for ei in range(9):
                shared = (ei == 8)
                w1_src = sw1 if shared else w1[ei]
                w3_src = sw3 if shared else w3[ei]
                w2_src = sw2 if shared else w2[ei]
                w_src_b = (sw1_b, sw3_b, sw2_b) if shared else \
                          (w1_b[ei], w3_b[ei], w2_b[ei])
                if not shared:
                    t_cw = k.dma(SY, cwrow[:], cwscr[ei:ei + 1, ts0:ts0 + 1024],
                                 wait=[dtr_last, prev_cwmm], inc="dtr")
                    tc1a = k.op(T, lambda e: e.matmul(
                        PCa[:, :], ones_row1[:], cwrow[0:1, 0:512],
                        start=True, stop=True), wait=[t_cw, last_w2mm], inc=None)
                    tc1b = k.op(T, lambda e: e.matmul(
                        PCb[:, :], ones_row1[:], cwrow[0:1, 512:1024],
                        start=True, stop=True), wait=[], inc="mm")
                    prev_cwmm = tc1b
                    tc2 = k.op(S, lambda e: e.copy(out=cwb_sb[:, 0:512], in_=PCa[:, :]),
                               wait=[tc1b], inc="cws")
                    tc2 = k.op(S, lambda e: e.copy(out=cwb_sb[:, 512:1024], in_=PCb[:, :]),
                               wait=[], inc="cws")
                else:
                    tc2 = None
                t_w2l = k.dma(SY, w2bf[:],
                              w_src_b[2].rearrange("(c p) d -> p c d", p=128),
                              wait=[last_w2mm, ctok], inc="wl0")
                w2b_toks = [t_w2l] * 16
                eo_toks = []
                prev_sg = prev_tt = prev_te = None
                prev_g_mm = prev_u_mm = None
                gpar = {0: None, 1: None}
                upar = {0: None, 1: None}
                for hs in range(8):
                    cols = slice(hs * 256, hs * 256 + 256)
                    bi = hs % 2
                    t_c1 = k.dma(SY, w1bf[bi][:],
                                 w_src_b[0].rearrange("(c p) h -> p c h", p=128)
                                 [:, :, cols],
                                 wait=[gpar[bi], last_w2mm, ctok],
                                 inc="wl0" if bi == 0 else "wl1")
                    t_c3 = k.dma(SY, w3bf[bi][:],
                                 w_src_b[1].rearrange("(c p) h -> p c h", p=128)
                                 [:, :, cols],
                                 wait=[upar[bi], last_w2mm, ctok],
                                 inc="w3l0" if bi == 0 else "w3l1")
                    for hh in range(2):
                        hc = hs * 2 + hh
                        hcol = slice(hh * 128, hh * 128 + 128)
                        for c in range(DC):
                            k.op(T, lambda e, c=c, hcol=hcol, ts0=ts0, bi=bi:
                                 e.matmul(
                                PGa[:, :], w1bf[bi][:, c, hcol],
                                hbf[:, c, ts0:ts0 + 512],
                                start=(c == 0), stop=(c == DC - 1)),
                                wait=[t_c1, ln_all, prev_sg], inc=None)
                            prev_g_mm = k.op(T, lambda e, c=c, hcol=hcol, ts0=ts0,
                                             bi=bi: e.matmul(
                                PGb[:, :], w1bf[bi][:, c, hcol],
                                hbf[:, c, ts0 + 512:ts0 + 1024],
                                start=(c == 0), stop=(c == DC - 1)),
                                wait=[], inc="mm" if c == DC - 1 else None)
                        for c in range(DC):
                            k.op(T, lambda e, c=c, hcol=hcol, ts0=ts0, bi=bi:
                                 e.matmul(
                                PUa[:, :], w3bf[bi][:, c, hcol],
                                hbf[:, c, ts0:ts0 + 512],
                                start=(c == 0), stop=(c == DC - 1)),
                                wait=[t_c3, prev_tt], inc=None)
                            prev_u_mm = k.op(T, lambda e, c=c, hcol=hcol, ts0=ts0,
                                             bi=bi: e.matmul(
                                PUb[:, :], w3bf[bi][:, c, hcol],
                                hbf[:, c, ts0 + 512:ts0 + 1024],
                                start=(c == 0), stop=(c == DC - 1)),
                                wait=[], inc="mm" if c == DC - 1 else None)
                        gpar[bi] = prev_g_mm
                        upar[bi] = prev_u_mm
                        k.op(S, lambda e: e.activation(out=sg[:, 0:512], in_=PGa[:, :],
                                                       func=AF.Silu),
                             wait=[prev_g_mm, prev_tt], inc="es")
                        t_sg = k.op(S, lambda e: e.activation(
                            out=sg[:, 512:1024], in_=PGb[:, :], func=AF.Silu),
                            wait=[], inc="es")
                        prev_sg = t_sg
                        k.op(V, lambda e: e.tensor_mul(tt_s[:, 0:512], sg[:, 0:512],
                                                       PUa[:, :]),
                             wait=[t_sg, prev_u_mm, prev_te], inc="ev")
                        t_t = k.op(V, lambda e: e.tensor_mul(
                            tt_s[:, 512:1024], sg[:, 512:1024], PUb[:, :]),
                            wait=[], inc="ev")
                        prev_tt = t_t
                        if shared:
                            t_e = k.op(G, lambda e, hc=hc: e.tensor_copy(
                                eo[:, hc, :], tt_s[:]), wait=[t_t], inc="eg")
                        else:
                            t_e = k.op(G, lambda e, hc=hc: e.tensor_mul(
                                eo[:, hc, :], tt_s[:], cwb_sb[:]),
                                wait=[t_t, tc2], inc="eg")
                        eo_toks.append(t_e)
                        prev_te = t_e
                prev_macc = {0: None, 1: None}
                for tsub in range(2):
                    tcol = slice(tsub * 512, tsub * 512 + 512)
                    pwh = (PWa if tsub == 0 else PWb)[:, :]
                    for dc in range(DC):
                        tm = None
                        for hc in range(16):
                            tm = k.op(T, lambda e, hc=hc, dc=dc, tcol=tcol, p=pwh: e.matmul(
                                p, w2bf[:, hc, dc * 128:dc * 128 + 128], eo[:, hc, tcol],
                                start=(hc == 0), stop=(hc == 15)),
                                wait=[w2b_toks[hc], eo_toks[hc]] +
                                     ([prev_macc[tsub]] if hc == 0 else []),
                                inc="mm" if hc == 15 else None)
                        last_w2mm = tm
                        prev_macc[tsub] = k.op(V, lambda e, dc=dc, tcol=tcol, p=pwh:
                            e.tensor_add(macc[:, dc, tcol], macc[:, dc, tcol], p),
                            wait=[tm, t_ms], inc="ev")
            lastv = k.op(V, lambda e: e.nop(), inc="ev")
            for sti in range(4):
                c0 = sti * 256
                t_x = k.dma(SY, xt[:], fm_view(xo)[:, :, ts0 + c0:ts0 + c0 + 256],
                            wait=[lastv, prev_st], inc="ld")
                t_y = k.op(V, lambda e, c0=c0: e.scalar_tensor_tensor(
                    out=xt[:], in0=macc[:, :, c0:c0 + 256], scalar=0.5, in1=xt[:],
                    op0=ALU.mult, op1=ALU.add), wait=[t_x], inc="ev")
                prev_st = k.dma(SY, fm_view(y)[:, :, ts0 + c0:ts0 + c0 + 256], xt[:],
                                wait=[t_y], inc="st")
                prev_ty = t_y


def build_moe():
    nc = bass.Bass("TRN2", target_bir_lowering=False, debug=False)
    xo = nc.dram_tensor("xo", [DM, OWN], F32, kind="ExternalInput").ap()
    rwT = nc.dram_tensor("rwT", [DM, 8], F32, kind="ExternalInput").ap()
    eb = nc.dram_tensor("eb", [8], F32, kind="ExternalInput").ap()
    w1 = nc.dram_tensor("w1", [8, DM, 2048], F32, kind="ExternalInput").ap()
    w3 = nc.dram_tensor("w3", [8, DM, 2048], F32, kind="ExternalInput").ap()
    w2 = nc.dram_tensor("w2", [8, 2048, DM], F32, kind="ExternalInput").ap()
    sw1 = nc.dram_tensor("sw1", [DM, 2048], F32, kind="ExternalInput").ap()
    sw3 = nc.dram_tensor("sw3", [DM, 2048], F32, kind="ExternalInput").ap()
    sw2 = nc.dram_tensor("sw2", [2048, DM], F32, kind="ExternalInput").ap()
    eye = nc.dram_tensor("eye", [128, 128], F32, kind="ExternalInput").ap()
    y = nc.dram_tensor("y", [DM, OWN], F32, kind="ExternalOutput").ap()
    cwscr = nc.dram_tensor("cwscr", [8, OWN], F32, kind="Internal").ap()
    w1_b = nc.dram_tensor("w1_b", [8, DM, 2048], BF16).ap()
    w3_b = nc.dram_tensor("w3_b", [8, DM, 2048], BF16).ap()
    w2_b = nc.dram_tensor("w2_b", [8, 2048, DM], BF16).ap()
    sw1_b = nc.dram_tensor("sw1_b", [DM, 2048], BF16).ap()
    sw3_b = nc.dram_tensor("sw3_b", [DM, 2048], BF16).ap()
    sw2_b = nc.dram_tensor("sw2_b", [2048, DM], BF16).ap()
    stack = ExitStack()
    with stack:
        k = Sched(nc, stack)
        with ExitStack() as cs:
            ctok = emit_bf16_cache(nc, k, cs, [
                (w1, w1_b, 8 * DM * 2048), (w3, w3_b, 8 * DM * 2048),
                (w2, w2_b, 8 * 2048 * DM), (sw1, sw1_b, DM * 2048),
                (sw3, sw3_b, DM * 2048), (sw2, sw2_b, 2048 * DM)], tag="m")
        with ExitStack() as ps:
            emit_moe(nc, k, ps, xo, rwT, eb, w1, w3, w2, sw1, sw3, sw2, eye, y,
                     cwscr, w1_b=w1_b, w3_b=w3_b, w2_b=w2_b, sw1_b=sw1_b,
                     sw3_b=sw3_b, sw2_b=sw2_b, ctok=ctok)
        k.raw_wait("sync", "st", k.cnt["st"])
        k.finalize()
    return nc


def _shards_fm(x):
    out = []
    for core in range(NCORES):
        r, h = core // 2, core % 2
        out.append(np.ascontiguousarray(x[r, h * OWN:(h + 1) * OWN, :].T))
    return out


def _assemble_fm(shards):
    x = np.empty((4, ROWT, DM), np.float32)
    for core in range(NCORES):
        r, h = core // 2, core % 2
        x[r, h * OWN:(h + 1) * OWN, :] = shards[core].T
    return x


_EYE = np.eye(128, dtype=np.float32)


def run_moe_launch(nc_moe, x, W):
    shards = _shards_fm(x)
    base = {
        "rwT": np.ascontiguousarray(np.asarray(W["L1_ffn_router_w"]).T),
        "eb": np.asarray(W["L1_ffn_experts_bias"]),
        "w1": np.asarray(W["L1_ffn_w1"]), "w3": np.asarray(W["L1_ffn_w3"]),
        "w2": np.asarray(W["L1_ffn_w2"]),
        "sw1": np.asarray(W["L1_ffn_sw1"])[0], "sw3": np.asarray(W["L1_ffn_sw3"])[0],
        "sw2": np.asarray(W["L1_ffn_sw2"])[0], "eye": _EYE,
    }
    in_maps = [dict(base, xo=shards[c]) for c in range(NCORES)]
    res = run_bass_kernel_spmd(nc_moe, in_maps, core_ids=list(range(NCORES)))
    return _assemble_fm([np.asarray(res.results[c]["y"]) for c in range(NCORES)])


def emit_l1attn(nc, k, stack, xo, xr, Wq, Wk, Wv, Wo, vecs, eye, y,
                row_read=None):
    if True:
        sfx = _uniq()
        sb = lambda n, sh, dt: stack.enter_context(nc.sbuf_tensor(n + sfx, sh, dt))
        psal = lambda n, sh: stack.enter_context(nc.psum_tensor(n + sfx, sh, F32))
        vtm = sb("vtm", [128, 32, DM], BF16)
        wb1 = sb("wb1", [128, DC, DM], BF16)
        wb2 = sb("wb2", [128, DC, DM], BF16)
        wst = sb("wst", [128, DC, 256], F32)
        hkb = sb("hkb", [128, DC, 512], BF16)
        xt = sb("xt", [128, DC, 256], F32)
        osb = sb("osb", [128, DC, 512], BF16)
        qs_sb = sb("qs_sb", [128, 512], F32)
        sq = [sb("sq0", [128, 256], F32), sb("sq1", [128, 256], F32)]
        tqb = sb("tqb", [128, OWN], F32)
        tkcol = sb("tkcol", [128, 32], F32)
        vscol = sb("vscol", [128, 8], F32)
        dt_s = sb("dt_s", [128, 512], F32)
        ab_s = sb("ab_s", [128, 512], F32)
        et_s = sb("et_s", [128, 512], F32)
        mt_s = sb("mt_s", [128, 512], F32)
        atb = [sb("atb0", [128, 512], BF16), sb("atb1", [128, 512], BF16)]
        rows = sb("rows", [1, 6 * 256], F32)
        tksc = sb("tksc", [1, 512], F32)
        grows = sb("grows", [1, 8 * DM], F32)
        browb = sb("browb", [1, 4 * DM], BF16)
        eye_sb = sb("eye_sb", [128, 128], F32)
        ones_col = sb("ones_col", [128, 1], F32)
        ones_row1 = sb("ones_row1", [1, 128], F32)
        ones_rbf = sb("ones_rbf", [1, 512], BF16)
        ones_cbf = sb("ones_cbf", [128, 1], BF16)
        epscol = sb("epscol", [1, 1], F32)

        P = [psal(f"P{i}", [128, 512]) for i in range(8)]

        for s in ["lnt", "mm"]:
            k.sem(s)
        for s in ["ld", "wl0", "wl1", "st"]:
            k.sem(s, dma=True)
        S, V, T, G, SY = "scalar", "vector", "tensor", "gpsimd", "sync"
        if row_read is None:
            def row_read(dst, t0, n, wait):
                return k.dma(SY, dst, fm_view(xr)[:, :, t0:t0 + n],
                             wait=wait, inc="ld")

        k.op(G, lambda e: e.memset(ones_col[:], 1.0))
        k.op(G, lambda e: e.memset(ones_row1[:], 1.0))
        k.op(G, lambda e: e.memset(ones_rbf[:], 1.0))
        k.op(G, lambda e: e.memset(ones_cbf[:], 1.0))
        t_const = k.op(G, lambda e: e.memset(epscol[:], 1e-5))
        t_eye = k.dma(SY, eye_sb[:], eye, inc="ld")

        r_t = grows[0:1, 7 * DM:8 * DM]
        c_toks = []
        gam_toks = []
        prev_vl = None
        for pi in range(3):
            t_vl = k.dma(SY, grows[0:1, 0:4 * DM],
                         vecs[4 * pi:4 * pi + 4].rearrange("(o a) d -> o (a d)", o=1),
                         wait=[prev_vl], inc="ld")
            g_r = grows[0:1, 0:DM]
            b_r = grows[0:1, DM:2 * DM]
            m_r = grows[0:1, 2 * DM:3 * DM]
            v_r = grows[0:1, 3 * DM:4 * DM]
            gam = grows[0:1, (4 + pi) * DM:(5 + pi) * DM]
            t1 = k.op(S, lambda e, v_r=v_r: e.activation(
                out=r_t, in_=v_r, func=AF.Ln, bias=epscol[0:1, 0:1]),
                wait=[t_vl, t_const])
            t2 = k.op(S, lambda e: e.activation(out=r_t, in_=r_t, func=AF.Exp,
                                                scale=-0.5))
            t3 = k.op(V, lambda e, g_r=g_r, gam=gam: e.tensor_mul(gam, g_r, r_t),
                      wait=[t2])
            gam_toks.append(t3)
            t4 = k.op(V, lambda e, m_r=m_r, gam=gam: e.tensor_mul(r_t, m_r, gam))
            t5 = k.op(V, lambda e, b_r=b_r: e.tensor_sub(r_t, b_r, r_t))
            t6 = k.op(V, lambda e: e.tensor_scalar(out=r_t, in0=r_t, scalar1=1.0,
                                                   scalar2=-1.0, op0=ALU.mult,
                                                   op1=ALU.add))
            tc = k.op(S, lambda e, pi=pi: e.copy(
                out=browb[0:1, pi * DM:(pi + 1) * DM], in_=r_t), wait=[t6])
            c_toks.append(tc)
            prev_vl = tc
        t_bo = k.dma(SY, grows[0:1, 0:DM],
                     vecs[12:13].rearrange("(o a) d -> o (a d)", o=1),
                     wait=[prev_vl], inc="ld")
        t_bo = k.op(S, lambda e: e.copy(out=browb[0:1, 3 * DM:4 * DM],
                                        in_=grows[0:1, 0:DM]), wait=[t_bo])

        def fold_w(wsrc, wdst, pi, waitfor):
            last = waitfor
            if pi is not None:
                gam = grows[0:1, (4 + pi) * DM:(5 + pi) * DM]
                gb1 = k.op(T, lambda e, gam=gam: e.matmul(
                    P[6][:, :], ones_row1[:], gam[0:1, 0:512], start=True, stop=True),
                    wait=[gam_toks[pi]] + waitfor, inc="mm")
                gb2 = k.op(T, lambda e, gam=gam: e.matmul(
                    P[7][:, :], ones_row1[:], gam[0:1, 512:1024], start=True, stop=True),
                    wait=[], inc="mm")
            toks = []
            tf = None
            for qtr in range(4):
                cols = slice(qtr * 256, qtr * 256 + 256)
                t_l = k.dma(SY, wst[:],
                            wsrc.rearrange("(c p) d -> p c d", p=128)[:, :, cols],
                            wait=[tf] + waitfor, inc="wl%d" % (qtr % 2))
                for c in range(DC):
                    if pi is None:
                        tf = k.op(V, lambda e, c=c, cols=cols: e.tensor_copy(
                            wdst[:, c, cols], wst[:, c, :]), wait=[t_l])
                    else:
                        pp = P[6] if qtr < 2 else P[7]
                        po = (qtr % 2) * 256
                        tf = k.op(V, lambda e, c=c, cols=cols, pp=pp, po=po:
                                  e.tensor_mul(wdst[:, c, cols], wst[:, c, :],
                                               pp[:, po:po + 256]),
                                  wait=[t_l, gb2])
                toks.append(tf)
            return toks[-1]

        t_wk = fold_w(Wk, wb1, 1, [])
        t_wv = fold_w(Wv, wb2, 2, [t_wk])

        def emit_ln_tile(reader, t0, waits):
            out_toks = []
            t_prev = None
            for sub in range(2):
                o = sub * 256
                t_x = reader(xt[:], t0 + o, 256, waits + [t_prev])
                toks = emit_ln_fm(
                    k, lambda c: xt[:, c, :], 256,
                    lambda c, o=o: hkb[:, c, o:o + 256],
                    (P[2], P[3]), (P[4], P[5]), sq, rows, ones_col[:], ones_row1[:],
                    [t_x, t_const], tag="ln", eps_col=epscol[0:1, 0:1])
                out_toks.extend(toks)
                t_prev = toks[-1]
            return out_toks[-1]

        vtm_toks = [None] * 32
        prev_sig = None
        prev_ln = None
        tq_tok = None
        tkc_toks = []
        tqb_toks = []
        for ti in range(12):
            qphase = ti >= 8
            t0 = (ti - 8) * 512 if qphase else ti * 512
            if qphase:
                rd = lambda dst, tt0, n, wait: k.dma(
                    SY, dst, fm_view(xo)[:, :, tt0:tt0 + n], wait=wait, inc="ld")
            else:
                rd = row_read
            t_ln = emit_ln_tile(rd, t0, [prev_ln])
            if not qphase:
                for j in range(4):
                    ktc = ti * 4 + j
                    tsl = slice(j * 128, j * 128 + 128)
                    for half in range(2):
                        ph = P[6] if half == 0 else P[7]
                        dcols = slice(half * 512, half * 512 + 512)
                        for c in range(DC):
                            k.op(T, lambda e, c=c, tsl=tsl, dcols=dcols, ph=ph:
                                 e.matmul(ph[:, :], hkb[:, c, tsl], wb2[:, c, dcols],
                                          start=(c == 0), stop=False),
                                 wait=[t_ln, t_wv, prev_sig], inc=None)
                        tm = k.op(T, lambda e, half=half, ph=ph: e.matmul(
                            ph[:, :], ones_rbf[0:1, 0:128],
                            browb[0:1, 2 * DM + half * 512:2 * DM + half * 512 + 512],
                            start=False, stop=True),
                            wait=[c_toks[2]], inc="mm")
                        prev_sig = k.op(S, lambda e, ktc=ktc, dcols=dcols, ph=ph:
                                        e.activation(out=vtm[:, ktc, dcols],
                                                     in_=ph[:, :],
                                                     func=AF.Sigmoid, scale=4.0),
                                        wait=[tm])
                    vtm_toks[ktc] = prev_sig
            wb = wb1
            crow_off = 0 if qphase else DM
            w_tok = t_wq if qphase else t_wk
            tsum = None
            for dc in range(DC):
                ph = P[6] if dc % 2 == 0 else P[7]
                for c in range(DC):
                    k.op(T, lambda e, c=c, dc=dc, wb=wb, ph=ph: e.matmul(
                        ph[:, :], wb[:, c, dc * 128:dc * 128 + 128], hkb[:, c, :],
                        start=(c == 0), stop=False),
                        wait=[t_ln, w_tok, prev_sig], inc=None)
                tm = k.op(T, lambda e, dc=dc, ph=ph, crow_off=crow_off: e.matmul(
                    ph[:, :],
                    browb[0:1, crow_off + dc * 128:crow_off + dc * 128 + 128],
                    ones_rbf[:], start=False, stop=True),
                    wait=[c_toks[0] if qphase else c_toks[1]], inc="mm")
                prev_sig = k.op(S, lambda e, ph=ph: e.activation(
                    out=qs_sb[:], in_=ph[:, :], func=AF.Sigmoid, scale=4.0),
                    wait=[tm])
                tsum = k.op(T, lambda e, st=(dc == 0), sp=(dc == DC - 1): e.matmul(
                    P[2][0:1, 0:512], ones_col[:], qs_sb[:], start=st, stop=sp),
                    wait=[prev_sig, prev_ln], inc="mm")
            tq_tok = k.op(V, lambda e: e.tensor_scalar(
                out=tksc[:], in0=P[2][0:1, 0:512],
                scalar1=-1.0 / DM, scalar2=1.0, op0=ALU.mult, op1=ALU.add),
                wait=[tsum])
            if qphase:
                qt = ti - 8
                tb = k.op(T, lambda e: e.matmul(
                    P[7][:, :], ones_row1[:], tksc[:], start=True, stop=True),
                    wait=[tq_tok, prev_sig, t_eye], inc="mm")
                tq_tok = k.op(S, lambda e, qt=qt: e.copy(
                    out=tqb[:, qt * 512:qt * 512 + 512], in_=P[7][:, :]), wait=[tb])
                tqb_toks.append(tq_tok)
            else:
                for j in range(4):
                    tt1 = k.op(T, lambda e, j=j: e.matmul(
                        P[7][0:128, 0:1], tksc[0:1, j * 128:j * 128 + 128],
                        ones_row1[0:1, 0:1], start=True, stop=True),
                        wait=[tq_tok, prev_sig] +
                             ([tkc_toks[-1]] if tkc_toks else []), inc="mm")
                    tq_tok = k.op(S, lambda e, ti=ti, j=j: e.copy(
                        out=tkcol[:, ti * 4 + j:ti * 4 + j + 1],
                        in_=P[7][0:128, 0:1]), wait=[tt1])
                    tkc_toks.append(tq_tok)
            prev_ln = tq_tok
            if ti == 7:
                t_wq = fold_w(Wq, wb1, 0, [prev_sig, tq_tok])

        vs_toks = []
        prev_vs = tq_tok
        for half in range(2):
            tm = None
            for ktc in range(32):
                tm = k.op(T, lambda e, ktc=ktc, half=half, st=(ktc == 0),
                          sp=(ktc == 31): e.matmul(
                    P[6][0:1, 0:512], ones_cbf[:],
                    vtm[:, ktc, half * 512:half * 512 + 512],
                    start=st, stop=sp),
                    wait=[vtm_toks[ktc]] + ([prev_vs] if ktc == 0 else []), inc="mm")
            tr = k.op(V, lambda e: e.tensor_scalar_mul(
                rows[0:1, 0:512], P[6][0:1, 0:512], 0.9), wait=[tm])
            for j in range(4):
                tt1 = k.op(T, lambda e, j=j: e.matmul(
                    P[7][0:128, 0:1], rows[0:1, j * 128:j * 128 + 128],
                    ones_row1[0:1, 0:1], start=True, stop=True),
                    wait=[tr, prev_vs], inc="mm")
                prev_vs = k.op(S, lambda e, j=j, half=half: e.copy(
                    out=vscol[:, half * 4 + j:half * 4 + j + 1],
                    in_=P[7][0:128, 0:1]), wait=[tt1])
                vs_toks.append(prev_vs)
        t_vs = vs_toks[-1]

        t_wo = fold_w(Wo, wb2, None, [vtm_toks[31], prev_sig, t_vs])

        prev_st = None
        prev_add = None
        prev_osb = None
        at_rd = [None, None]
        for qt in range(4):
            qsl = slice(qt * 512, qt * 512 + 512)
            av_last = None
            for ktc in range(32):
                bi = ktc % 2
                tv1 = k.op(V, lambda e, qsl=qsl, ktc=ktc: e.tensor_scalar(
                    out=dt_s[:], in0=tqb[:, qsl], scalar1=tkcol[:, ktc:ktc + 1],
                    scalar2=None, op0=ALU.subtract),
                    wait=[tkc_toks[ktc], tqb_toks[qt], at_rd[bi]])
                tg1 = k.op(S, lambda e: e.activation(out=ab_s[:], in_=dt_s[:],
                                                     func=AF.Abs), wait=[tv1])
                ts1 = k.op(S, lambda e: e.activation(out=et_s[:], in_=ab_s[:],
                                                     func=AF.Exp, scale=-2.0),
                           wait=[tg1])
                tv2 = k.op(V, lambda e: e.tensor_scalar(
                    out=mt_s[:], in0=dt_s[:], scalar1=0.0, scalar2=None,
                    op0=ALU.is_lt), wait=[tv1])
                tg2 = k.op(V, lambda e: e.tensor_scalar(
                    out=mt_s[:], in0=mt_s[:], scalar1=2.0, scalar2=-1.0,
                    op0=ALU.mult, op1=ALU.add), wait=[tv2])
                tv3 = k.op(V, lambda e, bi=bi: e.tensor_mul(atb[bi][:], mt_s[:],
                                                            et_s[:]),
                           wait=[tg2, ts1])
                for dc in range(DC):
                    av_last = k.op(T, lambda e, dc=dc, ktc=ktc, bi=bi,
                                   st=(ktc == 0), sp=(ktc == 31): e.matmul(
                        P[dc][:, :], vtm[:, ktc, dc * 128:dc * 128 + 128],
                        atb[bi][:], start=st, stop=sp),
                        wait=[tv3, vtm_toks[ktc], t_wo] +
                             ([prev_osb] if ktc == 0 else []),
                        inc="mm" if dc == DC - 1 else None)
                at_rd[bi] = av_last
            for dc in range(DC):
                prev_osb = k.op(V, lambda e, dc=dc: e.tensor_scalar(
                    out=osb[:, dc, :], in0=P[dc][:, :], scalar1=0.9,
                    scalar2=vscol[:, dc:dc + 1], op0=ALU.mult, op1=ALU.add),
                    wait=[av_last, t_vs])
            t0 = qt * 512
            for sub in range(2):
                o = sub * 256
                t_x = k.dma(SY, xt[:], fm_view(xo)[:, :, t0 + o:t0 + o + 256],
                            wait=[prev_st, prev_add, prev_ln], inc="ld")
                for dc in range(DC):
                    ph = P[dc % 2]
                    for c in range(DC):
                        k.op(T, lambda e, c=c, dc=dc, ph=ph, o=o: e.matmul(
                            ph[:, 0:256], wb2[:, c, dc * 128:dc * 128 + 128],
                            osb[:, c, o:o + 256], start=(c == 0), stop=False),
                            wait=[prev_osb, t_wo] + ([prev_add] if c == 0 else []),
                            inc=None)
                    tm = k.op(T, lambda e, dc=dc, ph=ph: e.matmul(
                        ph[:, 0:256],
                        browb[0:1, 3 * DM + dc * 128:3 * DM + dc * 128 + 128],
                        ones_rbf[0:1, 0:256], start=False, stop=True),
                        wait=[t_bo], inc="mm")
                    prev_add = k.op(V, lambda e, dc=dc, ph=ph: e.tensor_add(
                        xt[:, dc, :], xt[:, dc, :], ph[:, 0:256]), wait=[tm, t_x])
                prev_st = k.dma(SY, fm_view(y)[:, :, t0 + o:t0 + o + 256], xt[:],
                                wait=[prev_add], inc="st")




def build_l1attn():
    nc = bass.Bass("TRN2", target_bir_lowering=False, debug=False)
    xo = nc.dram_tensor("xo", [DM, OWN], F32, kind="ExternalInput").ap()
    xr = nc.dram_tensor("xr", [DM, ROWT], F32, kind="ExternalInput").ap()
    Wq = nc.dram_tensor("Wq", [DM, DM], F32, kind="ExternalInput").ap()
    Wk = nc.dram_tensor("Wk", [DM, DM], F32, kind="ExternalInput").ap()
    Wv = nc.dram_tensor("Wv", [DM, DM], F32, kind="ExternalInput").ap()
    Wo = nc.dram_tensor("Wo", [DM, DM], F32, kind="ExternalInput").ap()
    vecs = nc.dram_tensor("vecs", [13, DM], F32, kind="ExternalInput").ap()
    eye = nc.dram_tensor("eye", [128, 128], F32, kind="ExternalInput").ap()
    y = nc.dram_tensor("y", [DM, OWN], F32, kind="ExternalOutput").ap()
    stack = ExitStack()
    with stack:
        k = Sched(nc, stack)
        with ExitStack() as ps:
            emit_l1attn(nc, k, ps, xo, xr, Wq, Wk, Wv, Wo, vecs, eye, y)
        k.raw_wait("sync", "st", k.cnt["st"])
        k.finalize()
    return nc


def run_l1attn_launch(nc_a, x, W):
    shards = _shards_fm(x)
    vecs = np.stack([
        W["L1_attn_gq"], W["L1_attn_bq"], W["L1_attn_mq"], W["L1_attn_vq"],
        W["L1_attn_gk"], W["L1_attn_bk"], W["L1_attn_mk"], W["L1_attn_vk"],
        W["L1_attn_gv"], W["L1_attn_bv"], W["L1_attn_mv"], W["L1_attn_vv"],
        W["L1_attn_bo"]]).astype(np.float32)
    base = {
        "Wq": np.asarray(W["L1_attn_Wq"]), "Wk": np.asarray(W["L1_attn_Wk"]),
        "Wv": np.asarray(W["L1_attn_Wv"]), "Wo": np.asarray(W["L1_attn_Wo"]),
        "vecs": vecs, "eye": _EYE,
    }
    in_maps = []
    for c in range(NCORES):
        r = c // 2
        xrow = np.ascontiguousarray(x[r].T)
        in_maps.append(dict(base, xo=shards[c], xr=xrow))
    res = run_bass_kernel_spmd(nc_a, in_maps, core_ids=list(range(NCORES)))
    return _assemble_fm([np.asarray(res.results[c]["y"]) for c in range(NCORES)])


def emit_l0attn(nc, k, stack, xo, xr, Wq, Wk, Wv, Wo, brows, tcol, eyeb, y,
                row_read=None):
    if True:
        sfx = _uniq()
        sb = lambda n, sh, dt: stack.enter_context(nc.sbuf_tensor(n + sfx, sh, dt))
        qbf = sb("qbf", [128, DC, OWN], BF16)
        osb = sb("osb", [128, DC, OWN], BF16)
        wbuf = sb("wbuf", [128, DC, DM], BF16)
        wst = sb("wst", [128, DC, 256], F32)
        hkb = sb("hkb", [128, DC, 256], BF16)
        hf = sb("hf", [128, DC, 256], F32)
        xt = sb("xt", [128, DC, 256], F32)
        summ = sb("summ", [128, DC, 64], F32)
        summ_b = sb("summ_b", [128, DC, 64], BF16)
        kbf = sb("kbf", [128, DC, 64], BF16)
        vtm = sb("vtm", [64, DM], BF16)
        sq = [sb("sq0", [128, 256], F32), sb("sq1", [128, 256], F32)]
        rows = sb("rows", [1, 6 * 256], F32)
        browb = sb("browb", [1, 4 * DM], BF16)
        brf = sb("brf", [1, 4 * DM], F32)
        i64b = sb("i64b", [128, 64], F32)
        i64r = sb("i64r", [1, 64], F32)
        tcs = sb("tcs", [128, 16], F32)
        mask01 = sb("mask01", [128, 64], F32)
        ms_s = sb("ms_s", [128, 64], F32)
        att_f = sb("att_f", [128, 64], F32)
        att_b = sb("att_b", [128, 64], BF16)
        attT = sb("attT", [64, 128], BF16)
        mcol = sb("mcol", [128, 4], F32)
        eye_b = sb("eye_b", [128, 128], BF16)
        eye_f = sb("eye_f", [128, 128], F32)
        ones_col = sb("ones_col", [128, 1], F32)
        ones_row1 = sb("ones_row1", [1, 128], F32)
        ones_rbf = sb("ones_rbf", [1, 512], BF16)
        epscol = sb("epscol", [1, 1], F32)

        P = [stack.enter_context(nc.psum_tensor(f"P{i}{sfx}", [128, 512], F32))
             for i in range(6)]
        PTb = stack.enter_context(nc.psum_tensor("PTb" + sfx, [64, 512], BF16))
        PO = stack.enter_context(nc.psum_tensor("PO" + sfx, [64, 512], F32))

        for s in ["lnt", "mm"]:
            k.sem(s)
        for s in ["ld", "wl0", "wl1", "st"]:
            k.sem(s, dma=True)
        S, V, T, G, SY = "scalar", "vector", "tensor", "gpsimd", "sync"
        if row_read is None:
            def row_read(dst, t0, n, wait, slow=False):
                return k.dma(SY, dst, fm_view(xr)[:, :, t0:t0 + n],
                             wait=wait, inc="ld", slow=slow)

        k.op(G, lambda e: e.memset(ones_col[:], 1.0))
        k.op(G, lambda e: e.memset(ones_row1[:], 1.0))
        k.op(G, lambda e: e.memset(ones_rbf[:], 1.0))
        t_const = k.op(G, lambda e: e.memset(epscol[:], 1e-5))
        t_eye = k.dma(SY, eye_f[:], eyeb, inc="ld")
        t_eyeb = k.op(S, lambda e: e.copy(out=eye_b[:], in_=eye_f[:]), wait=[t_eye])
        t_tc = k.dma(SY, tcs[:], tcol, inc="ld")
        t_br = k.dma(SY, brf[:], brows.rearrange("(o a) d -> o (a d)", o=1), inc="ld")
        t_brb = k.op(S, lambda e: e.copy(out=browb[:], in_=brf[:]), wait=[t_br])
        t_io = k.op(G, lambda e: e.iota(i64r[:], pattern=[[64, 64]], base=0,
                                        channel_multiplier=0, allow_small_or_imprecise_dtypes=True))
        t_iob = k.op(T, lambda e: e.matmul(P[4][:, 0:64], ones_row1[:], i64r[:],
                                           start=True, stop=True),
                     wait=[t_io], inc="mm")
        t_iobs = k.op(S, lambda e: e.copy(out=i64b[:], in_=P[4][:, 0:64]),
                      wait=[t_iob])

        def cast_w(wsrc, waitfor):
            tf = None
            for qtr in range(4):
                cols = slice(qtr * 256, qtr * 256 + 256)
                t_l = k.dma(SY, wst[:],
                            wsrc.rearrange("(c p) d -> p c d", p=128)[:, :, cols],
                            wait=[tf] + waitfor, inc="wl%d" % (qtr % 2))
                for c in range(DC):
                    tf = k.op(V, lambda e, c=c, cols=cols: e.tensor_copy(
                        wbuf[:, c, cols], wst[:, c, :]), wait=[t_l])
            return tf

        t_sm0 = k.op(G, lambda e: e.memset(summ[:], 0.0))
        prev_ln = None
        sum_toks = []
        t_x0 = row_read(xt[:, :, 0:1], 0, 1, [], slow=True)
        toks0 = emit_ln_fm(
            k, lambda c: xt[:, c, 0:1], 1,
            lambda c: hkb[:, c, 0:1],
            (P[0], P[1]), (P[2], P[3]), sq, rows, ones_col[:], ones_row1[:],
            [t_x0, t_const, t_sm0], tag="ln", eps_col=epscol[0:1, 0:1],
            extra_dst=lambda c: summ[:, c, 0:1])
        prev_ln = toks0[-1]
        sum_toks.append(prev_ln)
        for ti in range(16):
            t0 = 1 + ti * 256
            ntk = 256 if ti < 15 else 192
            ng = ntk // 64
            t_x = row_read(xt[:, :, 0:ntk], t0, ntk, [prev_ln])
            toks = emit_ln_fm(
                k, lambda c, ntk=ntk: xt[:, c, 0:ntk], ntk,
                lambda c, ntk=ntk: hkb[:, c, 0:ntk],
                (P[0], P[1]), (P[2], P[3]), sq, rows, ones_col[:], ones_row1[:],
                [t_x, t_const], tag="ln", eps_col=epscol[0:1, 0:1],
                extra_dst=lambda c, ntk=ntk: hf[:, c, 0:ntk])
            g0 = 1 + ti * 4
            for c in range(DC):
                tr = k.op(V, lambda e, c=c, g0=g0, ng=ng, ntk=ntk: e.reduce_sum(
                    summ[:, c, g0:g0 + ng],
                    hf[:, c, 0:ntk].rearrange("p (g j) -> p g j", j=64),
                    axis=AX.X), wait=[toks[c], t_sm0])
                sum_toks.append(tr)
            prev_ln = sum_toks[-1]
        t_means = k.op(V, lambda e: e.tensor_scalar_mul(
            summ[:, :, 1:64], summ[:, :, 1:64], 1.0 / 64.0), wait=[prev_ln])
        t_sumb = k.op(S, lambda e: e.copy(out=summ_b[:], in_=summ[:]),
                      wait=[t_means])

        t_wk = cast_w(Wk, [])
        prev_sig = None
        for dc in range(DC):
            ph = P[4] if dc % 2 == 0 else P[5]
            for c in range(DC):
                k.op(T, lambda e, c=c, dc=dc, ph=ph: e.matmul(
                    ph[:, 0:64], wbuf[:, c, dc * 128:dc * 128 + 128],
                    summ_b[:, c, :], start=(c == 0), stop=False),
                    wait=[t_sumb, t_wk, prev_sig, t_iobs], inc=None)
            tm = k.op(T, lambda e, dc=dc, ph=ph: e.matmul(
                ph[:, 0:64], browb[0:1, DM + dc * 128:DM + dc * 128 + 128],
                ones_rbf[0:1, 0:64], start=False, stop=True),
                wait=[t_brb], inc="mm")
            prev_sig = k.op(S, lambda e, dc=dc, ph=ph: e.copy(
                out=kbf[:, dc, :], in_=ph[:, 0:64]), wait=[tm])
        t_wv = cast_w(Wv, [prev_sig])
        vt_toks = []
        for half in range(2):
            ph = P[4] if half == 0 else P[5]
            dcols = slice(half * 512, half * 512 + 512)
            for c in range(DC):
                k.op(T, lambda e, c=c, dcols=dcols, ph=ph: e.matmul(
                    ph[0:64, :], summ_b[:, c, :], wbuf[:, c, dcols],
                    start=(c == 0), stop=False),
                    wait=[t_wv, prev_sig], inc=None)
            tm = k.op(T, lambda e, half=half, ph=ph: e.matmul(
                ph[0:64, :], ones_rbf[0:1, 0:64],
                browb[0:1, 2 * DM + half * 512:2 * DM + half * 512 + 512],
                start=False, stop=True), wait=[t_brb], inc="mm")
            prev_sig = k.op(S, lambda e, dcols=dcols, ph=ph: e.copy(
                out=vtm[:, dcols], in_=ph[0:64, :]), wait=[tm])
            vt_toks.append(prev_sig)
        t_vtm = vt_toks[-1]

        t_wq = cast_w(Wq, [t_vtm])
        prev_q = {0: None, 1: None}
        q_toks = []
        for ti in range(8):
            t0 = ti * 256
            t_x = k.dma(SY, xt[:, :, 0:256], fm_view(xo)[:, :, t0:t0 + 256],
                        wait=[prev_ln, prev_q[0], prev_q[1]], inc="ld")
            toks = emit_ln_fm(
                k, lambda c: xt[:, c, 0:256], 256,
                lambda c: hkb[:, c, 0:256],
                (P[0], P[1]), (P[2], P[3]), sq, rows, ones_col[:], ones_row1[:],
                [t_x, t_const], tag="ln", eps_col=epscol[0:1, 0:1])
            t_ln = toks[-1]
            prev_ln = t_ln
            for dc in range(DC):
                bi = dc % 2
                ph = P[4] if bi == 0 else P[5]
                for c in range(DC):
                    k.op(T, lambda e, c=c, dc=dc, ph=ph: e.matmul(
                        ph[:, 0:256], wbuf[:, c, dc * 128:dc * 128 + 128],
                        hkb[:, c, 0:256], start=(c == 0), stop=False),
                        wait=[t_ln, t_wq, prev_q[bi]], inc=None)
                tm = k.op(T, lambda e, dc=dc, ph=ph: e.matmul(
                    ph[:, 0:256], browb[0:1, dc * 128:dc * 128 + 128],
                    ones_rbf[0:1, 0:256], start=False, stop=True),
                    wait=[t_brb], inc="mm")
                prev_q[bi] = k.op(S, lambda e, dc=dc, t0=t0, ph=ph: e.copy(
                    out=qbf[:, dc, t0:t0 + 256], in_=ph[:, 0:256]), wait=[tm])
            q_toks.append(prev_q[0])
            q_toks.append(prev_q[1])
        t_qall = q_toks[-1]

        prev_sc = {0: None, 1: None}
        prev_tp = None
        prev_av = None
        osb_toks = []
        for qc in range(16):
            qsl = slice(qc * 128, qc * 128 + 128)
            t_mk = k.op(V, lambda e, qc=qc: e.tensor_scalar(
                out=mask01[:], in0=i64b[:], scalar1=tcs[:, qc:qc + 1],
                scalar2=None, op0=ALU.is_le), wait=[t_iobs, t_tc, prev_av])
            for h in range(16):
                bi = h % 2
                ph = P[4] if bi == 0 else P[5]
                hc, hp = h // 2, 64 * (h % 2)
                tsc = k.op(T, lambda e, ph=ph, hc=hc, hp=hp, qsl=qsl: e.matmul(
                    ph[:, 0:64], qbf[hp:hp + 64, hc, qsl], kbf[hp:hp + 64, hc, :],
                    start=True, stop=True),
                    wait=[t_qall, prev_sig, prev_sc[bi]], inc="mm")
                tv1 = k.op(V, lambda e, ph=ph: e.scalar_tensor_tensor(
                    out=ms_s[:], in0=mask01[:], scalar=30000.0, in1=ph[:, 0:64],
                    op0=ALU.mult, op1=ALU.add), wait=[tsc, t_mk])
                prev_sc[bi] = tv1
                tv2 = k.op(V, lambda e: e.reduce_max(
                    mcol[:, 0:1], ms_s[:], axis=AX.X, negate=True), wait=[])
                tv3 = k.op(V, lambda e: e.tensor_scalar_mul(
                    mcol[:, 1:2], mcol[:, 0:1], 0.125), wait=[])
                ts1 = k.op(S, lambda e: e.activation(
                    out=att_f[:], in_=ms_s[:], func=AF.Exp, scale=0.125,
                    bias=mcol[:, 1:2]), wait=[tv3])
                tv4 = k.op(V, lambda e: e.reduce_sum(
                    mcol[:, 2:3], att_f[:], axis=AX.X), wait=[ts1])
                tv5 = k.op(V, lambda e: e.reciprocal(
                    mcol[:, 3:4], mcol[:, 2:3]), wait=[])
                tv6 = k.op(V, lambda e: e.tensor_scalar(
                    out=att_b[:], in0=att_f[:], scalar1=mcol[:, 3:4],
                    scalar2=None, op0=ALU.mult), wait=[tv5])
                ttp = k.op(T, lambda e: e.transpose(
                    PTb[0:64, 0:128], att_b[:], eye_b[:]),
                    wait=[tv6, t_eyeb, prev_tp], inc="mm")
                tcp = k.op(S, lambda e: e.copy(out=attT[:], in_=PTb[0:64, 0:128]),
                           wait=[ttp])
                prev_tp = tcp
                tav = k.op(T, lambda e, h=h: e.matmul(
                    PO[0:64, 0:128], vtm[:, h * 64:h * 64 + 64].rearrange(
                        "c d -> c d") if False else vtm[0:64, h * 64:h * 64 + 64],
                    attT[:], start=True, stop=True),
                    wait=[tcp, t_vtm, prev_av], inc="mm")
                prev_av = k.op(S, lambda e, hc=hc, hp=hp, qsl=qsl: e.copy(
                    out=osb[hp:hp + 64, hc, qsl], in_=PO[0:64, 0:128]),
                    wait=[tav])
                osb_toks.append(prev_av)
        t_oall = osb_toks[-1]

        t_wo = cast_w(Wo, [t_oall])
        prev_st = None
        prev_add = None
        for ti in range(8):
            t0 = ti * 256
            t_x = k.dma(SY, xt[:, :, 0:256], fm_view(xo)[:, :, t0:t0 + 256],
                        wait=[prev_st, prev_add, prev_ln], inc="ld")
            for dc in range(DC):
                bi = dc % 2
                ph = P[4] if bi == 0 else P[5]
                for c in range(DC):
                    k.op(T, lambda e, c=c, dc=dc, ph=ph, t0=t0: e.matmul(
                        ph[:, 0:256], wbuf[:, c, dc * 128:dc * 128 + 128],
                        osb[:, c, t0:t0 + 256], start=(c == 0), stop=False),
                        wait=[t_oall, t_wo] + ([prev_add] if c == 0 else []),
                        inc=None)
                tm = k.op(T, lambda e, dc=dc, ph=ph: e.matmul(
                    ph[:, 0:256], browb[0:1, 3 * DM + dc * 128:3 * DM + dc * 128 + 128],
                    ones_rbf[0:1, 0:256], start=False, stop=True),
                    wait=[t_brb], inc="mm")
                prev_add = k.op(V, lambda e, dc=dc, ph=ph: e.tensor_add(
                    xt[:, dc, :], xt[:, dc, :], ph[:, 0:256]), wait=[tm, t_x])
            prev_st = k.dma(SY, fm_view(y)[:, :, t0:t0 + 256], xt[:, :, 0:256],
                            wait=[prev_add], inc="st")


def build_l0attn():
    nc = bass.Bass("TRN2", target_bir_lowering=False, debug=False)
    xo = nc.dram_tensor("xo", [DM, OWN], F32, kind="ExternalInput").ap()
    xr = nc.dram_tensor("xr", [DM, ROWT], F32, kind="ExternalInput").ap()
    Wq = nc.dram_tensor("Wq", [DM, DM], F32, kind="ExternalInput").ap()
    Wk = nc.dram_tensor("Wk", [DM, DM], F32, kind="ExternalInput").ap()
    Wv = nc.dram_tensor("Wv", [DM, DM], F32, kind="ExternalInput").ap()
    Wo = nc.dram_tensor("Wo", [DM, DM], F32, kind="ExternalInput").ap()
    brows = nc.dram_tensor("brows", [4, DM], F32, kind="ExternalInput").ap()
    tcol = nc.dram_tensor("tcol", [128, 16], F32, kind="ExternalInput").ap()
    eyeb = nc.dram_tensor("eyeb", [128, 128], F32, kind="ExternalInput").ap()
    y = nc.dram_tensor("y", [DM, OWN], F32, kind="ExternalOutput").ap()
    stack = ExitStack()
    with stack:
        k = Sched(nc, stack)
        with ExitStack() as ps:
            emit_l0attn(nc, k, ps, xo, xr, Wq, Wk, Wv, Wo, brows, tcol, eyeb, y)
        k.raw_wait("sync", "st", k.cnt["st"])
        k.finalize()
    return nc


def run_l0attn_launch(nc_a, x, W):
    shards = _shards_fm(x)
    brows = np.stack([
        np.asarray(W["L0_attn_bq"]), np.asarray(W["L0_attn_bk"]),
        np.asarray(W["L0_attn_bv"]), np.asarray(W["L0_attn_bo"])]).astype(np.float32)
    base = {
        "Wq": np.asarray(W["L0_attn_Wq"]), "Wk": np.asarray(W["L0_attn_Wk"]),
        "Wv": np.asarray(W["L0_attn_Wv"]), "Wo": np.asarray(W["L0_attn_Wo"]),
        "brows": brows, "eyeb": _EYE,
    }
    in_maps = []
    for c in range(NCORES):
        r, h = c // 2, c % 2
        s = h * OWN
        tcolv = (s + np.arange(OWN).reshape(16, 128).T).astype(np.float32)
        in_maps.append(dict(base, xo=shards[c],
                            xr=np.ascontiguousarray(x[r].T),
                            tcol=np.ascontiguousarray(tcolv)))
    res = run_bass_kernel_spmd(nc_a, in_maps, core_ids=list(range(NCORES)))
    return _assemble_fm([np.asarray(res.results[c]["y"]) for c in range(NCORES)])


def emit_l0ffn(nc, k, stack, xc, wqkv, wo, w1, w2, lrows, b1r, qkcol, lgbi,
               Cq, Sq, Ck, Sk, Psw, E8i, E8Ti, eyef, mbk, hscol, y,
               wqkv_b=None, wo_b=None, w1_b=None, w2_b=None, ctok=None):
    if True:
        sfx = _uniq()
        sb = lambda n, sh, dt: stack.enter_context(nc.sbuf_tensor(n + sfx, sh, dt))
        h1buf = sb("h1buf", [128, DC, 3073], BF16)
        h0r = sb("h0r", [128, DC, 513], BF16)
        xmid = sb("xmid", [128, DC, 513], F32)
        hln = sb("hln", [128, DC, 512], BF16)
        eo = sb("eo", [128, 22, 512], BF16)
        wqkvb = sb("wqkvb", [128, DC, 1536], BF16)
        wob = sb("wob", [128, 4, DM], BF16)
        w1as = sb("w1as", [128, DC, 128], F32)
        w1ab = [sb("w1ab0", [128, DC, 128], BF16), sb("w1ab1", [128, DC, 128], BF16)]
        w1gb = [sb("w1gb0", [128, DC, 128], BF16), sb("w1gb1", [128, DC, 128], BF16)]
        w2sb = [sb("w2sb0", [128, 512], BF16), sb("w2sb1", [128, 512], BF16)]
        qh = sb("qh", [128, 512], F32)
        t1s = sb("t1s", [128, 512], F32)
        qrot = sb("qrot", [128, 4, 512], BF16)
        krot = sb("krot", [128, 4, 1024], BF16)
        vtm = sb("vtm", [128, 8, 512], BF16)
        attf = sb("attf", [128, 1024], F32)
        attT = sb("attT", [128, 128], BF16)
        o_bf = sb("o_bf", [128, 4, 512], BF16)
        sq = [sb("sq0", [128, 256], F32), sb("sq1", [128, 256], F32)]
        gg = sb("gg", [128, 512], F32)
        rss = sb("rss", [8, 512], F32)
        rbs = sb("rbs", [128, 512], F32)
        mbs = sb("mbs", [128, 4, 1024], BF16)
        E8s = sb("E8s", [128, 4, 8], F32)
        E8Ts = sb("E8Ts", [8, 4, 128], F32)
        psws = sb("psws", [128, 128], F32)
        eyes = sb("eyes", [128, 128], F32)
        lgb = sb("lgb", [128, DC, 4], F32)
        b2b = sb("b2b", [1, DM], BF16)
        b1sb = sb("b1sb", [1, 256], BF16)
        qkc = sb("qkc", [128, 4], F32)
        hss = sb("hss", [128, 2], F32)
        mcol = sb("mcol", [128, 4], F32)
        rows = sb("rows", [1, 5 * 128 + 256], F32)
        ones_col = sb("ones_col", [128, 1], F32)
        ones_row1 = sb("ones_row1", [1, 128], F32)
        ones_rbf = sb("ones_rbf", [1, 512], BF16)
        epscol = sb("epscol", [1, 1], F32)

        P = [stack.enter_context(nc.psum_tensor(f"P{i}{sfx}", [128, 512], F32))
             for i in range(8)]

        for s in ["lnt", "mm"]:
            k.sem(s)
        for s in ["ld", "wl0", "wl1", "wg0", "wg1", "bl", "tb", "st"]:
            k.sem(s, dma=True)
        S, V, T, G, SY = "scalar", "vector", "tensor", "gpsimd", "sync"

        k.op(G, lambda e: e.memset(ones_col[:], 1.0))
        k.op(G, lambda e: e.memset(ones_row1[:], 1.0))
        k.op(G, lambda e: e.memset(ones_rbf[:], 1.0))
        t_const = k.op(G, lambda e: e.memset(epscol[:], 1e-5))
        t_cs = []
        for (dst, src_) in [(mbs, mbk), (E8s, E8i), (E8Ts, E8Ti),
                            (psws, Psw), (eyes, eyef), (hss, hscol)]:
            t_cs.append(k.dma(SY, dst[:], src_, inc="ld"))
        t_b2b = None
        t_lg = k.dma(SY, lgb[:], lgbi, inc="ld")
        t_lb = t_lg
        t_qk = k.dma(SY, qkc[:], qkcol.rearrange("l a p -> p (l a)"), inc="ld",
                     slow=True)
        t_q8a = k.op(S, lambda e: e.mul(out=qkc[:, 0:1], in_=qkc[:, 0:1], mul=8.0),
                     wait=[t_qk])
        t_q8b = k.op(S, lambda e: e.mul(out=qkc[:, 2:3], in_=qkc[:, 2:3], mul=8.0),
                     wait=[t_qk])
        t_con = k.op(V, lambda e: e.nop(),
                     wait=t_cs + [t_q8a, t_q8b, t_const, t_lg, t_lb])

        st = {"qkvmm": None, "attmm": None, "womm": None, "w2mm": None,
              "w2mmlast": None, "b2tok": None,
              "ssmm": None, "ffnmm": None, "ystore": None,
              "xmid_tok": None, "hln_tok": None, "h1_tok": None,
              "h0_tok": None, "qr": None, "kr": {}, "vt": {},
              "o_tok": None, "eo_toks": [None] * 22, "roll": None,
              "sig": None, "evac": [None] * 8, "attread": None,
              "tabq": None, "tabk": None, "wtok": None, "b1tok": None,
              "w2cast": None, "w1cast": None, "lnlast": None}

        def load_layer_weights(l, waits):
            k.sem("wfin")
            t1 = k.dma(SY, wqkvb[:],
                       wqkv_b[l].rearrange("(c p) d -> p c d", p=128),
                       wait=waits + [ctok], inc="wl0")
            t2 = k.dma(SY, wob[:],
                       wo_b[l].rearrange("(c p) d -> p c d", p=128),
                       wait=waits + [ctok], inc="wl1")
            return k.op(SY, lambda e: e.nop(), wait=[t1, t2], inc="wfin")

        def hget(l, c, w0, shifted):
            off = -1 if shifted else 0
            if l == 0:
                return h0r[:, c, 1 + off:513 + off]
            b = w0 - 1024 + off
            return h1buf[:, c, b:b + 512]

        def hget_tile(l, c, w0, shifted, t0, n):
            a = hget(l, c, w0, shifted)
            return a[:, t0:t0 + n]

        def emit_h0_from_x(w0):
            out = []
            prev = None
            for sub in range(4):
                o = sub * 128
                t_x = k.dma(SY, w1as[:], fm_view(xc)[:, :, w0 + o:w0 + o + 128],
                            wait=[prev, st["roll"], st["lnlast"],
                                  st["w1cast"]], inc="ld")
                toks = emit_ln_fm(
                    k, lambda c: w1as[:, c, :], 128,
                    lambda c, o=o: h0r[:, c, 1 + o:1 + o + 128],
                    (P[0], P[1]), (P[2], P[3]), sq, rows,
                    ones_col[:], ones_row1[:],
                    [t_x, t_con, st["qkvmm"]], tag="ln",
                    eps_col=epscol[0:1, 0:1])
                out.extend(toks)
                prev = toks[-1]
            st["lnlast"] = prev
            return out

        def emit_qkv(l, w, w0):
            wt = st["wtok"]
            for which in range(2):
                tabsrc = (Cq, Sq) if which == 0 else (Ck, Sk)
                t_t1 = k.dma(SY, attf[:, 0:512], tabsrc[0][:, w0:w0 + 512],
                             wait=[st["tabq"], st["attread"]], inc="tb")
                t_t2 = k.dma(SY, attf[:, 512:1024], tabsrc[1][:, w0:w0 + 512],
                             wait=[], inc="tb")
                qp_toks = []
                sq_toks = []
                for oc in range(4):
                    col0 = which * 512 + oc * 128
                    tm = None
                    for c in range(DC):
                        tm = k.op(T, lambda e, c=c, oc=oc, col0=col0, l=l, w0=w0:
                                  e.matmul(P[4 + oc][:, :],
                                           wqkvb[:, c, col0:col0 + 128],
                                           hget(l, c, w0, c >= 4),
                                           start=(c == 0), stop=(c == DC - 1)),
                                  wait=[st["h0_tok"], wt, st["sig"],
                                        st["attread"], st["evac"][7],
                                        st.get("geread"), st["h1_tok"],
                                        st.get("yread"), st["xmid_tok"],
                                        st["hln_tok"]] if c == 0 else [],
                                  inc="mm" if c == DC - 1 else None)
                    qp_toks.append(tm)
                    t_s = k.op(S, lambda e, oc=oc: e.square(out=t1s[:],
                                                            in_=P[4 + oc][:, :]),
                               wait=[tm], inc=None)
                    t_ss = k.op(T, lambda e, oc=oc: e.matmul(
                        P[0][0:8, :], E8s[:, oc, :], t1s[:],
                        start=(oc == 0), stop=(oc == 3)),
                        wait=[t_s, t_con, st["lnlast"]],
                        inc="mm" if oc == 3 else None)
                    sq_toks.append(t_s)
                t_ssl = t_ss
                t_r1 = k.op(S, lambda e: e.sqrt(out=rss[:], in_=P[0][0:8, :]),
                            wait=[t_ssl])
                t_r2 = k.op(V, lambda e: e.tensor_scalar_max(rss[:], rss[:], 1e-12),
                            wait=[t_r1])
                t_r3 = k.op(V, lambda e: e.reciprocal(rss[:], rss[:]), wait=[t_r2])
                qcolix = which if l == 0 else 2 + which
                prev_m2 = None
                for oc in range(4):
                    t_rb = k.op(T, lambda e, oc=oc: e.matmul(
                        P[1][:, :], E8Ts[0:8, oc, :], rss[:],
                        start=True, stop=True),
                        wait=[t_r3, st["lnlast"], prev_m2], inc="mm")
                    t_rc = k.op(S, lambda e: e.copy(out=rbs[:], in_=P[1][:, :]),
                                wait=[t_rb])
                    t_qh = k.op(V, lambda e, oc=oc, qcolix=qcolix:
                                e.scalar_tensor_tensor(
                                    out=qh[:], in0=P[4 + oc][:, :],
                                    scalar=qkc[:, qcolix:qcolix + 1], in1=rbs[:],
                                    op0=ALU.mult, op1=ALU.mult),
                                wait=[t_rc, st["ystore"]], inc=None)
                    t_pw = k.op(T, lambda e: e.matmul(
                        P[1][:, :], psws[:], qh[:], start=True, stop=True),
                        wait=[t_qh, t_rc], inc="mm")
                    t_m1 = k.op(V, lambda e: e.tensor_mul(t1s[:], qh[:],
                                                          attf[:, 0:512]),
                                wait=[t_t1, t_t2, t_pw], inc=None)
                    t_m2 = k.op(V, lambda e: e.tensor_mul(gg[:], P[1][:, :],
                                                          attf[:, 512:1024]),
                                wait=[t_pw, st["b2tok"]], inc=None)
                    prev_m2 = t_m2
                    if which == 0:
                        t_o = k.op(V, lambda e, oc=oc: e.tensor_add(
                            qrot[:, oc, :], t1s[:], gg[:]), wait=[], inc=None)
                        st["qr"] = t_o
                    else:
                        half = slice((w % 2) * 512, (w % 2) * 512 + 512)
                        t_o = k.op(V, lambda e, oc=oc, half=half: e.tensor_add(
                            krot[:, oc, half], t1s[:], gg[:]), wait=[], inc=None)
                        st["kr"][(w, oc)] = t_o
                st["tabq"] = t_o
                st["sig"] = t_o
            for j in range(4):
                slot = (w % 2) * 4 + j
                tsl = slice(j * 128, j * 128 + 128)
                ph = P[4 + j]
                tm = None
                for c in range(DC):
                    tm = k.op(T, lambda e, c=c, tsl=tsl, l=l, w0=w0, ph=ph:
                              e.matmul(ph[:, :],
                                       hget(l, c, w0, c >= 4)[:, tsl],
                                       wqkvb[:, c, 1024:1536],
                                       start=(c == 0), stop=(c == DC - 1)),
                              wait=[st["sig"], st["h0_tok"]] if c == 0 else [],
                              inc="mm" if c == DC - 1 else None)
                t_v = k.op(S, lambda e, slot=slot, ph=ph: e.copy(
                    out=vtm[:, slot, :], in_=ph[:, :]), wait=[tm])
                st["vt"][(w, j)] = t_v
                st["sig"] = t_v
            st["qkvmm"] = tm

        def emit_attn(l, w, w0, first_scored):
            use4 = (w == 4)
            for h in range(8):
                hc, hp = h // 2, 64 * (h % 2)
                for qc in range(4):
                    qsl = slice(qc * 128, qc * 128 + 128)
                    prevhalf = slice(((w - 1) % 2) * 512, ((w - 1) % 2) * 512 + 512)
                    curhalf = slice((w % 2) * 512, (w % 2) * 512 + 512)
                    t_s1 = k.op(T, lambda e, hp=hp, hc=hc, qsl=qsl, ph=prevhalf:
                                e.matmul(P[6][:, :], qrot[hp:hp + 64, hc, qsl],
                                         krot[hp:hp + 64, hc, ph],
                                         start=True, stop=True),
                                wait=[st["qr"], st["kr"].get((w - 1, hc), None),
                                      st["kr"].get((w, hc), None),
                                      st["attread"], st["sig"]], inc="mm")
                    t_s2 = k.op(T, lambda e, hp=hp, hc=hc, qsl=qsl, ch=curhalf:
                                e.matmul(P[7][:, :], qrot[hp:hp + 64, hc, qsl],
                                         krot[hp:hp + 64, hc, ch],
                                         start=True, stop=True),
                                wait=[], inc="mm")
                    tv1 = k.op(V, lambda e, qc=qc: e.scalar_tensor_tensor(
                        out=attf[:, 0:512], in0=mbs[:, qc, 0:512], scalar=1.0,
                        in1=P[6][:, :], op0=ALU.mult, op1=ALU.add),
                        wait=[t_s1, t_con])
                    if use4:
                        tv1 = k.op(V, lambda e: e.tensor_scalar(
                            out=attf[:, 0:512], in0=attf[:, 0:512],
                            scalar1=hss[:, 1:2], scalar2=None, op0=ALU.add),
                            wait=[])
                    tv2 = k.op(V, lambda e, qc=qc: e.scalar_tensor_tensor(
                        out=attf[:, 512:1024], in0=mbs[:, qc, 512:1024],
                        scalar=1.0, in1=P[7][:, :],
                        op0=ALU.mult, op1=ALU.add), wait=[t_s2])
                    st["attread"] = tv2
                    tv3 = k.op(V, lambda e: e.reduce_max(
                        mcol[:, 0:1], attf[:], axis=AX.X, negate=True), wait=[])
                    ts1 = k.op(S, lambda e: e.activation(
                        out=attf[:], in_=attf[:], func=AF.Exp,
                        bias=mcol[:, 0:1]), wait=[tv3])
                    tv4 = k.op(V, lambda e: e.reduce_sum(
                        mcol[:, 1:2], attf[:], axis=AX.X), wait=[ts1])
                    tv5 = k.op(V, lambda e: e.reciprocal(
                        mcol[:, 2:3], mcol[:, 1:2]), wait=[])
                    tv6 = k.op(V, lambda e: e.tensor_scalar(
                        out=attf[:], in0=attf[:], scalar1=mcol[:, 2:3],
                        scalar2=None, op0=ALU.mult), wait=[tv5])
                    tav = None
                    prev_cp = None
                    t_tp = None
                    for ktc in range(8):
                        slot = (((w - 1) % 2) * 4 + ktc) if ktc < 4 else \
                               ((w % 2) * 4 + (ktc - 4))
                        t_tp = k.op(T, lambda e, ktc=ktc: e.transpose(
                            P[5][:, 0:128], attf[:, ktc * 128:ktc * 128 + 128],
                            eyes[:]), wait=[tv6, t_con, prev_cp], inc="mm")
                        t_cp = k.op(S, lambda e: e.copy(out=attT[:],
                                                        in_=P[5][:, 0:128]),
                                    wait=[t_tp, tav])
                        tav = k.op(T, lambda e, ktc=ktc, slot=slot, h=h:
                                   e.matmul(P[4][0:64, 0:128],
                                            vtm[:, slot, h * 64:h * 64 + 64],
                                            attT[:],
                                            start=(ktc == 0), stop=(ktc == 7)),
                                   wait=[t_cp, st["vt"].get((w - 1, slot % 4), None)
                                         if ktc < 4 else
                                         st["vt"].get((w, slot % 4), None)],
                                   inc="mm")
                        prev_cp = t_cp
                    t_ob = k.op(S, lambda e, hp=hp, hc=hc, qsl=qsl: e.copy(
                        out=o_bf[hp:hp + 64, hc, qsl], in_=P[4][0:64, 0:128]),
                        wait=[tav])
                    st["o_tok"] = t_ob
                    st["sig"] = t_ob
                    st["attread"] = t_tp

        def emit_wo(l, w, w0):
            prev_bank = {0: None, 1: None}
            for dc in range(DC):
                ph = P[6] if dc % 2 == 0 else P[7]
                tm = None
                for c in range(4):
                    tm = k.op(T, lambda e, c=c, dc=dc, ph=ph: e.matmul(
                        ph[:, :], wob[:, c, dc * 128:dc * 128 + 128],
                        o_bf[:, c, :], start=(c == 0), stop=(c == 3)),
                        wait=[st["o_tok"], st["wtok"], st["evac"][dc],
                              prev_bank[dc % 2]] if c == 0 else [],
                        inc="mm" if c == 3 else None)
                t_e = k.op(V, lambda e, dc=dc, ph=ph, l=l, w0=w0: e.tensor_add(
                    xmid[:, dc, 1:513], hget(l, dc, w0, False), ph[:, :]),
                    wait=[tm, st["xmid_tok"], st["hln_tok"]])
                st["evac"][dc] = t_e
                prev_bank[dc % 2] = t_e
            st["womm"] = tm
            st["xmid_w"] = st["evac"][:]

        def emit_ffn(l, w, w0, sink):
            lntoks = []
            prevt = None
            for sub in range(4):
                o = sub * 128
                toks = emit_ln_fm(
                    k, lambda c, o=o: xmid[:, c, (1 + o if c < 4 else o):
                                           (1 + o if c < 4 else o) + 128], 128,
                    lambda c, o=o: hln[:, c, o:o + 128],
                    (P[0], P[1]), (P[2], P[3]), sq, rows,
                    ones_col[:], ones_row1[:],
                    [st["evac"][7], t_con, st["ffnmm"], prevt, st["lnlast"]],
                    tag="ln", eps_col=epscol[0:1, 0:1],
                    affine=lambda c: (lgb[:, c, l:l + 1], lgb[:, c, 2 + l:3 + l]))
                lntoks.extend(toks)
                prevt = toks[-1]
            st["hln_tok"] = lntoks[-1]
            st["lnlast"] = lntoks[-1]
            prev_b1 = st["b1tok"]
            mm_par = {0: None, 1: None}
            for hc in range(22):
                sz = 128 if hc < 21 else 42
                bi = hc % 2
                t_ca = k.dma(SY, w1ab[bi][:, :, 0:sz],
                             w1_b[l].rearrange("(c p) d -> p c d", p=128)
                             [:, :, hc * 128:hc * 128 + sz],
                             wait=[mm_par[bi], ctok], inc="wl%d" % bi)
                t_cg = k.dma(SY, w1gb[bi][:, :, 0:sz],
                             w1_b[l].rearrange("(c p) d -> p c d", p=128)
                             [:, :, 2730 + hc * 128:2730 + hc * 128 + sz],
                             wait=[mm_par[bi], ctok], inc="wg%d" % bi)
                t_b = k.dma(SY, rows[0:1, 0:sz],
                            b1r[l:l + 1, hc * 128:hc * 128 + sz],
                            wait=[prev_b1, st["lnlast"], st["b2tok"]], inc="bl")
                t_b2 = k.dma(SY, rows[0:1, 128:128 + sz],
                             b1r[l:l + 1, 2730 + hc * 128:2730 + hc * 128 + sz],
                             wait=[], inc="bl")
                t_bc = k.op(S, lambda e: e.copy(out=b1sb[:], in_=rows[0:1, 0:256]),
                            wait=[t_b, t_b2])
                prev_b1 = t_bc
                st["b1tok"] = t_bc
                tma = tmg = None
                for c in range(DC):
                    tma = k.op(T, lambda e, c=c, sz=sz, bi=bi: e.matmul(
                        P[4][0:sz, :], w1ab[bi][:, c, 0:sz], hln[:, c, :],
                        start=(c == 0), stop=False),
                        wait=[t_ca, st["hln_tok"], st["geread"]
                              if "geread" in st else None], inc=None)
                tma = k.op(T, lambda e, sz=sz: e.matmul(
                    P[4][0:sz, :], b1sb[0:1, 0:sz], ones_rbf[:],
                    start=False, stop=True), wait=[t_bc], inc="mm")
                for c in range(DC):
                    tmg = k.op(T, lambda e, c=c, sz=sz, bi=bi: e.matmul(
                        P[6][0:sz, :], w1gb[bi][:, c, 0:sz], hln[:, c, :],
                        start=(c == 0), stop=False),
                        wait=[t_cg] if c == 0 else [], inc=None)
                tmg = k.op(T, lambda e, sz=sz: e.matmul(
                    P[6][0:sz, :], b1sb[0:1, 128:128 + sz], ones_rbf[:],
                    start=False, stop=True), wait=[t_bc], inc="mm")
                mm_par[bi] = tmg
                t_gl = k.op(S, lambda e, sz=sz: e.activation(
                    out=gg[0:sz, :], in_=P[6][0:sz, :], func=AF.Gelu),
                    wait=[tmg])
                t_eo = k.op(V, lambda e, hc=hc, sz=sz: e.tensor_mul(
                    eo[0:sz, hc, :], P[4][0:sz, :], gg[0:sz, :]),
                    wait=[t_gl, tma])
                st["eo_toks"][hc] = t_eo
                st["geread"] = t_eo
            w2mms = [None] * 8
            w2_par = {0: None, 1: None}
            for hc in range(22):
                sz = 128 if hc < 21 else 42
                for half in range(2):
                    dcl = slice(half * 512, half * 512 + 512)
                    bi = (hc * 2 + half) % 2
                    t_c2 = k.dma(SY, w2sb[bi][0:sz, :],
                                 w2_b[l][hc * 128:hc * 128 + sz, dcl],
                                 wait=[w2_par[bi], ctok],
                                 inc="wl0" if bi == 0 else "wl1")
                    for dc4 in range(4):
                        dc = half * 4 + dc4
                        w2mms[dc] = k.op(T, lambda e, hc=hc, dc=dc, dc4=dc4,
                                         sz=sz, bi=bi: e.matmul(
                            P[dc][:, :], w2sb[bi][0:sz, dc4 * 128:dc4 * 128 + 128],
                            eo[0:sz, hc, :], start=(hc == 0), stop=False),
                            wait=[t_c2, st["eo_toks"][hc]] +
                                 ([st["yread"] if "yread" in st else None,
                                   st["h1_tok"], st["geread"]]
                                  if hc == 0 else []),
                            inc="mm" if dc4 == 3 else None)
                        if dc4 == 3:
                            w2_par[bi] = w2mms[dc]
            for dc in range(DC):
                w2mms[dc] = k.op(T, lambda e, dc=dc: e.matmul(
                    P[dc][:, :], b2b[0:1, dc * 128:dc * 128 + 128],
                    ones_rbf[:], start=False, stop=True),
                    wait=[st["b2tok"]], inc="mm")
            st["ffnmm"] = w2mms[7]
            if sink == "h1":
                for dc in range(DC):
                    if w == 1:
                        t_h = k.op(V, lambda e, dc=dc: e.tensor_add(
                            h1buf[:, dc, 0:1], xmid[:, dc, 512:513],
                            P[dc][:, 511:512]), wait=[w2mms[dc]])
                    else:
                        b = 512 * (w - 2) + 1
                        t_h = k.op(V, lambda e, dc=dc, b=b: e.tensor_add(
                            h1buf[:, dc, b:b + 512], xmid[:, dc, 1:513],
                            P[dc][:, :]), wait=[w2mms[dc]])
                    st["h1_tok"] = t_h
                st["w2mm"] = w2mms[7]
            else:
                for dc in range(DC):
                    t_t = k.op(V, lambda e, dc=dc: e.tensor_add(
                        t1s[:], P[dc][:, :], xmid[:, dc, 1:513]),
                        wait=[w2mms[dc], st["ystore"]])
                    t_x = k.dma(SY, qh[:], fm_view(xc)[:, dc, w0:w0 + 512],
                                wait=[st.get("yread"), st["ystore"]],
                                inc="ld")
                    t_y = k.op(V, lambda e: e.scalar_tensor_tensor(
                        out=qh[:], in0=t1s[:], scalar=0.5, in1=qh[:],
                        op0=ALU.mult, op1=ALU.add), wait=[t_x, t_t])
                    st["yread"] = t_y
                    st["ystore"] = k.dma(
                        SY, fm_view(y)[:, dc, w0 - 2049:w0 - 2049 + 512], qh[:],
                        wait=[t_y], inc="st")
                st["w2mm"] = w2mms[7]

        def emit_layer(l, h_src, sink):
            wfirst = 0 if l == 0 else 2
            sfirst = wfirst + 1
            ffirst = sfirst if sink == "h1" else sfirst + 1
            st["wtok"] = load_layer_weights(l, [st["qkvmm"], st["womm"]])
            tb2 = None
            for hf in range(2):
                t_l2r = k.dma(SY, gg[0:1, 0:512],
                              lrows[l, 2:3].rearrange("(o a) d -> o (a d)", o=1)
                              [0:1, hf * 512:hf * 512 + 512],
                              wait=[st["ffnmm"], st["b1tok"], st["lnlast"],
                                    st["qkvmm"], st["sig"], tb2], inc="bl")
                tb2 = k.op(S, lambda e, hf=hf: e.copy(
                    out=b2b[0:1, hf * 512:hf * 512 + 512], in_=gg[0:1, 0:512]),
                    wait=[t_l2r])
            st["b2tok"] = tb2
            t_z = k.op(G, lambda e: e.memset(xmid[:, :, 0:1], 0.0),
                       wait=[st["ffnmm"], st["hln_tok"], t_con,
                             st["xmid_tok"], st["evac"][7]])
            st["xmid_tok"] = t_z
            if l == 0:
                t_z0 = k.op(G, lambda e: e.memset(h0r[:, :, 0:1], 0.0),
                            wait=[st["qkvmm"], t_con])
                st["roll"] = t_z0
            for w in range(wfirst, 8):
                w0 = 1 + 512 * w
                if h_src == "x":
                    toks = emit_h0_from_x(w0)
                    st["h0_tok"] = toks[-1]
                else:
                    st["h0_tok"] = st["h1_tok"]
                emit_qkv(l, w, w0)
                if w >= sfirst:
                    emit_attn(l, w, w0, sfirst)
                    emit_wo(l, w, w0)
                    if w >= ffirst:
                        emit_ffn(l, w, w0, sink)
                    t_r = k.op(V, lambda e: e.tensor_copy(
                        xmid[:, :, 0:1], xmid[:, :, 512:513]),
                        wait=[st["evac"][7], st["hln_tok"], st["ffnmm"]])
                    if w + 1 == 4:
                        t_r = k.op(V, lambda e: e.tensor_scalar(
                            out=xmid[:, :, 0:1], in0=xmid[:, :, 0:1],
                            scalar1=hss[:, 0:1], scalar2=None, op0=ALU.mult),
                            wait=[t_r])
                    st["xmid_tok"] = t_r
                if h_src == "x":
                    st["roll"] = k.op(V, lambda e: e.tensor_copy(
                        h0r[:, :, 0:1], h0r[:, :, 512:513]),
                        wait=[st["qkvmm"], st["h0_tok"]])

        emit_layer(0, "x", "h1")
        emit_layer(1, "h1", "y")


def build_l0ffn():
    nc = bass.Bass("TRN2", target_bir_lowering=False, debug=False)
    xc = nc.dram_tensor("xc", [DM, 4097], F32, kind="ExternalInput").ap()
    wqkv = nc.dram_tensor("wqkv", [2, DM, 1536], F32, kind="ExternalInput").ap()
    wo = nc.dram_tensor("wo", [2, 512, DM], F32, kind="ExternalInput").ap()
    w1 = nc.dram_tensor("w1", [2, DM, 5460], F32, kind="ExternalInput").ap()
    w2 = nc.dram_tensor("w2", [2, 2730, DM], F32, kind="ExternalInput").ap()
    lrows = nc.dram_tensor("lrows", [2, 3, DM], F32, kind="ExternalInput").ap()
    b1r = nc.dram_tensor("b1r", [2, 5460], F32, kind="ExternalInput").ap()
    qkcol = nc.dram_tensor("qkcol", [2, 2, 128], F32, kind="ExternalInput").ap()
    lgbi = nc.dram_tensor("lgbi", [128, DC, 4], F32, kind="ExternalInput").ap()
    Cq = nc.dram_tensor("Cq", [128, 4097], F32, kind="ExternalInput").ap()
    Sq = nc.dram_tensor("Sq", [128, 4097], F32, kind="ExternalInput").ap()
    Ck = nc.dram_tensor("Ck", [128, 4097], F32, kind="ExternalInput").ap()
    Sk = nc.dram_tensor("Sk", [128, 4097], F32, kind="ExternalInput").ap()
    Psw = nc.dram_tensor("Psw", [128, 128], F32, kind="ExternalInput").ap()
    E8i = nc.dram_tensor("E8i", [128, 4, 8], F32, kind="ExternalInput").ap()
    E8Ti = nc.dram_tensor("E8Ti", [8, 4, 128], F32, kind="ExternalInput").ap()
    eyef = nc.dram_tensor("eyef", [128, 128], F32, kind="ExternalInput").ap()
    mbk = nc.dram_tensor("mbk", [128, 4, 1024], BF16, kind="ExternalInput").ap()
    hscol = nc.dram_tensor("hscol", [128, 2], F32, kind="ExternalInput").ap()
    y = nc.dram_tensor("y", [DM, OWN], F32, kind="ExternalOutput").ap()
    wqkv_b = nc.dram_tensor("wqkv_b", [2, DM, 1536], BF16).ap()
    wo_b = nc.dram_tensor("wo_b", [2, 512, DM], BF16).ap()
    w1_b = nc.dram_tensor("w1_b", [2, DM, 5460], BF16).ap()
    w2_b = nc.dram_tensor("w2_b", [2, 2730, DM], BF16).ap()
    stack = ExitStack()
    with stack:
        k = Sched(nc, stack)
        with ExitStack() as cs:
            ctok = emit_bf16_cache(nc, k, cs, [
                (wqkv, wqkv_b, 2 * DM * 1536), (wo, wo_b, 2 * 512 * DM),
                (w1, w1_b, 2 * DM * 5460), (w2, w2_b, 2 * 2730 * DM)], tag="c")
        with ExitStack() as ps:
            emit_l0ffn(nc, k, ps, xc, wqkv, wo, w1, w2, lrows, b1r, qkcol, lgbi,
                       Cq, Sq, Ck, Sk, Psw, E8i, E8Ti, eyef, mbk, hscol, y,
                       wqkv_b=wqkv_b, wo_b=wo_b, w1_b=w1_b, w2_b=w2_b,
                       ctok=ctok)
        k.raw_wait("sync", "st", k.cnt["st"])
        k.finalize()
    return nc


def _l0ffn_host_consts(core):
    r, h = core // 2, core % 2
    s = h * OWN
    g = np.arange(-2049, 2048, dtype=np.float64) + s
    gc = np.clip(g, 0, None)
    half = np.arange(0, 64, 2, dtype=np.float64)
    inv_freq = 1.0 / 10000.0 ** (half / 64.0)
    freqs = gc[:, None] * inv_freq[None, :]
    base = (half + 0.4 * 64.0) / (1.4 * 64.0)
    power = (gc - 4096 // 2) / float(512 // 2)
    scale = base[None, :] ** power[:, None]
    cos = np.cos(freqs)
    sin = np.sin(freqs)
    cq = (cos * scale).astype(np.float32)
    sq_ = (sin * scale).astype(np.float32)
    ck = (cos / scale).astype(np.float32)
    sk = (sin / scale).astype(np.float32)

    def tab(a):
        full = np.concatenate([a, a], axis=1)
        return np.ascontiguousarray(np.tile(full.T, (2, 1)).astype(np.float32))

    Psw = np.zeros((128, 128), np.float32)
    for m in range(128):
        if m % 64 < 32:
            Psw[m + 32, m] = -1.0
        else:
            Psw[m - 32, m] = 1.0
    E8 = np.zeros((128, 4, 8), np.float32)
    E8T = np.zeros((8, 4, 128), np.float32)
    for oc in range(4):
        E8[0:64, oc, 2 * oc] = 1.0
        E8[64:128, oc, 2 * oc + 1] = 1.0
        E8T[2 * oc, oc, 0:64] = 1.0
        E8T[2 * oc + 1, oc, 64:128] = 1.0
    mb = np.full((128, 4, 1024), -30000.0, np.float32)
    for j in range(4):
        for p in range(128):
            a = 128 * j + p
            mb[p, j, a:a + 513] = 0.0
    hs = np.zeros((128, 2), np.float32)
    hs[:, 0] = 1.0 if h == 1 else 0.0
    hs[:, 1] = 0.0 if h == 1 else -30000.0
    return {
        "Cq": tab(cq), "Sq": tab(sq_), "Ck": tab(ck), "Sk": tab(sk),
        "Psw": Psw, "E8i": E8, "E8Ti": E8T, "eyef": _EYE,
        "mbk": mb.astype(ml_dtypes.bfloat16),
        "hscol": hs,
    }


def run_l0ffn_launch(nc_f, x, W):
    wqkv = np.asarray(W["L0_ffn_wqkv"])
    wov = np.asarray(W["L0_ffn_wo"])
    w1v = np.asarray(W["L0_ffn_w1"])
    w2v = np.asarray(W["L0_ffn_w2"])
    lrows = np.stack([np.asarray(W["L0_ffn_ln_g"]), np.asarray(W["L0_ffn_ln_b"]),
                      np.asarray(W["L0_ffn_b2"])], axis=1).astype(np.float32)
    lgbi = np.stack([np.asarray(W["L0_ffn_ln_g"])[0].reshape(DC, 128).T,
                     np.asarray(W["L0_ffn_ln_g"])[1].reshape(DC, 128).T,
                     np.asarray(W["L0_ffn_ln_b"])[0].reshape(DC, 128).T,
                     np.asarray(W["L0_ffn_ln_b"])[1].reshape(DC, 128).T],
                    axis=2).astype(np.float32)
    b1v = np.asarray(W["L0_ffn_b1"]).astype(np.float32)
    qs = np.asarray(W["L0_ffn_q_scale"])
    ks = np.asarray(W["L0_ffn_k_scale"])
    qkcol = np.stack([np.stack([np.tile(qs[l], 2), np.tile(ks[l], 2)])
                      for l in range(2)]).astype(np.float32)
    base = {"wqkv": wqkv, "wo": wov, "w1": w1v, "w2": w2v,
            "lrows": lrows, "b1r": b1v, "qkcol": qkcol,
            "lgbi": np.ascontiguousarray(lgbi)}
    in_maps = []
    for c in range(NCORES):
        r, h = c // 2, c % 2
        s = h * OWN
        xrT = x[r].T
        ctx = np.zeros((DM, 4097), np.float32)
        lo = s - 2049
        src_lo = max(lo, 0)
        ctx[:, src_lo - lo:] = xrT[:, src_lo:s + OWN]
        m = dict(base, xc=ctx)
        m.update(_l0ffn_host_consts(c))
        in_maps.append(m)
    res = run_bass_kernel_spmd(nc_f, in_maps, core_ids=list(range(NCORES)))
    return _assemble_fm([np.asarray(res.results[c]["y"]) for c in range(NCORES)])



def emit_bf16_cache(nc, k, stack, pairs, tag="c", waitfor=None):
    S, V, G, SY = "scalar", "vector", "gpsimd", "sync"
    sfx = _uniq()
    CW = 4096
    bf = [stack.enter_context(nc.sbuf_tensor(f"cbf{i}{sfx}", [128, CW], F32))
          for i in range(2)]
    bb = [stack.enter_context(nc.sbuf_tensor(f"cbb{i}{sfx}", [128, CW], BF16))
          for i in range(2)]
    for s in [tag + "l0", tag + "l1", tag + "s0", tag + "s1"]:
        k.sem(s, dma=True)
    k.sem(tag + "fin")
    last = {0: waitfor, 1: waitfor}
    tok = None
    i = 0
    for (srcap, dstap, ne) in pairs:
        nx = ne // 128
        for off in range(0, nx, CW):
            w = min(CW, nx - off)
            bi = i % 2
            t_l = k.dma(SY, bf[bi][:, 0:w],
                        _flat2d(srcap, ne)[:, off:off + w],
                        wait=[last[bi]], inc=tag + "l%d" % bi)
            eng = G if i % 2 == 0 else V
            t_c = k.op(eng, lambda e, bi=bi, w=w: e.tensor_copy(
                bb[bi][:, 0:w], bf[bi][:, 0:w]), wait=[t_l, waitfor])
            tok = k.dma(SY, _flat2d(dstap, ne)[:, off:off + w], bb[bi][:, 0:w],
                        wait=[t_c], inc=tag + "s%d" % bi)
            last[bi] = tok
            i += 1
    return k.op(SY, lambda e: e.nop(), wait=[last[0], last[1]],
                inc=tag + "fin")


def _flat2d(ap, ne):
    spec = " ".join("d%d" % i for i in range(len(ap.shape)))
    flat = ap.rearrange("%s -> (%s)" % (spec, spec))
    return flat.rearrange("(p x) -> p x", p=128)


RG_PAIRS = [[0, 1], [2, 3], [4, 5], [6, 7]]


def build_fused(rg=None):
    rg = rg or RG_PAIRS
    nc = bass.Bass("TRN2", target_bir_lowering=False, debug=False)
    dt = nc.dram_tensor
    xc0 = dt("xc0", [DM, 4097], F32, kind="ExternalInput").ap()
    f_wqkv = dt("f_wqkv", [2, DM, 1536], F32, kind="ExternalInput").ap()
    f_wo = dt("f_wo", [2, 512, DM], F32, kind="ExternalInput").ap()
    f_w1 = dt("f_w1", [2, DM, 5460], F32, kind="ExternalInput").ap()
    f_w2 = dt("f_w2", [2, 2730, DM], F32, kind="ExternalInput").ap()
    f_lrows = dt("f_lrows", [2, 3, DM], F32, kind="ExternalInput").ap()
    f_b1r = dt("f_b1r", [2, 5460], F32, kind="ExternalInput").ap()
    f_qkcol = dt("f_qkcol", [2, 2, 128], F32, kind="ExternalInput").ap()
    f_lgbi = dt("f_lgbi", [128, DC, 4], F32, kind="ExternalInput").ap()
    f_Cq = dt("f_Cq", [128, 4097], F32, kind="ExternalInput").ap()
    f_Sq = dt("f_Sq", [128, 4097], F32, kind="ExternalInput").ap()
    f_Ck = dt("f_Ck", [128, 4097], F32, kind="ExternalInput").ap()
    f_Sk = dt("f_Sk", [128, 4097], F32, kind="ExternalInput").ap()
    f_Psw = dt("f_Psw", [128, 128], F32, kind="ExternalInput").ap()
    f_E8i = dt("f_E8i", [128, 4, 8], F32, kind="ExternalInput").ap()
    f_E8Ti = dt("f_E8Ti", [8, 4, 128], F32, kind="ExternalInput").ap()
    f_eyef = dt("f_eyef", [128, 128], F32, kind="ExternalInput").ap()
    f_mbk = dt("f_mbk", [128, 4, 1024], BF16, kind="ExternalInput").ap()
    f_hscol = dt("f_hscol", [128, 2], F32, kind="ExternalInput").ap()
    a_Wq = dt("a_Wq", [DM, DM], F32, kind="ExternalInput").ap()
    a_Wk = dt("a_Wk", [DM, DM], F32, kind="ExternalInput").ap()
    a_Wv = dt("a_Wv", [DM, DM], F32, kind="ExternalInput").ap()
    a_Wo = dt("a_Wo", [DM, DM], F32, kind="ExternalInput").ap()
    a_brows = dt("a_brows", [4, DM], F32, kind="ExternalInput").ap()
    a_tcol = dt("a_tcol", [128, 16], F32, kind="ExternalInput").ap()
    m_rwT = dt("m_rwT", [DM, 8], F32, kind="ExternalInput").ap()
    m_eb = dt("m_eb", [8], F32, kind="ExternalInput").ap()
    m_w1 = dt("m_w1", [8, DM, 2048], F32, kind="ExternalInput").ap()
    m_w3 = dt("m_w3", [8, DM, 2048], F32, kind="ExternalInput").ap()
    m_w2 = dt("m_w2", [8, 2048, DM], F32, kind="ExternalInput").ap()
    m_sw1 = dt("m_sw1", [DM, 2048], F32, kind="ExternalInput").ap()
    m_sw3 = dt("m_sw3", [DM, 2048], F32, kind="ExternalInput").ap()
    m_sw2 = dt("m_sw2", [2048, DM], F32, kind="ExternalInput").ap()
    t_Wq = dt("t_Wq", [DM, DM], F32, kind="ExternalInput").ap()
    t_Wk = dt("t_Wk", [DM, DM], F32, kind="ExternalInput").ap()
    t_Wv = dt("t_Wv", [DM, DM], F32, kind="ExternalInput").ap()
    t_Wo = dt("t_Wo", [DM, DM], F32, kind="ExternalInput").ap()
    t_vecs = dt("t_vecs", [13, DM], F32, kind="ExternalInput").ap()
    yout = dt("yout", [DM, OWN], F32, kind="ExternalOutput").ap()
    xb = [dt(f"xb{i}", [DM, OWN], F32).ap() for i in range(1, 6)]
    ag2 = dt("ag2", [2 * DM, OWN], F32).ap()
    ag4 = dt("ag4", [2 * DM, OWN], F32).ap()
    ctxb = dt("ctxb", [DM, 4097], F32).ap()
    cwscr = dt("cwscr", [8, OWN], F32).ap()

    fc_wqkv = dt("fc_wqkv", [2, DM, 1536], BF16).ap()
    fc_wo = dt("fc_wo", [2, 512, DM], BF16).ap()
    fc_w1 = dt("fc_w1", [2, DM, 5460], BF16).ap()
    fc_w2 = dt("fc_w2", [2, 2730, DM], BF16).ap()
    mc_w1 = dt("mc_w1", [8, DM, 2048], BF16).ap()
    mc_w3 = dt("mc_w3", [8, DM, 2048], BF16).ap()
    mc_w2 = dt("mc_w2", [8, 2048, DM], BF16).ap()
    mc_sw1 = dt("mc_sw1", [DM, 2048], BF16).ap()
    mc_sw3 = dt("mc_sw3", [DM, 2048], BF16).ap()
    mc_sw2 = dt("mc_sw2", [2048, DM], BF16).ap()
    stack = ExitStack()
    with stack:
        k = Sched(nc, stack)
        SY, V, G = "sync", "vector", "gpsimd"
        k.sem("ccA")
        k.sem("ccB")
        with ExitStack() as cs:
            ctok = emit_bf16_cache(nc, k, cs, [
                (f_wqkv, fc_wqkv, 2 * DM * 1536), (f_wo, fc_wo, 2 * 512 * DM),
                (f_w1, fc_w1, 2 * DM * 5460), (f_w2, fc_w2, 2 * 2730 * DM)],
                tag="c")
        with ExitStack() as cs:
            mtok = emit_bf16_cache(nc, k, cs, [
                (m_w1, mc_w1, 8 * DM * 2048), (m_w3, mc_w3, 8 * DM * 2048),
                (m_w2, mc_w2, 8 * 2048 * DM), (m_sw1, mc_sw1, DM * 2048),
                (m_sw3, mc_sw3, DM * 2048), (m_sw2, mc_sw2, 2048 * DM)],
                tag="m", waitfor=ctok)

        def mk_rowread(ag_ap, cctok):
            lo = ag_ap[0:DM, :].rearrange("(c p) t -> p c t", p=128)
            hi = ag_ap[DM:2 * DM, :].rearrange("(c p) t -> p c t", p=128)

            def rr(dst, t0, n, wait, slow=False):
                t1 = min(n, max(0, OWN - t0))
                tok = None
                if t1 > 0:
                    tok = k.dma(SY, dst[:, :, 0:t1], lo[:, :, t0:t0 + t1],
                                wait=list(wait) + [cctok], inc="ld",
                                slow=slow or t1 < 8)
                if t1 < n:
                    tok = k.dma(SY, dst[:, :, t1:n],
                                hi[:, :, t0 + t1 - OWN:t0 + n - OWN],
                                wait=list(wait) + [cctok], inc="ld",
                                slow=slow or (n - t1) < 8)
                return tok
            return rr

        with ExitStack() as ps:
            emit_l0ffn(nc, k, ps, xc0, f_wqkv, f_wo, f_w1, f_w2, f_lrows,
                       f_b1r, f_qkcol, f_lgbi, f_Cq, f_Sq, f_Ck, f_Sk, f_Psw,
                       f_E8i, f_E8Ti, f_eyef, f_mbk, f_hscol, xb[0],
                       wqkv_b=fc_wqkv, wo_b=fc_wo, w1_b=fc_w1, w2_b=fc_w2,
                       ctok=ctok)
        k.barrier()
        tcc = k.op(G, lambda e: e.collective_compute(
            "AllGather", ALU.bypass, replica_groups=rg,
            ins=[xb[0].opt()], outs=[ag2.opt()]), inc="ccA")
        with ExitStack() as ps:
            emit_l0attn(nc, k, ps, xb[0], xb[0], a_Wq, a_Wk, a_Wv, a_Wo,
                        a_brows, a_tcol, f_eyef, xb[1],
                        row_read=mk_rowread(ag2, tcc))
        k.barrier()
        with ExitStack() as ps:
            cb = ps.enter_context(nc.sbuf_tensor("cb", [128, DC, 512], F32))
            hs2 = ps.enter_context(nc.sbuf_tensor("hs2", [128, 2], F32))
            t_h = k.dma(SY, hs2[:], f_hscol, inc="ld")
            t_z = k.op(G, lambda e: e.memset(cb[:, :, 0:1], 0.0))
            t_z2 = k.dma(SY, fm_view(ctxb)[:, :, 0:1], cb[:, :, 0:1],
                         wait=[t_z], inc="st", slow=True)
            prev = t_z2
            aglo = ag2[0:DM, :].rearrange("(c p) t -> p c t", p=128)
            for j in range(4):
                sl = slice(j * 512, j * 512 + 512)
                t_l = k.dma(SY, cb[:], aglo[:, :, sl], wait=[prev, tcc],
                            inc="ld")
                t_m = k.op(V, lambda e: e.tensor_scalar(
                    out=cb[:], in0=cb[:], scalar1=hs2[:, 0:1], scalar2=None,
                    op0=ALU.mult), wait=[t_l, t_h])
                prev = k.dma(SY, fm_view(ctxb)[:, :, 1 + j * 512:1 + j * 512 + 512],
                             cb[:], wait=[t_m], inc="st")
            for j in range(4):
                sl = slice(j * 512, j * 512 + 512)
                t_l = k.dma(SY, cb[:], fm_view(xb[1])[:, :, sl], wait=[prev],
                            inc="ld")
                prev = k.dma(SY, fm_view(ctxb)[:, :, 2049 + j * 512:
                                               2049 + j * 512 + 512],
                             cb[:], wait=[t_l], inc="st")
        k.barrier()
        with ExitStack() as ps:
            emit_l0ffn(nc, k, ps, ctxb, f_wqkv, f_wo, f_w1, f_w2, f_lrows,
                       f_b1r, f_qkcol, f_lgbi, f_Cq, f_Sq, f_Ck, f_Sk, f_Psw,
                       f_E8i, f_E8Ti, f_eyef, f_mbk, f_hscol, xb[2],
                       wqkv_b=fc_wqkv, wo_b=fc_wo, w1_b=fc_w1, w2_b=fc_w2,
                       ctok=ctok)
        k.barrier()
        with ExitStack() as ps:
            emit_moe(nc, k, ps, xb[2], m_rwT, m_eb, m_w1, m_w3, m_w2, m_sw1,
                     m_sw3, m_sw2, f_eyef, xb[3], cwscr, w1_b=mc_w1, w3_b=mc_w3,
                     w2_b=mc_w2, sw1_b=mc_sw1, sw3_b=mc_sw3, sw2_b=mc_sw2,
                     ctok=mtok)
        k.barrier()
        tcc2 = k.op(G, lambda e: e.collective_compute(
            "AllGather", ALU.bypass, replica_groups=rg,
            ins=[xb[3].opt()], outs=[ag4.opt()]), inc="ccB")
        with ExitStack() as ps:
            emit_l1attn(nc, k, ps, xb[3], xb[3], t_Wq, t_Wk, t_Wv, t_Wo,
                        t_vecs, f_eyef, xb[4], row_read=mk_rowread(ag4, tcc2))
        k.barrier()
        with ExitStack() as ps:
            emit_moe(nc, k, ps, xb[4], m_rwT, m_eb, m_w1, m_w3, m_w2, m_sw1,
                     m_sw3, m_sw2, f_eyef, yout, cwscr, w1_b=mc_w1, w3_b=mc_w3,
                     w2_b=mc_w2, sw1_b=mc_sw1, sw3_b=mc_sw3, sw2_b=mc_sw2,
                     ctok=mtok)
        k.raw_wait("sync", "st", k.cnt["st"])
        k.finalize()
    return nc


def run_fused(nc_z, x, W):
    shards = _shards_fm(x)
    base = {
        "f_wqkv": np.asarray(W["L0_ffn_wqkv"]), "f_wo": np.asarray(W["L0_ffn_wo"]),
        "f_w1": np.asarray(W["L0_ffn_w1"]), "f_w2": np.asarray(W["L0_ffn_w2"]),
        "f_b1r": np.asarray(W["L0_ffn_b1"]).astype(np.float32),
        "a_Wq": np.asarray(W["L0_attn_Wq"]), "a_Wk": np.asarray(W["L0_attn_Wk"]),
        "a_Wv": np.asarray(W["L0_attn_Wv"]), "a_Wo": np.asarray(W["L0_attn_Wo"]),
        "m_rwT": np.ascontiguousarray(np.asarray(W["L1_ffn_router_w"]).T),
        "m_eb": np.asarray(W["L1_ffn_experts_bias"]),
        "m_w1": np.asarray(W["L1_ffn_w1"]), "m_w3": np.asarray(W["L1_ffn_w3"]),
        "m_w2": np.asarray(W["L1_ffn_w2"]),
        "m_sw1": np.asarray(W["L1_ffn_sw1"])[0],
        "m_sw3": np.asarray(W["L1_ffn_sw3"])[0],
        "m_sw2": np.asarray(W["L1_ffn_sw2"])[0],
        "t_Wq": np.asarray(W["L1_attn_Wq"]), "t_Wk": np.asarray(W["L1_attn_Wk"]),
        "t_Wv": np.asarray(W["L1_attn_Wv"]), "t_Wo": np.asarray(W["L1_attn_Wo"]),
    }
    base["f_lrows"] = np.stack([np.asarray(W["L0_ffn_ln_g"]),
                                np.asarray(W["L0_ffn_ln_b"]),
                                np.asarray(W["L0_ffn_b2"])], axis=1).astype(np.float32)
    qs, ks = np.asarray(W["L0_ffn_q_scale"]), np.asarray(W["L0_ffn_k_scale"])
    base["f_qkcol"] = np.stack([np.stack([np.tile(qs[l], 2), np.tile(ks[l], 2)])
                                for l in range(2)]).astype(np.float32)
    base["f_lgbi"] = np.ascontiguousarray(np.stack(
        [np.asarray(W["L0_ffn_ln_g"])[0].reshape(DC, 128).T,
         np.asarray(W["L0_ffn_ln_g"])[1].reshape(DC, 128).T,
         np.asarray(W["L0_ffn_ln_b"])[0].reshape(DC, 128).T,
         np.asarray(W["L0_ffn_ln_b"])[1].reshape(DC, 128).T], axis=2)).astype(np.float32)
    base["a_brows"] = np.stack([np.asarray(W["L0_attn_bq"]),
                                np.asarray(W["L0_attn_bk"]),
                                np.asarray(W["L0_attn_bv"]),
                                np.asarray(W["L0_attn_bo"])]).astype(np.float32)
    base["t_vecs"] = np.stack([
        W["L1_attn_gq"], W["L1_attn_bq"], W["L1_attn_mq"], W["L1_attn_vq"],
        W["L1_attn_gk"], W["L1_attn_bk"], W["L1_attn_mk"], W["L1_attn_vk"],
        W["L1_attn_gv"], W["L1_attn_bv"], W["L1_attn_mv"], W["L1_attn_vv"],
        W["L1_attn_bo"]]).astype(np.float32)
    in_maps = []
    for c in range(NCORES):
        r, h = c // 2, c % 2
        s = h * OWN
        xrT = x[r].T
        ctx = np.zeros((DM, 4097), np.float32)
        lo = s - 2049
        src_lo = max(lo, 0)
        ctx[:, src_lo - lo:] = xrT[:, src_lo:s + OWN]
        m = dict(base, xc0=ctx)
        cst = _l0ffn_host_consts(c)
        m.update({"f_" + kk: vv for kk, vv in cst.items()})
        tcolv = (s + np.arange(OWN).reshape(16, 128).T).astype(np.float32)
        m["a_tcol"] = np.ascontiguousarray(tcolv)
        in_maps.append(m)
    res = run_bass_kernel_spmd(nc_z, in_maps, core_ids=list(range(NCORES)))
    return _assemble_fm([np.asarray(res.results[c]["yout"]) for c in range(NCORES)])


_NC_CACHE = {}


def _get_nc(name, builder):
    if name not in _NC_CACHE:
        _NC_CACHE[name] = builder()
    return _NC_CACHE[name]


def kernel(**inputs):
    x = np.asarray(inputs["L0_attn_x"], np.float32)
    nc_z = _get_nc("fused", build_fused)
    return run_fused(nc_z, x, inputs)


# version 06 (final)
# speedup vs baseline: 1.3497x; 1.1047x over previous
import numpy as np
import ml_dtypes
from contextlib import ExitStack

import concourse.bass as bass
import concourse.mybir as mybir
from concourse.bass_utils import run_bass_kernel_spmd

F32 = mybir.dt.float32
BF16 = mybir.dt.bfloat16
AF = mybir.ActivationFunctionType
ALU = mybir.AluOpType
AX = mybir.AxisListType

NCORES = 8
ROWT = 4096
OWN = 2048
DM = 1024
DC = 8


class Sched:
    ENGINES = ("sync", "tensor", "vector", "scalar", "gpsimd")

    def __init__(self, nc, stack):
        self.nc = nc
        self.stack = stack
        self._log = {e: [] for e in self.ENGINES}
        self.dma_sems = set()
        self.dma_wmax = {}
        self.dma_barrier = {}
        self.ops = {e: [] for e in self.ENGINES}
        self.sems = {}
        self.cnt = {}
        self.wdone = {e: {} for e in self.ENGINES}

    def sem(self, name, dma=False):
        if name in self.sems:
            return name
        s = self.stack.enter_context(self.nc.semaphore(name))
        self.sems[name] = s
        self.cnt[name] = 0
        if dma:
            self.dma_sems.add(name)
        return name

    def barrier(self):
        marks = []
        if "mm" in self.sems:
            marks.append(self.op("tensor", lambda e: e.drain(), inc="mm"))
        for aname, eng in [("@vector", "vector"), ("@scalar", "scalar"),
                           ("@gpsimd", "gpsimd")]:
            if aname in self.sems and self.cnt[aname] > 0:
                marks.append((aname, self.cnt[aname], eng))
        for s in sorted(self.dma_sems):
            if self.cnt[s] > 0:
                marks.append((s, self.cnt[s], None))
        toks = []
        for e in self.ENGINES:
            toks.append(self.op(e, lambda en: en.nop(), wait=marks, inc=None))
        return toks

    AUTO = ("vector", "scalar", "gpsimd")

    def op(self, eng, build, wait=(), inc=None, inc_by=1):
        if eng in self.AUTO:
            aname = "@" + eng
            if aname not in self.sems:
                self.sem(aname)
            prev = self.cnt[aname]
            wait = list(wait) + ([(aname, prev, None)] if prev > 0 else [])
            inc, inc_by = aname, 1
        waits = []
        logwaits = []
        for tok in wait:
            if tok is None:
                continue
            sname, v, peng = tok
            if sname in self.dma_sems:
                v = self.cnt[sname]
                self.dma_wmax[sname] = max(self.dma_wmax.get(sname, 0), v)
            elif peng == eng:
                continue
            if v <= 0 or self.wdone[eng].get(sname, -1) >= v:
                continue
            self.wdone[eng][sname] = v
            waits.append((self.sems[sname], v))
            logwaits.append((sname, v))
        tok = None
        semh = None
        if inc is not None:
            self.cnt[inc] += inc_by
            tok = (inc, self.cnt[inc], eng)
            semh = self.sems[inc]

        def f(engine, waits=waits, build=build, semh=semh, inc_by=inc_by):
            for (s, v) in waits:
                engine.wait_ge(s, v)
            ins = build(engine)
            if semh is not None:
                ins.then_inc(semh, inc_by)

        self.ops[eng].append(f)
        self._log[eng].append((logwaits, None if inc is None else (inc, inc_by)))
        return tok

    def dma(self, eng, out, in_, wait=(), inc=None, slow=False):
        wait = list(wait)
        if inc in self.dma_sems:
            w = self.dma_wmax.get(inc, 0)
            if w > self.dma_barrier.get(inc, 0):
                self.dma_barrier[inc] = w
                wait.append((inc, w, None))
        if slow:
            return self.op(eng, lambda e, o=out, i=in_: e.dma_start(
                out=o, in_=i, allow_slow_non_contiguous=True),
                wait=wait, inc=inc, inc_by=16)
        return self.op(eng, lambda e, o=out, i=in_: e.dma_start(out=o, in_=i),
                       wait=wait, inc=inc, inc_by=16)

    def raw_wait(self, eng, sname, v):
        def f(engine, s=self.sems[sname], v=v):
            engine.wait_ge(s, v)
            engine.nop()
        self.ops[eng].append(f)
        self._log[eng].append(([(sname, v)], None))

    def check(self):
        import collections
        meta = {e: [] for e in self.ENGINES}
        qs = {e: collections.deque(self._log[e]) for e in self.ENGINES}
        val = collections.defaultdict(int)
        progress = True
        while progress:
            progress = False
            for e in self.ENGINES:
                while qs[e]:
                    waits, inc = qs[e][0]
                    if any(val[s] < v for (s, v) in waits):
                        break
                    if inc is not None:
                        val[inc[0]] += inc[1]
                    qs[e].popleft()
                    progress = True
        stuck = {e: len(qs[e]) for e in self.ENGINES if qs[e]}
        if stuck:
            for e in self.ENGINES:
                if qs[e]:
                    waits, inc = qs[e][0]
                    bad = [(s, v, val[s]) for (s, v) in waits if val[s] < v]
                    print(f"DEADLOCK {e}: waiting {bad}")
            raise RuntimeError(f"deadlock, remaining ops {stuck}")

    def finalize(self):
        self.check()
        blk = self.stack.enter_context(self.nc.Block())
        for e in self.ENGINES:
            lst = self.ops[e]
            if not lst:
                continue
            def runner(engine, lst=lst):
                for f in lst:
                    f(engine)
            getattr(blk, e)(runner)


_NAME_N = [0]


def _uniq():
    _NAME_N[0] += 1
    return "_u%d" % _NAME_N[0]


def fm_view(dram_ap):
    return dram_ap.rearrange("(c p) t -> p c t", p=128)


def emit_ln_fm(k, src_getter, n_tok, dst_put, ps_stat, ps_b, sq, rows,
               ones_col, ones_row1, ready_toks, tag, affine=None, eps_col=None,
               extra_dst=None):
    S, V, T, G = "scalar", "vector", "tensor", "gpsimd"
    nt = n_tok
    pm = ps_stat[0][0:1, 0:nt]
    pq = ps_stat[1][0:1, 0:nt]
    sq_toks = [None, None]
    mm_last = None
    for c in range(DC):
        xc = src_getter(c)
        t_sq = k.op(S, lambda e, s=sq[c % 2], x=xc, n=nt:
                    e.square(out=s[:, 0:n], in_=x),
                    wait=list(ready_toks) + [sq_toks[c % 2]], inc=tag + "s")
        k.op(T, lambda e, x=xc, st=(c == 0), sp=(c == DC - 1):
             e.matmul(pm, ones_col, x, start=st, stop=sp),
             wait=list(ready_toks), inc=None)
        mm_last = k.op(T, lambda e, s=sq[c % 2], st=(c == 0), sp=(c == DC - 1), n=nt:
                       e.matmul(pq, ones_col, s[:, 0:n], start=st, stop=sp),
                       wait=[t_sq], inc=tag + "t")
        sq_toks[c % 2] = mm_last
    r_mu = rows[0:1, 0:nt]
    r_rstd = rows[0:1, nt:2 * nt]
    r_tmp = rows[0:1, 2 * nt:3 * nt]
    r_mrs = rows[0:1, 3 * nt:4 * nt]
    r_mu2 = rows[0:1, 4 * nt:5 * nt]
    tv1 = k.op(V, lambda e: e.tensor_scalar_mul(r_mu, pm, 1.0 / DM),
               wait=[mm_last], inc=tag + "v")
    k.op(V, lambda e: e.tensor_scalar_mul(r_tmp, pq, 1.0 / DM), wait=[], inc=None)
    ts1 = k.op(S, lambda e: e.square(out=r_mu2, in_=r_mu), wait=[tv1], inc=tag + "s")
    tv2 = k.op(V, lambda e: e.tensor_sub(r_tmp, r_tmp, r_mu2), wait=[ts1], inc=tag + "v")
    ts2 = k.op(S, lambda e: e.activation(out=r_tmp, in_=r_tmp, func=AF.Ln,
                                         bias=eps_col), wait=[tv2], inc=tag + "s")
    ts3 = k.op(S, lambda e: e.activation(out=r_rstd, in_=r_tmp, func=AF.Exp, scale=-0.5),
               wait=[], inc=tag + "s")
    tv3 = k.op(V, lambda e: e.tensor_mul(r_mrs, r_mu, r_rstd), wait=[ts3], inc=tag + "v")
    rb = ps_b[0][:, 0:nt]
    mb = ps_b[1][:, 0:nt]
    tb1 = k.op(T, lambda e: e.matmul(rb, ones_row1, r_rstd, start=True, stop=True),
               wait=[ts3] + list(ready_toks), inc=tag + "t")
    tb2 = k.op(T, lambda e: e.matmul(mb, ones_row1, r_mrs, start=True, stop=True),
               wait=[tv3], inc=tag + "t")
    out_toks = []
    for c in range(DC):
        xc = src_getter(c)
        k.op(V, lambda e, x=xc, s=sq[c % 2], n=nt: e.tensor_mul(s[:, 0:n], x, rb),
             wait=[tb1], inc=None)
        if affine is None and extra_dst is None:
            to = k.op(V, lambda e, s=sq[c % 2], o=dst_put(c), n=nt:
                      e.tensor_sub(o, s[:, 0:n], mb), wait=[tb2], inc=tag + "v")
        elif affine is None:
            tf = k.op(V, lambda e, s=sq[c % 2], d=extra_dst(c), n=nt:
                      e.tensor_sub(d, s[:, 0:n], mb), wait=[tb2], inc=tag + "v")
            to = k.op(S, lambda e, d=extra_dst(c), o=dst_put(c):
                      e.copy(out=o, in_=d), wait=[tf], inc=tag + "s")
        else:
            lg_col, lb_col = affine(c)
            tf = k.op(V, lambda e, s=sq[c % 2], n=nt:
                      e.tensor_sub(s[:, 0:n], s[:, 0:n], mb), wait=[tb2], inc=tag + "v")
            to = k.op(V, lambda e, s=sq[c % 2], o=dst_put(c), lg=lg_col, lb=lb_col, n=nt:
                      e.tensor_scalar(out=o, in0=s[:, 0:n], scalar1=lg, scalar2=lb,
                                      op0=ALU.mult, op1=ALU.add),
                      wait=[tf], inc=tag + "v")
        out_toks.append(to)
    return out_toks


def emit_moe(nc, k, stack, xo, rwT, eb, w1, w3, w2, sw1, sw3, sw2, eye, y,
             cwscr, w1_b=None, w3_b=None, w2_b=None, sw1_b=None, sw3_b=None,
             sw2_b=None, ctok=None):
    if True:
        sfx = _uniq()
        sb = lambda n, sh, dt: stack.enter_context(nc.sbuf_tensor(n + sfx, sh, dt))
        psal = lambda n, sh: stack.enter_context(nc.psum_tensor(n + sfx, sh, F32))
        hbf = sb("hbf", [128, DC, OWN], BF16)
        eo = sb("eo", [128, 16, 1024], BF16)
        macc = sb("macc", [128, DC, 1024], F32)
        w2bf = sb("w2bf", [128, 16, DM], BF16)
        w1bf = [sb("w1bf0", [128, DC, 256], BF16), sb("w1bf1", [128, DC, 256], BF16)]
        w3bf = [sb("w3bf0", [128, DC, 256], BF16), sb("w3bf1", [128, DC, 256], BF16)]
        xt = sb("xt", [128, DC, 256], F32)
        sq = [sb("sq0", [128, 1024], F32), sb("sq1", [128, 1024], F32)]
        sg = sb("sg", [128, 1024], F32)
        tt_s = sb("tt_s", [128, 1024], F32)
        cwb_sb = sb("cwb_sb", [128, 1024], F32)
        cw_sb = sb("cw_sb", [128, 8], F32)
        m1c = sb("m1c", [128, 6], F32)
        eq1 = sb("eq1", [128, 8], F32)
        eq2 = sb("eq2", [128, 8], F32)
        cwrow = sb("cwrow", [1, 1024], F32)
        cwtmp = sb("cwtmp", [8, 128], F32)
        rows = sb("rows", [1, 6 * 256], F32)
        tksc = sb("tksc", [1, 512], F32)
        rwTs = sb("rwTs", [128, DC, 8], F32)
        rwTb = sb("rwTb", [128, DC, 8], BF16)
        ebs = sb("ebs", [1, 8], F32)
        ebb = sb("ebb", [1, 8], BF16)
        eye_sb = sb("eye_sb", [128, 128], F32)
        ones_col = sb("ones_col", [128, 1], F32)
        ones_row1 = sb("ones_row1", [1, 128], F32)
        ones_rbf = sb("ones_rbf", [1, 128], BF16)
        epscol = sb("epscol", [1, 1], F32)

        PGa = psal("PGa", [128, 512])
        PGb = psal("PGb", [128, 512])
        PUa = psal("PUa", [128, 512])
        PUb = psal("PUb", [128, 512])
        PCa = psal("PCa", [128, 512])
        PCb = psal("PCb", [128, 512])
        PWa = psal("PWa", [128, 512])
        PWb = psal("PWb", [128, 512])

        for s in ["lns", "lnt", "lnv", "lng",
                  "mm", "ev", "es", "eg", "cwv", "cws"]:
            k.sem(s)
        for s in ["ld", "wl0", "wl1", "w3l0", "w3l1", "st", "dtr"]:
            k.sem(s, dma=True)

        S, V, T, G, SY = "scalar", "vector", "tensor", "gpsimd", "sync"

        k.op(G, lambda e: e.memset(ones_col[:], 1.0), inc="eg")
        k.op(G, lambda e: e.memset(epscol[:], 1e-5), inc="eg")
        k.op(G, lambda e: e.memset(ones_row1[:], 1.0), inc=None)
        t_ones3 = k.op(G, lambda e: e.memset(ones_rbf[:], 1.0), inc="eg")
        t_ones = t_eps = t_ones3
        t_rw = k.dma(SY, rwTs[:], rwT.rearrange("(c p) e -> p c e", p=128), inc="ld")
        t_eb = k.dma(SY, ebs[:], eb.rearrange("(o e) -> o e", o=1), inc="ld")
        t_eye = k.dma(SY, eye_sb[:], eye, inc="ld")
        t_rwb = k.op(S, lambda e: e.copy(out=rwTb[:], in_=rwTs[:]), wait=[t_rw], inc="lns")
        t_ebb = k.op(S, lambda e: e.copy(out=ebb[:], in_=ebs[:]), wait=[t_eb], inc="lns")

        ln_done = []
        for sti in range(8):
            t0 = sti * 256
            t_x = k.dma(SY, xt[:], fm_view(xo)[:, :, t0:t0 + 256],
                        wait=[ln_done[-1] if ln_done else None], inc="ld")
            toks = emit_ln_fm(
                k, lambda c: xt[:, c, :], 256,
                lambda c, t0=t0: hbf[:, c, t0:t0 + 256],
                (PGa, PGb), (PUa, PUb), sq, rows, ones_col[:], ones_row1[:],
                [t_x, t_ones, t_eps], tag="ln", eps_col=epscol[0:1, 0:1])
            ln_done.extend(toks)
        ln_all = ln_done[-1]

        prev_pw_read = None
        dtr_last = None
        for q in range(16):
            for c in range(DC):
                k.op(T, lambda e, c=c, q=q: e.matmul(
                    PWa[:, 0:8], hbf[:, c, q * 128:q * 128 + 128], rwTb[:, c, :],
                    start=(c == 0), stop=False),
                    wait=[ln_all, t_rwb, prev_pw_read], inc=None)
            tl = k.op(T, lambda e: e.matmul(PWa[:, 0:8], ones_rbf[:], ebb[:],
                                            start=False, stop=True),
                      wait=[t_ebb, t_ones3], inc="mm")
            v1 = k.op(V, lambda e: e.reduce_max(m1c[:, 0:1], PWa[:, 0:8], axis=AX.X),
                      wait=[tl], inc="cwv")
            v2 = k.op(V, lambda e: e.tensor_scalar(out=eq1[:], in0=PWa[:, 0:8],
                                                   scalar1=m1c[:, 0:1], scalar2=None,
                                                   op0=ALU.is_equal), wait=[], inc="cwv")
            k.op(V, lambda e: e.scalar_tensor_tensor(
                out=sq[0][:, 0:8], in0=eq1[:], scalar=-1e30, in1=PWa[:, 0:8],
                op0=ALU.mult, op1=ALU.add), wait=[], inc=None)
            v4 = k.op(V, lambda e: e.reduce_max(m1c[:, 1:2], sq[0][:, 0:8], axis=AX.X),
                      wait=[], inc="cwv")
            v5 = k.op(V, lambda e: e.tensor_sub(m1c[:, 2:3], m1c[:, 1:2], m1c[:, 0:1]),
                      wait=[], inc=None)
            s1 = k.op(S, lambda e: e.activation(out=m1c[:, 4:5], in_=m1c[:, 2:3],
                                                func=AF.Sigmoid), wait=[v5], inc="cws")
            v6 = k.op(V, lambda e: e.tensor_scalar(out=m1c[:, 3:4], in0=m1c[:, 4:5],
                                                   scalar1=-1.0, scalar2=1.0,
                                                   op0=ALU.mult, op1=ALU.add),
                      wait=[s1], inc="cwv")
            v7 = k.op(V, lambda e: e.tensor_scalar(out=eq2[:], in0=PWa[:, 0:8],
                                                   scalar1=m1c[:, 1:2], scalar2=None,
                                                   op0=ALU.is_equal), wait=[], inc="cwv")
            prev_pw_read = v7
            k.op(V, lambda e: e.tensor_scalar(out=cw_sb[:], in0=eq1[:],
                                              scalar1=m1c[:, 3:4], scalar2=None,
                                              op0=ALU.mult), wait=[], inc=None)
            v9 = k.op(V, lambda e: e.scalar_tensor_tensor(
                out=cw_sb[:], in0=eq2[:], scalar=m1c[:, 4:5], in1=cw_sb[:],
                op0=ALU.mult, op1=ALU.add), wait=[], inc="cwv")
            tt1 = k.op(T, lambda e: e.transpose(PCa[0:8, 0:128], cw_sb[:], eye_sb[:]),
                       wait=[v9, t_eye, dtr_last], inc="mm")
            tt2 = k.op(S, lambda e: e.copy(out=cwtmp[:], in_=PCa[0:8, 0:128]),
                       wait=[tt1, dtr_last], inc="cws")
            dtr_last = k.dma(SY, cwscr[:, q * 128:q * 128 + 128], cwtmp[:],
                             wait=[tt2], inc="dtr")

        prev_ty = None
        last_w2mm = None
        prev_cwmm = None
        prev_st = None
        for tti in range(2):
            ts0 = tti * 1024
            t_ms = k.op(G, lambda e: e.memset(macc[:], 0.0),
                        wait=[prev_ty], inc="eg")
            for ei in range(9):
                shared = (ei == 8)
                w1_src = sw1 if shared else w1[ei]
                w3_src = sw3 if shared else w3[ei]
                w2_src = sw2 if shared else w2[ei]
                w_src_b = (sw1_b, sw3_b, sw2_b) if shared else \
                          (w1_b[ei], w3_b[ei], w2_b[ei])
                if not shared:
                    t_cw = k.dma(SY, cwrow[:], cwscr[ei:ei + 1, ts0:ts0 + 1024],
                                 wait=[dtr_last, prev_cwmm], inc="dtr")
                    tc1a = k.op(T, lambda e: e.matmul(
                        PCa[:, :], ones_row1[:], cwrow[0:1, 0:512],
                        start=True, stop=True), wait=[t_cw, last_w2mm], inc=None)
                    tc1b = k.op(T, lambda e: e.matmul(
                        PCb[:, :], ones_row1[:], cwrow[0:1, 512:1024],
                        start=True, stop=True), wait=[], inc="mm")
                    prev_cwmm = tc1b
                    tc2 = k.op(S, lambda e: e.copy(out=cwb_sb[:, 0:512], in_=PCa[:, :]),
                               wait=[tc1b], inc="cws")
                    tc2 = k.op(S, lambda e: e.copy(out=cwb_sb[:, 512:1024], in_=PCb[:, :]),
                               wait=[], inc="cws")
                else:
                    tc2 = None
                t_w2l = k.dma(SY, w2bf[:],
                              w_src_b[2].rearrange("(c p) d -> p c d", p=128),
                              wait=[last_w2mm, ctok], inc="wl0")
                w2b_toks = [t_w2l] * 16
                eo_toks = []
                prev_sg = prev_tt = prev_te = None
                prev_g_mm = prev_u_mm = None
                gpar = {0: None, 1: None}
                upar = {0: None, 1: None}
                for hs in range(8):
                    cols = slice(hs * 256, hs * 256 + 256)
                    bi = hs % 2
                    t_c1 = k.dma(SY, w1bf[bi][:],
                                 w_src_b[0].rearrange("(c p) h -> p c h", p=128)
                                 [:, :, cols],
                                 wait=[gpar[bi], last_w2mm, ctok],
                                 inc="wl0" if bi == 0 else "wl1")
                    t_c3 = k.dma(SY, w3bf[bi][:],
                                 w_src_b[1].rearrange("(c p) h -> p c h", p=128)
                                 [:, :, cols],
                                 wait=[upar[bi], last_w2mm, ctok],
                                 inc="w3l0" if bi == 0 else "w3l1")
                    for hh in range(2):
                        hc = hs * 2 + hh
                        hcol = slice(hh * 128, hh * 128 + 128)
                        for c in range(DC):
                            k.op(T, lambda e, c=c, hcol=hcol, ts0=ts0, bi=bi:
                                 e.matmul(
                                PGa[:, :], w1bf[bi][:, c, hcol],
                                hbf[:, c, ts0:ts0 + 512],
                                start=(c == 0), stop=(c == DC - 1)),
                                wait=[t_c1, ln_all, prev_sg], inc=None)
                            prev_g_mm = k.op(T, lambda e, c=c, hcol=hcol, ts0=ts0,
                                             bi=bi: e.matmul(
                                PGb[:, :], w1bf[bi][:, c, hcol],
                                hbf[:, c, ts0 + 512:ts0 + 1024],
                                start=(c == 0), stop=(c == DC - 1)),
                                wait=[], inc="mm" if c == DC - 1 else None)
                        for c in range(DC):
                            k.op(T, lambda e, c=c, hcol=hcol, ts0=ts0, bi=bi:
                                 e.matmul(
                                PUa[:, :], w3bf[bi][:, c, hcol],
                                hbf[:, c, ts0:ts0 + 512],
                                start=(c == 0), stop=(c == DC - 1)),
                                wait=[t_c3, prev_tt], inc=None)
                            prev_u_mm = k.op(T, lambda e, c=c, hcol=hcol, ts0=ts0,
                                             bi=bi: e.matmul(
                                PUb[:, :], w3bf[bi][:, c, hcol],
                                hbf[:, c, ts0 + 512:ts0 + 1024],
                                start=(c == 0), stop=(c == DC - 1)),
                                wait=[], inc="mm" if c == DC - 1 else None)
                        gpar[bi] = prev_g_mm
                        upar[bi] = prev_u_mm
                        k.op(S, lambda e: e.activation(out=sg[:, 0:512], in_=PGa[:, :],
                                                       func=AF.Silu),
                             wait=[prev_g_mm, prev_tt], inc="es")
                        t_sg = k.op(S, lambda e: e.activation(
                            out=sg[:, 512:1024], in_=PGb[:, :], func=AF.Silu),
                            wait=[], inc="es")
                        prev_sg = t_sg
                        k.op(V, lambda e: e.tensor_mul(tt_s[:, 0:512], sg[:, 0:512],
                                                       PUa[:, :]),
                             wait=[t_sg, prev_u_mm, prev_te], inc="ev")
                        t_t = k.op(V, lambda e: e.tensor_mul(
                            tt_s[:, 512:1024], sg[:, 512:1024], PUb[:, :]),
                            wait=[], inc="ev")
                        prev_tt = t_t
                        if shared:
                            t_e = k.op(G, lambda e, hc=hc: e.tensor_copy(
                                eo[:, hc, :], tt_s[:]), wait=[t_t], inc="eg")
                        else:
                            t_e = k.op(G, lambda e, hc=hc: e.tensor_mul(
                                eo[:, hc, :], tt_s[:], cwb_sb[:]),
                                wait=[t_t, tc2], inc="eg")
                        eo_toks.append(t_e)
                        prev_te = t_e
                prev_macc = {0: None, 1: None}
                for tsub in range(2):
                    tcol = slice(tsub * 512, tsub * 512 + 512)
                    pwh = (PWa if tsub == 0 else PWb)[:, :]
                    for dc in range(DC):
                        tm = None
                        for hc in range(16):
                            tm = k.op(T, lambda e, hc=hc, dc=dc, tcol=tcol, p=pwh: e.matmul(
                                p, w2bf[:, hc, dc * 128:dc * 128 + 128], eo[:, hc, tcol],
                                start=(hc == 0), stop=(hc == 15)),
                                wait=[w2b_toks[hc], eo_toks[hc]] +
                                     ([prev_macc[tsub]] if hc == 0 else []),
                                inc="mm" if hc == 15 else None)
                        last_w2mm = tm
                        prev_macc[tsub] = k.op(V, lambda e, dc=dc, tcol=tcol, p=pwh:
                            e.tensor_add(macc[:, dc, tcol], macc[:, dc, tcol], p),
                            wait=[tm, t_ms], inc="ev")
            lastv = k.op(V, lambda e: e.nop(), inc="ev")
            for sti in range(4):
                c0 = sti * 256
                t_x = k.dma(SY, xt[:], fm_view(xo)[:, :, ts0 + c0:ts0 + c0 + 256],
                            wait=[lastv, prev_st], inc="ld")
                t_y = k.op(V, lambda e, c0=c0: e.scalar_tensor_tensor(
                    out=xt[:], in0=macc[:, :, c0:c0 + 256], scalar=0.5, in1=xt[:],
                    op0=ALU.mult, op1=ALU.add), wait=[t_x], inc="ev")
                prev_st = k.dma(SY, fm_view(y)[:, :, ts0 + c0:ts0 + c0 + 256], xt[:],
                                wait=[t_y], inc="st")
                prev_ty = t_y


def build_moe():
    nc = bass.Bass("TRN2", target_bir_lowering=False, debug=False)
    xo = nc.dram_tensor("xo", [DM, OWN], F32, kind="ExternalInput").ap()
    rwT = nc.dram_tensor("rwT", [DM, 8], F32, kind="ExternalInput").ap()
    eb = nc.dram_tensor("eb", [8], F32, kind="ExternalInput").ap()
    w1 = nc.dram_tensor("w1", [8, DM, 2048], F32, kind="ExternalInput").ap()
    w3 = nc.dram_tensor("w3", [8, DM, 2048], F32, kind="ExternalInput").ap()
    w2 = nc.dram_tensor("w2", [8, 2048, DM], F32, kind="ExternalInput").ap()
    sw1 = nc.dram_tensor("sw1", [DM, 2048], F32, kind="ExternalInput").ap()
    sw3 = nc.dram_tensor("sw3", [DM, 2048], F32, kind="ExternalInput").ap()
    sw2 = nc.dram_tensor("sw2", [2048, DM], F32, kind="ExternalInput").ap()
    eye = nc.dram_tensor("eye", [128, 128], F32, kind="ExternalInput").ap()
    y = nc.dram_tensor("y", [DM, OWN], F32, kind="ExternalOutput").ap()
    cwscr = nc.dram_tensor("cwscr", [8, OWN], F32, kind="Internal").ap()
    w1_b = nc.dram_tensor("w1_b", [8, DM, 2048], BF16).ap()
    w3_b = nc.dram_tensor("w3_b", [8, DM, 2048], BF16).ap()
    w2_b = nc.dram_tensor("w2_b", [8, 2048, DM], BF16).ap()
    sw1_b = nc.dram_tensor("sw1_b", [DM, 2048], BF16).ap()
    sw3_b = nc.dram_tensor("sw3_b", [DM, 2048], BF16).ap()
    sw2_b = nc.dram_tensor("sw2_b", [2048, DM], BF16).ap()
    stack = ExitStack()
    with stack:
        k = Sched(nc, stack)
        with ExitStack() as cs:
            ctok = emit_bf16_cache(nc, k, cs, [
                (w1, w1_b, 8 * DM * 2048), (w3, w3_b, 8 * DM * 2048),
                (w2, w2_b, 8 * 2048 * DM), (sw1, sw1_b, DM * 2048),
                (sw3, sw3_b, DM * 2048), (sw2, sw2_b, 2048 * DM)], tag="m")
        with ExitStack() as ps:
            emit_moe(nc, k, ps, xo, rwT, eb, w1, w3, w2, sw1, sw3, sw2, eye, y,
                     cwscr, w1_b=w1_b, w3_b=w3_b, w2_b=w2_b, sw1_b=sw1_b,
                     sw3_b=sw3_b, sw2_b=sw2_b, ctok=ctok)
        k.raw_wait("sync", "st", k.cnt["st"])
        k.finalize()
    return nc


def _shards_fm(x):
    out = []
    for core in range(NCORES):
        r, h = core // 2, core % 2
        out.append(np.ascontiguousarray(x[r, h * OWN:(h + 1) * OWN, :].T))
    return out


def _assemble_fm(shards):
    x = np.empty((4, ROWT, DM), np.float32)
    for core in range(NCORES):
        r, h = core // 2, core % 2
        x[r, h * OWN:(h + 1) * OWN, :] = shards[core].T
    return x


_EYE = np.eye(128, dtype=np.float32)


def run_moe_launch(nc_moe, x, W):
    shards = _shards_fm(x)
    base = {
        "rwT": np.ascontiguousarray(np.asarray(W["L1_ffn_router_w"]).T),
        "eb": np.asarray(W["L1_ffn_experts_bias"]),
        "w1": np.asarray(W["L1_ffn_w1"]), "w3": np.asarray(W["L1_ffn_w3"]),
        "w2": np.asarray(W["L1_ffn_w2"]),
        "sw1": np.asarray(W["L1_ffn_sw1"])[0], "sw3": np.asarray(W["L1_ffn_sw3"])[0],
        "sw2": np.asarray(W["L1_ffn_sw2"])[0], "eye": _EYE,
    }
    in_maps = [dict(base, xo=shards[c]) for c in range(NCORES)]
    res = run_bass_kernel_spmd(nc_moe, in_maps, core_ids=list(range(NCORES)))
    return _assemble_fm([np.asarray(res.results[c]["y"]) for c in range(NCORES)])


def emit_l1attn(nc, k, stack, xo, xr, Wq, Wk, Wv, Wo, vecs, eye, y,
                row_read=None):
    if True:
        sfx = _uniq()
        sb = lambda n, sh, dt: stack.enter_context(nc.sbuf_tensor(n + sfx, sh, dt))
        psal = lambda n, sh: stack.enter_context(nc.psum_tensor(n + sfx, sh, F32))
        vtm = sb("vtm", [128, 32, DM], BF16)
        wb1 = sb("wb1", [128, DC, DM], BF16)
        wb2 = sb("wb2", [128, DC, DM], BF16)
        wst = sb("wst", [128, DC, 256], F32)
        hkb = sb("hkb", [128, DC, 512], BF16)
        xt = sb("xt", [128, DC, 256], F32)
        osb = sb("osb", [128, DC, 512], BF16)
        qs_sb = sb("qs_sb", [128, 512], F32)
        sq = [sb("sq0", [128, 256], F32), sb("sq1", [128, 256], F32)]
        tqb = sb("tqb", [128, OWN], F32)
        tkcol = sb("tkcol", [128, 32], F32)
        vscol = sb("vscol", [128, 8], F32)
        dt_s = sb("dt_s", [128, 512], F32)
        ab_s = sb("ab_s", [128, 512], F32)
        et_s = sb("et_s", [128, 512], F32)
        mt_s = sb("mt_s", [128, 512], F32)
        atb = [sb("atb0", [128, 512], BF16), sb("atb1", [128, 512], BF16)]
        rows = sb("rows", [1, 6 * 256], F32)
        tksc = sb("tksc", [1, 512], F32)
        grows = sb("grows", [1, 8 * DM], F32)
        browb = sb("browb", [1, 4 * DM], BF16)
        eye_sb = sb("eye_sb", [128, 128], F32)
        ones_col = sb("ones_col", [128, 1], F32)
        ones_row1 = sb("ones_row1", [1, 128], F32)
        ones_rbf = sb("ones_rbf", [1, 512], BF16)
        ones_cbf = sb("ones_cbf", [128, 1], BF16)
        epscol = sb("epscol", [1, 1], F32)

        P = [psal(f"P{i}", [128, 512]) for i in range(8)]

        for s in ["lnt", "mm"]:
            k.sem(s)
        for s in ["ld", "wl0", "wl1", "st"]:
            k.sem(s, dma=True)
        S, V, T, G, SY = "scalar", "vector", "tensor", "gpsimd", "sync"
        if row_read is None:
            def row_read(dst, t0, n, wait):
                return k.dma(SY, dst, fm_view(xr)[:, :, t0:t0 + n],
                             wait=wait, inc="ld")

        k.op(G, lambda e: e.memset(ones_col[:], 1.0))
        k.op(G, lambda e: e.memset(ones_row1[:], 1.0))
        k.op(G, lambda e: e.memset(ones_rbf[:], 1.0))
        k.op(G, lambda e: e.memset(ones_cbf[:], 1.0))
        t_const = k.op(G, lambda e: e.memset(epscol[:], 1e-5))
        t_eye = k.dma(SY, eye_sb[:], eye, inc="ld")

        r_t = grows[0:1, 7 * DM:8 * DM]
        c_toks = []
        gam_toks = []
        prev_vl = None
        for pi in range(3):
            t_vl = k.dma(SY, grows[0:1, 0:4 * DM],
                         vecs[4 * pi:4 * pi + 4].rearrange("(o a) d -> o (a d)", o=1),
                         wait=[prev_vl], inc="ld")
            g_r = grows[0:1, 0:DM]
            b_r = grows[0:1, DM:2 * DM]
            m_r = grows[0:1, 2 * DM:3 * DM]
            v_r = grows[0:1, 3 * DM:4 * DM]
            gam = grows[0:1, (4 + pi) * DM:(5 + pi) * DM]
            t1 = k.op(S, lambda e, v_r=v_r: e.activation(
                out=r_t, in_=v_r, func=AF.Ln, bias=epscol[0:1, 0:1]),
                wait=[t_vl, t_const])
            t2 = k.op(S, lambda e: e.activation(out=r_t, in_=r_t, func=AF.Exp,
                                                scale=-0.5))
            t3 = k.op(V, lambda e, g_r=g_r, gam=gam: e.tensor_mul(gam, g_r, r_t),
                      wait=[t2])
            gam_toks.append(t3)
            t4 = k.op(V, lambda e, m_r=m_r, gam=gam: e.tensor_mul(r_t, m_r, gam))
            t5 = k.op(V, lambda e, b_r=b_r: e.tensor_sub(r_t, b_r, r_t))
            t6 = k.op(V, lambda e: e.tensor_scalar(out=r_t, in0=r_t, scalar1=1.0,
                                                   scalar2=-1.0, op0=ALU.mult,
                                                   op1=ALU.add))
            tc = k.op(S, lambda e, pi=pi: e.copy(
                out=browb[0:1, pi * DM:(pi + 1) * DM], in_=r_t), wait=[t6])
            c_toks.append(tc)
            prev_vl = tc
        t_bo = k.dma(SY, grows[0:1, 0:DM],
                     vecs[12:13].rearrange("(o a) d -> o (a d)", o=1),
                     wait=[prev_vl], inc="ld")
        t_bo = k.op(S, lambda e: e.copy(out=browb[0:1, 3 * DM:4 * DM],
                                        in_=grows[0:1, 0:DM]), wait=[t_bo])

        def fold_w(wsrc, wdst, pi, waitfor):
            last = waitfor
            if pi is not None:
                gam = grows[0:1, (4 + pi) * DM:(5 + pi) * DM]
                gb1 = k.op(T, lambda e, gam=gam: e.matmul(
                    P[6][:, :], ones_row1[:], gam[0:1, 0:512], start=True, stop=True),
                    wait=[gam_toks[pi]] + waitfor, inc="mm")
                gb2 = k.op(T, lambda e, gam=gam: e.matmul(
                    P[7][:, :], ones_row1[:], gam[0:1, 512:1024], start=True, stop=True),
                    wait=[], inc="mm")
            toks = []
            tf = None
            for qtr in range(4):
                cols = slice(qtr * 256, qtr * 256 + 256)
                t_l = k.dma(SY, wst[:],
                            wsrc.rearrange("(c p) d -> p c d", p=128)[:, :, cols],
                            wait=[tf] + waitfor, inc="wl%d" % (qtr % 2))
                for c in range(DC):
                    if pi is None:
                        tf = k.op(V, lambda e, c=c, cols=cols: e.tensor_copy(
                            wdst[:, c, cols], wst[:, c, :]), wait=[t_l])
                    else:
                        pp = P[6] if qtr < 2 else P[7]
                        po = (qtr % 2) * 256
                        tf = k.op(V, lambda e, c=c, cols=cols, pp=pp, po=po:
                                  e.tensor_mul(wdst[:, c, cols], wst[:, c, :],
                                               pp[:, po:po + 256]),
                                  wait=[t_l, gb2])
                toks.append(tf)
            return toks[-1]

        t_wk = fold_w(Wk, wb1, 1, [])
        t_wv = fold_w(Wv, wb2, 2, [t_wk])

        def emit_ln_tile(reader, t0, waits):
            out_toks = []
            t_prev = None
            for sub in range(2):
                o = sub * 256
                t_x = reader(xt[:], t0 + o, 256, waits + [t_prev])
                toks = emit_ln_fm(
                    k, lambda c: xt[:, c, :], 256,
                    lambda c, o=o: hkb[:, c, o:o + 256],
                    (P[2], P[3]), (P[4], P[5]), sq, rows, ones_col[:], ones_row1[:],
                    [t_x, t_const], tag="ln", eps_col=epscol[0:1, 0:1])
                out_toks.extend(toks)
                t_prev = toks[-1]
            return out_toks[-1]

        vtm_toks = [None] * 32
        prev_sig = None
        prev_ln = None
        tq_tok = None
        tkc_toks = []
        tqb_toks = []
        for ti in range(12):
            qphase = ti >= 8
            t0 = (ti - 8) * 512 if qphase else ti * 512
            if qphase:
                rd = lambda dst, tt0, n, wait: k.dma(
                    SY, dst, fm_view(xo)[:, :, tt0:tt0 + n], wait=wait, inc="ld")
            else:
                rd = row_read
            t_ln = emit_ln_tile(rd, t0, [prev_ln])
            if not qphase:
                for j in range(4):
                    ktc = ti * 4 + j
                    tsl = slice(j * 128, j * 128 + 128)
                    for half in range(2):
                        ph = P[6] if half == 0 else P[7]
                        dcols = slice(half * 512, half * 512 + 512)
                        for c in range(DC):
                            k.op(T, lambda e, c=c, tsl=tsl, dcols=dcols, ph=ph:
                                 e.matmul(ph[:, :], hkb[:, c, tsl], wb2[:, c, dcols],
                                          start=(c == 0), stop=False),
                                 wait=[t_ln, t_wv, prev_sig], inc=None)
                        tm = k.op(T, lambda e, half=half, ph=ph: e.matmul(
                            ph[:, :], ones_rbf[0:1, 0:128],
                            browb[0:1, 2 * DM + half * 512:2 * DM + half * 512 + 512],
                            start=False, stop=True),
                            wait=[c_toks[2]], inc="mm")
                        prev_sig = k.op(S, lambda e, ktc=ktc, dcols=dcols, ph=ph:
                                        e.activation(out=vtm[:, ktc, dcols],
                                                     in_=ph[:, :],
                                                     func=AF.Sigmoid, scale=4.0),
                                        wait=[tm])
                    vtm_toks[ktc] = prev_sig
            wb = wb1
            crow_off = 0 if qphase else DM
            w_tok = t_wq if qphase else t_wk
            tsum = None
            for dc in range(DC):
                ph = P[6] if dc % 2 == 0 else P[7]
                for c in range(DC):
                    k.op(T, lambda e, c=c, dc=dc, wb=wb, ph=ph: e.matmul(
                        ph[:, :], wb[:, c, dc * 128:dc * 128 + 128], hkb[:, c, :],
                        start=(c == 0), stop=False),
                        wait=[t_ln, w_tok, prev_sig], inc=None)
                tm = k.op(T, lambda e, dc=dc, ph=ph, crow_off=crow_off: e.matmul(
                    ph[:, :],
                    browb[0:1, crow_off + dc * 128:crow_off + dc * 128 + 128],
                    ones_rbf[:], start=False, stop=True),
                    wait=[c_toks[0] if qphase else c_toks[1]], inc="mm")
                prev_sig = k.op(S, lambda e, ph=ph: e.activation(
                    out=qs_sb[:], in_=ph[:, :], func=AF.Sigmoid, scale=4.0),
                    wait=[tm])
                tsum = k.op(T, lambda e, st=(dc == 0), sp=(dc == DC - 1): e.matmul(
                    P[2][0:1, 0:512], ones_col[:], qs_sb[:], start=st, stop=sp),
                    wait=[prev_sig, prev_ln], inc="mm")
            tq_tok = k.op(V, lambda e: e.tensor_scalar(
                out=tksc[:], in0=P[2][0:1, 0:512],
                scalar1=-1.0 / DM, scalar2=1.0, op0=ALU.mult, op1=ALU.add),
                wait=[tsum])
            if qphase:
                qt = ti - 8
                tb = k.op(T, lambda e: e.matmul(
                    P[7][:, :], ones_row1[:], tksc[:], start=True, stop=True),
                    wait=[tq_tok, prev_sig, t_eye], inc="mm")
                tq_tok = k.op(S, lambda e, qt=qt: e.copy(
                    out=tqb[:, qt * 512:qt * 512 + 512], in_=P[7][:, :]), wait=[tb])
                tqb_toks.append(tq_tok)
            else:
                for j in range(4):
                    tt1 = k.op(T, lambda e, j=j: e.matmul(
                        P[7][0:128, 0:1], tksc[0:1, j * 128:j * 128 + 128],
                        ones_row1[0:1, 0:1], start=True, stop=True),
                        wait=[tq_tok, prev_sig] +
                             ([tkc_toks[-1]] if tkc_toks else []), inc="mm")
                    tq_tok = k.op(S, lambda e, ti=ti, j=j: e.copy(
                        out=tkcol[:, ti * 4 + j:ti * 4 + j + 1],
                        in_=P[7][0:128, 0:1]), wait=[tt1])
                    tkc_toks.append(tq_tok)
            prev_ln = tq_tok
            if ti == 7:
                t_wq = fold_w(Wq, wb1, 0, [prev_sig, tq_tok])

        vs_toks = []
        prev_vs = tq_tok
        for half in range(2):
            tm = None
            for ktc in range(32):
                tm = k.op(T, lambda e, ktc=ktc, half=half, st=(ktc == 0),
                          sp=(ktc == 31): e.matmul(
                    P[6][0:1, 0:512], ones_cbf[:],
                    vtm[:, ktc, half * 512:half * 512 + 512],
                    start=st, stop=sp),
                    wait=[vtm_toks[ktc]] + ([prev_vs] if ktc == 0 else []), inc="mm")
            tr = k.op(V, lambda e: e.tensor_scalar_mul(
                rows[0:1, 0:512], P[6][0:1, 0:512], 0.9), wait=[tm])
            for j in range(4):
                tt1 = k.op(T, lambda e, j=j: e.matmul(
                    P[7][0:128, 0:1], rows[0:1, j * 128:j * 128 + 128],
                    ones_row1[0:1, 0:1], start=True, stop=True),
                    wait=[tr, prev_vs], inc="mm")
                prev_vs = k.op(S, lambda e, j=j, half=half: e.copy(
                    out=vscol[:, half * 4 + j:half * 4 + j + 1],
                    in_=P[7][0:128, 0:1]), wait=[tt1])
                vs_toks.append(prev_vs)
        t_vs = vs_toks[-1]

        t_wo = fold_w(Wo, wb2, None, [vtm_toks[31], prev_sig, t_vs])

        prev_st = None
        prev_add = None
        prev_osb = None
        at_rd = [None, None]
        for qt in range(4):
            qsl = slice(qt * 512, qt * 512 + 512)
            av_last = None
            for ktc in range(32):
                bi = ktc % 2
                tv1 = k.op(V, lambda e, qsl=qsl, ktc=ktc: e.tensor_scalar(
                    out=dt_s[:], in0=tqb[:, qsl], scalar1=tkcol[:, ktc:ktc + 1],
                    scalar2=None, op0=ALU.subtract),
                    wait=[tkc_toks[ktc], tqb_toks[qt], at_rd[bi]])
                tg1 = k.op(S, lambda e: e.activation(out=ab_s[:], in_=dt_s[:],
                                                     func=AF.Abs), wait=[tv1])
                ts1 = k.op(S, lambda e: e.activation(out=et_s[:], in_=ab_s[:],
                                                     func=AF.Exp, scale=-2.0),
                           wait=[tg1])
                tv2 = k.op(V, lambda e: e.tensor_scalar(
                    out=mt_s[:], in0=dt_s[:], scalar1=0.0, scalar2=None,
                    op0=ALU.is_lt), wait=[tv1])
                tg2 = k.op(V, lambda e: e.tensor_scalar(
                    out=mt_s[:], in0=mt_s[:], scalar1=2.0, scalar2=-1.0,
                    op0=ALU.mult, op1=ALU.add), wait=[tv2])
                tv3 = k.op(V, lambda e, bi=bi: e.tensor_mul(atb[bi][:], mt_s[:],
                                                            et_s[:]),
                           wait=[tg2, ts1])
                for dc in range(DC):
                    av_last = k.op(T, lambda e, dc=dc, ktc=ktc, bi=bi,
                                   st=(ktc == 0), sp=(ktc == 31): e.matmul(
                        P[dc][:, :], vtm[:, ktc, dc * 128:dc * 128 + 128],
                        atb[bi][:], start=st, stop=sp),
                        wait=[tv3, vtm_toks[ktc], t_wo] +
                             ([prev_osb] if ktc == 0 else []),
                        inc="mm" if dc == DC - 1 else None)
                at_rd[bi] = av_last
            for dc in range(DC):
                prev_osb = k.op(V, lambda e, dc=dc: e.tensor_scalar(
                    out=osb[:, dc, :], in0=P[dc][:, :], scalar1=0.9,
                    scalar2=vscol[:, dc:dc + 1], op0=ALU.mult, op1=ALU.add),
                    wait=[av_last, t_vs])
            t0 = qt * 512
            for sub in range(2):
                o = sub * 256
                t_x = k.dma(SY, xt[:], fm_view(xo)[:, :, t0 + o:t0 + o + 256],
                            wait=[prev_st, prev_add, prev_ln], inc="ld")
                for dc in range(DC):
                    ph = P[dc % 2]
                    for c in range(DC):
                        k.op(T, lambda e, c=c, dc=dc, ph=ph, o=o: e.matmul(
                            ph[:, 0:256], wb2[:, c, dc * 128:dc * 128 + 128],
                            osb[:, c, o:o + 256], start=(c == 0), stop=False),
                            wait=[prev_osb, t_wo] + ([prev_add] if c == 0 else []),
                            inc=None)
                    tm = k.op(T, lambda e, dc=dc, ph=ph: e.matmul(
                        ph[:, 0:256],
                        browb[0:1, 3 * DM + dc * 128:3 * DM + dc * 128 + 128],
                        ones_rbf[0:1, 0:256], start=False, stop=True),
                        wait=[t_bo], inc="mm")
                    prev_add = k.op(V, lambda e, dc=dc, ph=ph: e.tensor_add(
                        xt[:, dc, :], xt[:, dc, :], ph[:, 0:256]), wait=[tm, t_x])
                prev_st = k.dma(SY, fm_view(y)[:, :, t0 + o:t0 + o + 256], xt[:],
                                wait=[prev_add], inc="st")




def build_l1attn():
    nc = bass.Bass("TRN2", target_bir_lowering=False, debug=False)
    xo = nc.dram_tensor("xo", [DM, OWN], F32, kind="ExternalInput").ap()
    xr = nc.dram_tensor("xr", [DM, ROWT], F32, kind="ExternalInput").ap()
    Wq = nc.dram_tensor("Wq", [DM, DM], F32, kind="ExternalInput").ap()
    Wk = nc.dram_tensor("Wk", [DM, DM], F32, kind="ExternalInput").ap()
    Wv = nc.dram_tensor("Wv", [DM, DM], F32, kind="ExternalInput").ap()
    Wo = nc.dram_tensor("Wo", [DM, DM], F32, kind="ExternalInput").ap()
    vecs = nc.dram_tensor("vecs", [13, DM], F32, kind="ExternalInput").ap()
    eye = nc.dram_tensor("eye", [128, 128], F32, kind="ExternalInput").ap()
    y = nc.dram_tensor("y", [DM, OWN], F32, kind="ExternalOutput").ap()
    stack = ExitStack()
    with stack:
        k = Sched(nc, stack)
        with ExitStack() as ps:
            emit_l1attn(nc, k, ps, xo, xr, Wq, Wk, Wv, Wo, vecs, eye, y)
        k.raw_wait("sync", "st", k.cnt["st"])
        k.finalize()
    return nc


def run_l1attn_launch(nc_a, x, W):
    shards = _shards_fm(x)
    vecs = np.stack([
        W["L1_attn_gq"], W["L1_attn_bq"], W["L1_attn_mq"], W["L1_attn_vq"],
        W["L1_attn_gk"], W["L1_attn_bk"], W["L1_attn_mk"], W["L1_attn_vk"],
        W["L1_attn_gv"], W["L1_attn_bv"], W["L1_attn_mv"], W["L1_attn_vv"],
        W["L1_attn_bo"]]).astype(np.float32)
    base = {
        "Wq": np.asarray(W["L1_attn_Wq"]), "Wk": np.asarray(W["L1_attn_Wk"]),
        "Wv": np.asarray(W["L1_attn_Wv"]), "Wo": np.asarray(W["L1_attn_Wo"]),
        "vecs": vecs, "eye": _EYE,
    }
    in_maps = []
    for c in range(NCORES):
        r = c // 2
        xrow = np.ascontiguousarray(x[r].T)
        in_maps.append(dict(base, xo=shards[c], xr=xrow))
    res = run_bass_kernel_spmd(nc_a, in_maps, core_ids=list(range(NCORES)))
    return _assemble_fm([np.asarray(res.results[c]["y"]) for c in range(NCORES)])


def emit_l0attn(nc, k, stack, xo, xr, Wq, Wk, Wv, Wo, brows, tcol, eyeb, y,
                row_read=None):
    if True:
        sfx = _uniq()
        sb = lambda n, sh, dt: stack.enter_context(nc.sbuf_tensor(n + sfx, sh, dt))
        qbf = sb("qbf", [128, DC, OWN], BF16)
        osb = sb("osb", [128, DC, OWN], BF16)
        wbuf = sb("wbuf", [128, DC, DM], BF16)
        wst = sb("wst", [128, DC, 256], F32)
        hkb = sb("hkb", [128, DC, 256], BF16)
        hf = sb("hf", [128, DC, 256], F32)
        xt = sb("xt", [128, DC, 256], F32)
        summ = sb("summ", [128, DC, 64], F32)
        summ_b = sb("summ_b", [128, DC, 64], BF16)
        kbf = sb("kbf", [128, DC, 64], BF16)
        vtm = sb("vtm", [64, DM], BF16)
        sq = [sb("sq0", [128, 256], F32), sb("sq1", [128, 256], F32)]
        rows = sb("rows", [1, 6 * 256], F32)
        browb = sb("browb", [1, 4 * DM], BF16)
        brf = sb("brf", [1, 4 * DM], F32)
        i64b = sb("i64b", [128, 64], F32)
        i64r = sb("i64r", [1, 64], F32)
        tcs = sb("tcs", [128, 16], F32)
        mask01 = sb("mask01", [128, 64], F32)
        ms_s = sb("ms_s", [128, 64], F32)
        att_f = sb("att_f", [128, 64], F32)
        att_b = sb("att_b", [128, 64], BF16)
        attT = sb("attT", [64, 128], BF16)
        mcol = sb("mcol", [128, 4], F32)
        eye_b = sb("eye_b", [128, 128], BF16)
        eye_f = sb("eye_f", [128, 128], F32)
        ones_col = sb("ones_col", [128, 1], F32)
        ones_row1 = sb("ones_row1", [1, 128], F32)
        ones_rbf = sb("ones_rbf", [1, 512], BF16)
        epscol = sb("epscol", [1, 1], F32)

        P = [stack.enter_context(nc.psum_tensor(f"P{i}{sfx}", [128, 512], F32))
             for i in range(6)]
        PTb = stack.enter_context(nc.psum_tensor("PTb" + sfx, [64, 512], BF16))
        PO = stack.enter_context(nc.psum_tensor("PO" + sfx, [64, 512], F32))

        for s in ["lnt", "mm"]:
            k.sem(s)
        for s in ["ld", "wl0", "wl1", "st"]:
            k.sem(s, dma=True)
        S, V, T, G, SY = "scalar", "vector", "tensor", "gpsimd", "sync"
        if row_read is None:
            def row_read(dst, t0, n, wait, slow=False):
                return k.dma(SY, dst, fm_view(xr)[:, :, t0:t0 + n],
                             wait=wait, inc="ld", slow=slow)

        k.op(G, lambda e: e.memset(ones_col[:], 1.0))
        k.op(G, lambda e: e.memset(ones_row1[:], 1.0))
        k.op(G, lambda e: e.memset(ones_rbf[:], 1.0))
        t_const = k.op(G, lambda e: e.memset(epscol[:], 1e-5))
        t_eye = k.dma(SY, eye_f[:], eyeb, inc="ld")
        t_eyeb = k.op(S, lambda e: e.copy(out=eye_b[:], in_=eye_f[:]), wait=[t_eye])
        t_tc = k.dma(SY, tcs[:], tcol, inc="ld")
        t_br = k.dma(SY, brf[:], brows.rearrange("(o a) d -> o (a d)", o=1), inc="ld")
        t_brb = k.op(S, lambda e: e.copy(out=browb[:], in_=brf[:]), wait=[t_br])
        t_io = k.op(G, lambda e: e.iota(i64r[:], pattern=[[64, 64]], base=0,
                                        channel_multiplier=0, allow_small_or_imprecise_dtypes=True))
        t_iob = k.op(T, lambda e: e.matmul(P[4][:, 0:64], ones_row1[:], i64r[:],
                                           start=True, stop=True),
                     wait=[t_io], inc="mm")
        t_iobs = k.op(S, lambda e: e.copy(out=i64b[:], in_=P[4][:, 0:64]),
                      wait=[t_iob])

        def cast_w(wsrc, waitfor):
            tf = None
            for qtr in range(4):
                cols = slice(qtr * 256, qtr * 256 + 256)
                t_l = k.dma(SY, wst[:],
                            wsrc.rearrange("(c p) d -> p c d", p=128)[:, :, cols],
                            wait=[tf] + waitfor, inc="wl%d" % (qtr % 2))
                for c in range(DC):
                    tf = k.op(V, lambda e, c=c, cols=cols: e.tensor_copy(
                        wbuf[:, c, cols], wst[:, c, :]), wait=[t_l])
            return tf

        t_sm0 = k.op(G, lambda e: e.memset(summ[:], 0.0))
        prev_ln = None
        sum_toks = []
        t_x0 = row_read(xt[:, :, 0:1], 0, 1, [], slow=True)
        toks0 = emit_ln_fm(
            k, lambda c: xt[:, c, 0:1], 1,
            lambda c: hkb[:, c, 0:1],
            (P[0], P[1]), (P[2], P[3]), sq, rows, ones_col[:], ones_row1[:],
            [t_x0, t_const, t_sm0], tag="ln", eps_col=epscol[0:1, 0:1],
            extra_dst=lambda c: summ[:, c, 0:1])
        prev_ln = toks0[-1]
        sum_toks.append(prev_ln)
        for ti in range(16):
            t0 = 1 + ti * 256
            ntk = 256 if ti < 15 else 192
            ng = ntk // 64
            t_x = row_read(xt[:, :, 0:ntk], t0, ntk, [prev_ln])
            toks = emit_ln_fm(
                k, lambda c, ntk=ntk: xt[:, c, 0:ntk], ntk,
                lambda c, ntk=ntk: hkb[:, c, 0:ntk],
                (P[0], P[1]), (P[2], P[3]), sq, rows, ones_col[:], ones_row1[:],
                [t_x, t_const], tag="ln", eps_col=epscol[0:1, 0:1],
                extra_dst=lambda c, ntk=ntk: hf[:, c, 0:ntk])
            g0 = 1 + ti * 4
            for c in range(DC):
                tr = k.op(V, lambda e, c=c, g0=g0, ng=ng, ntk=ntk: e.reduce_sum(
                    summ[:, c, g0:g0 + ng],
                    hf[:, c, 0:ntk].rearrange("p (g j) -> p g j", j=64),
                    axis=AX.X), wait=[toks[c], t_sm0])
                sum_toks.append(tr)
            prev_ln = sum_toks[-1]
        t_means = k.op(V, lambda e: e.tensor_scalar_mul(
            summ[:, :, 1:64], summ[:, :, 1:64], 1.0 / 64.0), wait=[prev_ln])
        t_sumb = k.op(S, lambda e: e.copy(out=summ_b[:], in_=summ[:]),
                      wait=[t_means])

        t_wk = cast_w(Wk, [])
        prev_sig = None
        for dc in range(DC):
            ph = P[4] if dc % 2 == 0 else P[5]
            for c in range(DC):
                k.op(T, lambda e, c=c, dc=dc, ph=ph: e.matmul(
                    ph[:, 0:64], wbuf[:, c, dc * 128:dc * 128 + 128],
                    summ_b[:, c, :], start=(c == 0), stop=False),
                    wait=[t_sumb, t_wk, prev_sig, t_iobs], inc=None)
            tm = k.op(T, lambda e, dc=dc, ph=ph: e.matmul(
                ph[:, 0:64], browb[0:1, DM + dc * 128:DM + dc * 128 + 128],
                ones_rbf[0:1, 0:64], start=False, stop=True),
                wait=[t_brb], inc="mm")
            prev_sig = k.op(S, lambda e, dc=dc, ph=ph: e.copy(
                out=kbf[:, dc, :], in_=ph[:, 0:64]), wait=[tm])
        t_wv = cast_w(Wv, [prev_sig])
        vt_toks = []
        for half in range(2):
            ph = P[4] if half == 0 else P[5]
            dcols = slice(half * 512, half * 512 + 512)
            for c in range(DC):
                k.op(T, lambda e, c=c, dcols=dcols, ph=ph: e.matmul(
                    ph[0:64, :], summ_b[:, c, :], wbuf[:, c, dcols],
                    start=(c == 0), stop=False),
                    wait=[t_wv, prev_sig], inc=None)
            tm = k.op(T, lambda e, half=half, ph=ph: e.matmul(
                ph[0:64, :], ones_rbf[0:1, 0:64],
                browb[0:1, 2 * DM + half * 512:2 * DM + half * 512 + 512],
                start=False, stop=True), wait=[t_brb], inc="mm")
            prev_sig = k.op(S, lambda e, dcols=dcols, ph=ph: e.copy(
                out=vtm[:, dcols], in_=ph[0:64, :]), wait=[tm])
            vt_toks.append(prev_sig)
        t_vtm = vt_toks[-1]

        t_wq = cast_w(Wq, [t_vtm])
        prev_q = {0: None, 1: None}
        q_toks = []
        for ti in range(8):
            t0 = ti * 256
            t_x = k.dma(SY, xt[:, :, 0:256], fm_view(xo)[:, :, t0:t0 + 256],
                        wait=[prev_ln, prev_q[0], prev_q[1]], inc="ld")
            toks = emit_ln_fm(
                k, lambda c: xt[:, c, 0:256], 256,
                lambda c: hkb[:, c, 0:256],
                (P[0], P[1]), (P[2], P[3]), sq, rows, ones_col[:], ones_row1[:],
                [t_x, t_const], tag="ln", eps_col=epscol[0:1, 0:1])
            t_ln = toks[-1]
            prev_ln = t_ln
            for dc in range(DC):
                bi = dc % 2
                ph = P[4] if bi == 0 else P[5]
                for c in range(DC):
                    k.op(T, lambda e, c=c, dc=dc, ph=ph: e.matmul(
                        ph[:, 0:256], wbuf[:, c, dc * 128:dc * 128 + 128],
                        hkb[:, c, 0:256], start=(c == 0), stop=False),
                        wait=[t_ln, t_wq, prev_q[bi]], inc=None)
                tm = k.op(T, lambda e, dc=dc, ph=ph: e.matmul(
                    ph[:, 0:256], browb[0:1, dc * 128:dc * 128 + 128],
                    ones_rbf[0:1, 0:256], start=False, stop=True),
                    wait=[t_brb], inc="mm")
                prev_q[bi] = k.op(S, lambda e, dc=dc, t0=t0, ph=ph: e.copy(
                    out=qbf[:, dc, t0:t0 + 256], in_=ph[:, 0:256]), wait=[tm])
            q_toks.append(prev_q[0])
            q_toks.append(prev_q[1])
        t_qall = q_toks[-1]

        prev_sc = {0: None, 1: None}
        prev_tp = None
        prev_av = None
        osb_toks = []
        for qc in range(16):
            qsl = slice(qc * 128, qc * 128 + 128)
            t_mk = k.op(V, lambda e, qc=qc: e.tensor_scalar(
                out=mask01[:], in0=i64b[:], scalar1=tcs[:, qc:qc + 1],
                scalar2=None, op0=ALU.is_le), wait=[t_iobs, t_tc, prev_av])
            for h in range(16):
                bi = h % 2
                ph = P[4] if bi == 0 else P[5]
                hc, hp = h // 2, 64 * (h % 2)
                tsc = k.op(T, lambda e, ph=ph, hc=hc, hp=hp, qsl=qsl: e.matmul(
                    ph[:, 0:64], qbf[hp:hp + 64, hc, qsl], kbf[hp:hp + 64, hc, :],
                    start=True, stop=True),
                    wait=[t_qall, prev_sig, prev_sc[bi]], inc="mm")
                tv1 = k.op(V, lambda e, ph=ph: e.scalar_tensor_tensor(
                    out=ms_s[:], in0=mask01[:], scalar=30000.0, in1=ph[:, 0:64],
                    op0=ALU.mult, op1=ALU.add), wait=[tsc, t_mk])
                prev_sc[bi] = tv1
                tv2 = k.op(V, lambda e: e.reduce_max(
                    mcol[:, 0:1], ms_s[:], axis=AX.X, negate=True), wait=[])
                tv3 = k.op(V, lambda e: e.tensor_scalar_mul(
                    mcol[:, 1:2], mcol[:, 0:1], 0.125), wait=[])
                ts1 = k.op(S, lambda e: e.activation(
                    out=att_f[:], in_=ms_s[:], func=AF.Exp, scale=0.125,
                    bias=mcol[:, 1:2]), wait=[tv3])
                tv4 = k.op(V, lambda e: e.reduce_sum(
                    mcol[:, 2:3], att_f[:], axis=AX.X), wait=[ts1])
                tv5 = k.op(V, lambda e: e.reciprocal(
                    mcol[:, 3:4], mcol[:, 2:3]), wait=[])
                tv6 = k.op(V, lambda e: e.tensor_scalar(
                    out=att_b[:], in0=att_f[:], scalar1=mcol[:, 3:4],
                    scalar2=None, op0=ALU.mult), wait=[tv5])
                ttp = k.op(T, lambda e: e.transpose(
                    PTb[0:64, 0:128], att_b[:], eye_b[:]),
                    wait=[tv6, t_eyeb, prev_tp], inc="mm")
                tcp = k.op(S, lambda e: e.copy(out=attT[:], in_=PTb[0:64, 0:128]),
                           wait=[ttp])
                prev_tp = tcp
                tav = k.op(T, lambda e, h=h: e.matmul(
                    PO[0:64, 0:128], vtm[:, h * 64:h * 64 + 64].rearrange(
                        "c d -> c d") if False else vtm[0:64, h * 64:h * 64 + 64],
                    attT[:], start=True, stop=True),
                    wait=[tcp, t_vtm, prev_av], inc="mm")
                prev_av = k.op(S, lambda e, hc=hc, hp=hp, qsl=qsl: e.copy(
                    out=osb[hp:hp + 64, hc, qsl], in_=PO[0:64, 0:128]),
                    wait=[tav])
                osb_toks.append(prev_av)
        t_oall = osb_toks[-1]

        t_wo = cast_w(Wo, [t_oall])
        prev_st = None
        prev_add = None
        for ti in range(8):
            t0 = ti * 256
            t_x = k.dma(SY, xt[:, :, 0:256], fm_view(xo)[:, :, t0:t0 + 256],
                        wait=[prev_st, prev_add, prev_ln], inc="ld")
            for dc in range(DC):
                bi = dc % 2
                ph = P[4] if bi == 0 else P[5]
                for c in range(DC):
                    k.op(T, lambda e, c=c, dc=dc, ph=ph, t0=t0: e.matmul(
                        ph[:, 0:256], wbuf[:, c, dc * 128:dc * 128 + 128],
                        osb[:, c, t0:t0 + 256], start=(c == 0), stop=False),
                        wait=[t_oall, t_wo] + ([prev_add] if c == 0 else []),
                        inc=None)
                tm = k.op(T, lambda e, dc=dc, ph=ph: e.matmul(
                    ph[:, 0:256], browb[0:1, 3 * DM + dc * 128:3 * DM + dc * 128 + 128],
                    ones_rbf[0:1, 0:256], start=False, stop=True),
                    wait=[t_brb], inc="mm")
                prev_add = k.op(V, lambda e, dc=dc, ph=ph: e.tensor_add(
                    xt[:, dc, :], xt[:, dc, :], ph[:, 0:256]), wait=[tm, t_x])
            prev_st = k.dma(SY, fm_view(y)[:, :, t0:t0 + 256], xt[:, :, 0:256],
                            wait=[prev_add], inc="st")


def build_l0attn():
    nc = bass.Bass("TRN2", target_bir_lowering=False, debug=False)
    xo = nc.dram_tensor("xo", [DM, OWN], F32, kind="ExternalInput").ap()
    xr = nc.dram_tensor("xr", [DM, ROWT], F32, kind="ExternalInput").ap()
    Wq = nc.dram_tensor("Wq", [DM, DM], F32, kind="ExternalInput").ap()
    Wk = nc.dram_tensor("Wk", [DM, DM], F32, kind="ExternalInput").ap()
    Wv = nc.dram_tensor("Wv", [DM, DM], F32, kind="ExternalInput").ap()
    Wo = nc.dram_tensor("Wo", [DM, DM], F32, kind="ExternalInput").ap()
    brows = nc.dram_tensor("brows", [4, DM], F32, kind="ExternalInput").ap()
    tcol = nc.dram_tensor("tcol", [128, 16], F32, kind="ExternalInput").ap()
    eyeb = nc.dram_tensor("eyeb", [128, 128], F32, kind="ExternalInput").ap()
    y = nc.dram_tensor("y", [DM, OWN], F32, kind="ExternalOutput").ap()
    stack = ExitStack()
    with stack:
        k = Sched(nc, stack)
        with ExitStack() as ps:
            emit_l0attn(nc, k, ps, xo, xr, Wq, Wk, Wv, Wo, brows, tcol, eyeb, y)
        k.raw_wait("sync", "st", k.cnt["st"])
        k.finalize()
    return nc


def run_l0attn_launch(nc_a, x, W):
    shards = _shards_fm(x)
    brows = np.stack([
        np.asarray(W["L0_attn_bq"]), np.asarray(W["L0_attn_bk"]),
        np.asarray(W["L0_attn_bv"]), np.asarray(W["L0_attn_bo"])]).astype(np.float32)
    base = {
        "Wq": np.asarray(W["L0_attn_Wq"]), "Wk": np.asarray(W["L0_attn_Wk"]),
        "Wv": np.asarray(W["L0_attn_Wv"]), "Wo": np.asarray(W["L0_attn_Wo"]),
        "brows": brows, "eyeb": _EYE,
    }
    in_maps = []
    for c in range(NCORES):
        r, h = c // 2, c % 2
        s = h * OWN
        tcolv = (s + np.arange(OWN).reshape(16, 128).T).astype(np.float32)
        in_maps.append(dict(base, xo=shards[c],
                            xr=np.ascontiguousarray(x[r].T),
                            tcol=np.ascontiguousarray(tcolv)))
    res = run_bass_kernel_spmd(nc_a, in_maps, core_ids=list(range(NCORES)))
    return _assemble_fm([np.asarray(res.results[c]["y"]) for c in range(NCORES)])


def emit_l0ffn(nc, k, stack, xc, wqkv, wo, w1, w2, lrows, b1r, qkcol, lgbi,
               Cq, Sq, Ck, Sk, Psw, E8i, E8Ti, eyef, mbk, hscol, y,
               wqkv_b=None, wo_b=None, w1_b=None, w2_b=None, ctok=None):
    if True:
        sfx = _uniq()
        sb = lambda n, sh, dt: stack.enter_context(nc.sbuf_tensor(n + sfx, sh, dt))
        h1buf = sb("h1buf", [128, DC, 3073], BF16)
        h0r = sb("h0r", [128, DC, 513], BF16)
        xmid = sb("xmid", [128, DC, 513], F32)
        hln = sb("hln", [128, DC, 512], BF16)
        eo = sb("eo", [128, 22, 512], BF16)
        wqkvb = sb("wqkvb", [128, DC, 1536], BF16)
        wob = sb("wob", [128, 4, DM], BF16)
        w1as = sb("w1as", [128, DC, 128], F32)
        w1ab = [sb("w1ab0", [128, DC, 128], BF16), sb("w1ab1", [128, DC, 128], BF16)]
        w1gb = [sb("w1gb0", [128, DC, 128], BF16), sb("w1gb1", [128, DC, 128], BF16)]
        w2sb = [sb("w2sb0", [128, 512], BF16), sb("w2sb1", [128, 512], BF16)]
        qh = sb("qh", [128, 512], F32)
        t1s = sb("t1s", [128, 512], F32)
        qrot = sb("qrot", [128, 4, 512], BF16)
        krot = sb("krot", [128, 4, 1024], BF16)
        vtm = sb("vtm", [128, 8, 512], BF16)
        attf = sb("attf", [128, 1024], F32)
        attT = sb("attT", [128, 128], BF16)
        o_bf = sb("o_bf", [128, 4, 512], BF16)
        sq = [sb("sq0", [128, 256], F32), sb("sq1", [128, 256], F32)]
        gg = sb("gg", [128, 512], F32)
        rss = sb("rss", [8, 512], F32)
        rbs = sb("rbs", [128, 512], F32)
        mbs = sb("mbs", [128, 4, 1024], BF16)
        E8s = sb("E8s", [128, 4, 8], F32)
        E8Ts = sb("E8Ts", [8, 4, 128], F32)
        psws = sb("psws", [128, 128], F32)
        eyes = sb("eyes", [128, 128], F32)
        lgb = sb("lgb", [128, DC, 4], F32)
        b2b = sb("b2b", [1, DM], BF16)
        b1sb = sb("b1sb", [1, 256], BF16)
        qkc = sb("qkc", [128, 4], F32)
        hss = sb("hss", [128, 2], F32)
        mcol = sb("mcol", [128, 4], F32)
        rows = sb("rows", [1, 5 * 128 + 256], F32)
        ones_col = sb("ones_col", [128, 1], F32)
        ones_row1 = sb("ones_row1", [1, 128], F32)
        ones_rbf = sb("ones_rbf", [1, 512], BF16)
        epscol = sb("epscol", [1, 1], F32)

        P = [stack.enter_context(nc.psum_tensor(f"P{i}{sfx}", [128, 512], F32))
             for i in range(8)]

        for s in ["lnt", "mm"]:
            k.sem(s)
        for s in ["ld", "wl0", "wl1", "wg0", "wg1", "bl", "tb", "st"]:
            k.sem(s, dma=True)
        S, V, T, G, SY = "scalar", "vector", "tensor", "gpsimd", "sync"

        k.op(G, lambda e: e.memset(ones_col[:], 1.0))
        k.op(G, lambda e: e.memset(ones_row1[:], 1.0))
        k.op(G, lambda e: e.memset(ones_rbf[:], 1.0))
        t_const = k.op(G, lambda e: e.memset(epscol[:], 1e-5))
        t_cs = []
        for (dst, src_) in [(mbs, mbk), (E8s, E8i), (E8Ts, E8Ti),
                            (psws, Psw), (eyes, eyef), (hss, hscol)]:
            t_cs.append(k.dma(SY, dst[:], src_, inc="ld"))
        t_b2b = None
        t_lg = k.dma(SY, lgb[:], lgbi, inc="ld")
        t_lb = t_lg
        t_qk = k.dma(SY, qkc[:], qkcol.rearrange("l a p -> p (l a)"), inc="ld",
                     slow=True)
        t_q8a = k.op(S, lambda e: e.mul(out=qkc[:, 0:1], in_=qkc[:, 0:1], mul=8.0),
                     wait=[t_qk])
        t_q8b = k.op(S, lambda e: e.mul(out=qkc[:, 2:3], in_=qkc[:, 2:3], mul=8.0),
                     wait=[t_qk])
        t_con = k.op(V, lambda e: e.nop(),
                     wait=t_cs + [t_q8a, t_q8b, t_const, t_lg, t_lb])

        st = {"qkvmm": None, "attmm": None, "womm": None, "w2mm": None,
              "w2mmlast": None, "b2tok": None,
              "ssmm": None, "ffnmm": None, "ystore": None,
              "xmid_tok": None, "hln_tok": None, "h1_tok": None,
              "h0_tok": None, "qr": None, "kr": {}, "vt": {},
              "o_tok": None, "eo_toks": [None] * 22, "roll": None,
              "sig": None, "evac": [None] * 8, "attread": None,
              "tabq": None, "tabk": None, "wtok": None, "b1tok": None,
              "w2cast": None, "w1cast": None, "lnlast": None}

        def load_layer_weights(l, waits):
            k.sem("wfin")
            t1 = k.dma(SY, wqkvb[:],
                       wqkv_b[l].rearrange("(c p) d -> p c d", p=128),
                       wait=waits + [ctok], inc="wl0")
            t2 = k.dma(SY, wob[:],
                       wo_b[l].rearrange("(c p) d -> p c d", p=128),
                       wait=waits + [ctok], inc="wl1")
            return k.op(SY, lambda e: e.nop(), wait=[t1, t2], inc="wfin")

        def hget(l, c, w0, shifted):
            off = -1 if shifted else 0
            if l == 0:
                return h0r[:, c, 1 + off:513 + off]
            b = w0 - 1024 + off
            return h1buf[:, c, b:b + 512]

        def hget_tile(l, c, w0, shifted, t0, n):
            a = hget(l, c, w0, shifted)
            return a[:, t0:t0 + n]

        def emit_h0_from_x(w0):
            out = []
            prev = None
            for sub in range(4):
                o = sub * 128
                t_x = k.dma(SY, w1as[:], fm_view(xc)[:, :, w0 + o:w0 + o + 128],
                            wait=[prev, st["roll"], st["lnlast"],
                                  st["w1cast"]], inc="ld")
                toks = emit_ln_fm(
                    k, lambda c: w1as[:, c, :], 128,
                    lambda c, o=o: h0r[:, c, 1 + o:1 + o + 128],
                    (P[0], P[1]), (P[2], P[3]), sq, rows,
                    ones_col[:], ones_row1[:],
                    [t_x, t_con, st["qkvmm"]], tag="ln",
                    eps_col=epscol[0:1, 0:1])
                out.extend(toks)
                prev = toks[-1]
            st["lnlast"] = prev
            return out

        def emit_qkv(l, w, w0):
            wt = st["wtok"]
            for which in range(2):
                tabsrc = (Cq, Sq) if which == 0 else (Ck, Sk)
                t_t1 = k.dma(SY, attf[:, 0:512], tabsrc[0][:, w0:w0 + 512],
                             wait=[st["tabq"], st["attread"]], inc="tb")
                t_t2 = k.dma(SY, attf[:, 512:1024], tabsrc[1][:, w0:w0 + 512],
                             wait=[], inc="tb")
                qp_toks = []
                sq_toks = []
                for oc in range(4):
                    col0 = which * 512 + oc * 128
                    tm = None
                    for c in range(DC):
                        tm = k.op(T, lambda e, c=c, oc=oc, col0=col0, l=l, w0=w0:
                                  e.matmul(P[4 + oc][:, :],
                                           wqkvb[:, c, col0:col0 + 128],
                                           hget(l, c, w0, c >= 4),
                                           start=(c == 0), stop=(c == DC - 1)),
                                  wait=[st["h0_tok"], wt, st["sig"],
                                        st["attread"], st["evac"][7],
                                        st.get("geread"), st["h1_tok"],
                                        st.get("yread"), st["xmid_tok"],
                                        st["hln_tok"]] if c == 0 else [],
                                  inc="mm" if c == DC - 1 else None)
                    qp_toks.append(tm)
                    t_s = k.op(S, lambda e, oc=oc: e.square(out=t1s[:],
                                                            in_=P[4 + oc][:, :]),
                               wait=[tm], inc=None)
                    t_ss = k.op(T, lambda e, oc=oc: e.matmul(
                        P[0][0:8, :], E8s[:, oc, :], t1s[:],
                        start=(oc == 0), stop=(oc == 3)),
                        wait=[t_s, t_con, st["lnlast"]],
                        inc="mm" if oc == 3 else None)
                    sq_toks.append(t_s)
                t_ssl = t_ss
                t_r1 = k.op(S, lambda e: e.sqrt(out=rss[:], in_=P[0][0:8, :]),
                            wait=[t_ssl])
                t_r2 = k.op(V, lambda e: e.tensor_scalar_max(rss[:], rss[:], 1e-12),
                            wait=[t_r1])
                t_r3 = k.op(V, lambda e: e.reciprocal(rss[:], rss[:]), wait=[t_r2])
                qcolix = which if l == 0 else 2 + which
                prev_m2 = None
                for oc in range(4):
                    t_rb = k.op(T, lambda e, oc=oc: e.matmul(
                        P[1][:, :], E8Ts[0:8, oc, :], rss[:],
                        start=True, stop=True),
                        wait=[t_r3, st["lnlast"], prev_m2], inc="mm")
                    t_rc = k.op(S, lambda e: e.copy(out=rbs[:], in_=P[1][:, :]),
                                wait=[t_rb])
                    t_qh = k.op(V, lambda e, oc=oc, qcolix=qcolix:
                                e.scalar_tensor_tensor(
                                    out=qh[:], in0=P[4 + oc][:, :],
                                    scalar=qkc[:, qcolix:qcolix + 1], in1=rbs[:],
                                    op0=ALU.mult, op1=ALU.mult),
                                wait=[t_rc, st["ystore"]], inc=None)
                    t_pw = k.op(T, lambda e: e.matmul(
                        P[1][:, :], psws[:], qh[:], start=True, stop=True),
                        wait=[t_qh, t_rc], inc="mm")
                    t_m1 = k.op(V, lambda e: e.tensor_mul(t1s[:], qh[:],
                                                          attf[:, 0:512]),
                                wait=[t_t1, t_t2, t_pw], inc=None)
                    t_m2 = k.op(V, lambda e: e.tensor_mul(gg[:], P[1][:, :],
                                                          attf[:, 512:1024]),
                                wait=[t_pw, st["b2tok"]], inc=None)
                    prev_m2 = t_m2
                    if which == 0:
                        t_o = k.op(V, lambda e, oc=oc: e.tensor_add(
                            qrot[:, oc, :], t1s[:], gg[:]), wait=[], inc=None)
                        st["qr"] = t_o
                    else:
                        half = slice((w % 2) * 512, (w % 2) * 512 + 512)
                        t_o = k.op(V, lambda e, oc=oc, half=half: e.tensor_add(
                            krot[:, oc, half], t1s[:], gg[:]), wait=[], inc=None)
                        st["kr"][(w, oc)] = t_o
                st["tabq"] = t_o
                st["sig"] = t_o
            for j in range(4):
                slot = (w % 2) * 4 + j
                tsl = slice(j * 128, j * 128 + 128)
                ph = P[4 + j]
                tm = None
                for c in range(DC):
                    tm = k.op(T, lambda e, c=c, tsl=tsl, l=l, w0=w0, ph=ph:
                              e.matmul(ph[:, :],
                                       hget(l, c, w0, c >= 4)[:, tsl],
                                       wqkvb[:, c, 1024:1536],
                                       start=(c == 0), stop=(c == DC - 1)),
                              wait=[st["sig"], st["h0_tok"]] if c == 0 else [],
                              inc="mm" if c == DC - 1 else None)
                t_v = k.op(S, lambda e, slot=slot, ph=ph: e.copy(
                    out=vtm[:, slot, :], in_=ph[:, :]), wait=[tm])
                st["vt"][(w, j)] = t_v
                st["sig"] = t_v
            st["qkvmm"] = tm

        def emit_attn(l, w, w0, first_scored):
            use4 = (w == 4)
            for h in range(8):
                hc, hp = h // 2, 64 * (h % 2)
                for qc in range(4):
                    qsl = slice(qc * 128, qc * 128 + 128)
                    lo = 128 * qc
                    wp = 512 - lo
                    wc = 640 - wp
                    pbase = ((w - 1) % 2) * 512
                    cbase = (w % 2) * 512
                    t_s1 = k.op(T, lambda e, hp=hp, hc=hc, qsl=qsl, lo=lo,
                                wp=wp, pbase=pbase:
                                e.matmul(P[6][:, 0:wp],
                                         qrot[hp:hp + 64, hc, qsl],
                                         krot[hp:hp + 64, hc,
                                              pbase + lo:pbase + 512],
                                         start=True, stop=True),
                                wait=[st["qr"], st["kr"].get((w - 1, hc), None),
                                      st["kr"].get((w, hc), None),
                                      st["attread"], st["sig"]], inc="mm")
                    t_s2 = k.op(T, lambda e, hp=hp, hc=hc, qsl=qsl, wc=wc,
                                cbase=cbase:
                                e.matmul(P[7][:, 0:wc],
                                         qrot[hp:hp + 64, hc, qsl],
                                         krot[hp:hp + 64, hc, cbase:cbase + wc],
                                         start=True, stop=True),
                                wait=[], inc="mm")
                    tv1 = k.op(V, lambda e, qc=qc, lo=lo, wp=wp:
                               e.scalar_tensor_tensor(
                        out=attf[:, 0:wp], in0=mbs[:, qc, lo:512], scalar=1.0,
                        in1=P[6][:, 0:wp], op0=ALU.mult, op1=ALU.add),
                        wait=[t_s1, t_con])
                    if use4:
                        tv1 = k.op(V, lambda e, wp=wp: e.tensor_scalar(
                            out=attf[:, 0:wp], in0=attf[:, 0:wp],
                            scalar1=hss[:, 1:2], scalar2=None, op0=ALU.add),
                            wait=[])
                    tv2 = k.op(V, lambda e, qc=qc, wp=wp, wc=wc:
                               e.scalar_tensor_tensor(
                        out=attf[:, wp:640], in0=mbs[:, qc, 512:512 + wc],
                        scalar=1.0, in1=P[7][:, 0:wc],
                        op0=ALU.mult, op1=ALU.add), wait=[t_s2])
                    st["attread"] = tv2
                    tv3 = k.op(V, lambda e: e.reduce_max(
                        mcol[:, 0:1], attf[:, 0:640], axis=AX.X, negate=True),
                        wait=[])
                    ts1 = k.op(S, lambda e: e.activation(
                        out=attf[:, 0:640], in_=attf[:, 0:640], func=AF.Exp,
                        bias=mcol[:, 0:1]), wait=[tv3])
                    tv4 = k.op(V, lambda e: e.reduce_sum(
                        mcol[:, 1:2], attf[:, 0:640], axis=AX.X), wait=[ts1])
                    tv5 = k.op(V, lambda e: e.reciprocal(
                        mcol[:, 2:3], mcol[:, 1:2]), wait=[])
                    tv6 = k.op(V, lambda e: e.tensor_scalar(
                        out=attf[:, 0:640], in0=attf[:, 0:640],
                        scalar1=mcol[:, 2:3],
                        scalar2=None, op0=ALU.mult), wait=[tv5])
                    tav = None
                    prev_cp = None
                    t_tp = None
                    for k2 in range(5):
                        g = qc + k2
                        slot = (((w - 1) % 2) * 4 + g) if g < 4 else \
                               ((w % 2) * 4 + (g - 4))
                        t_tp = k.op(T, lambda e, k2=k2: e.transpose(
                            P[5][:, 0:128], attf[:, k2 * 128:k2 * 128 + 128],
                            eyes[:]), wait=[tv6, t_con, prev_cp], inc="mm")
                        t_cp = k.op(S, lambda e: e.copy(out=attT[:],
                                                        in_=P[5][:, 0:128]),
                                    wait=[t_tp, tav])
                        tav = k.op(T, lambda e, k2=k2, slot=slot, h=h:
                                   e.matmul(P[4][0:64, 0:128],
                                            vtm[:, slot, h * 64:h * 64 + 64],
                                            attT[:],
                                            start=(k2 == 0), stop=(k2 == 4)),
                                   wait=[t_cp, st["vt"].get((w - 1, slot % 4), None)
                                         if g < 4 else
                                         st["vt"].get((w, slot % 4), None)],
                                   inc="mm")
                        prev_cp = t_cp
                    t_ob = k.op(S, lambda e, hp=hp, hc=hc, qsl=qsl: e.copy(
                        out=o_bf[hp:hp + 64, hc, qsl], in_=P[4][0:64, 0:128]),
                        wait=[tav])
                    st["o_tok"] = t_ob
                    st["sig"] = t_ob
                    st["attread"] = t_tp

        def emit_wo(l, w, w0):
            prev_bank = {0: None, 1: None}
            for dc in range(DC):
                ph = P[6] if dc % 2 == 0 else P[7]
                tm = None
                for c in range(4):
                    tm = k.op(T, lambda e, c=c, dc=dc, ph=ph: e.matmul(
                        ph[:, :], wob[:, c, dc * 128:dc * 128 + 128],
                        o_bf[:, c, :], start=(c == 0), stop=(c == 3)),
                        wait=[st["o_tok"], st["wtok"], st["evac"][dc],
                              prev_bank[dc % 2]] if c == 0 else [],
                        inc="mm" if c == 3 else None)
                t_e = k.op(V, lambda e, dc=dc, ph=ph, l=l, w0=w0: e.tensor_add(
                    xmid[:, dc, 1:513], hget(l, dc, w0, False), ph[:, :]),
                    wait=[tm, st["xmid_tok"], st["hln_tok"]])
                st["evac"][dc] = t_e
                prev_bank[dc % 2] = t_e
            st["womm"] = tm
            st["xmid_w"] = st["evac"][:]

        def emit_ffn(l, w, w0, sink):
            lntoks = []
            prevt = None
            for sub in range(4):
                o = sub * 128
                toks = emit_ln_fm(
                    k, lambda c, o=o: xmid[:, c, (1 + o if c < 4 else o):
                                           (1 + o if c < 4 else o) + 128], 128,
                    lambda c, o=o: hln[:, c, o:o + 128],
                    (P[0], P[1]), (P[2], P[3]), sq, rows,
                    ones_col[:], ones_row1[:],
                    [st["evac"][7], t_con, st["ffnmm"], prevt, st["lnlast"]],
                    tag="ln", eps_col=epscol[0:1, 0:1],
                    affine=lambda c: (lgb[:, c, l:l + 1], lgb[:, c, 2 + l:3 + l]))
                lntoks.extend(toks)
                prevt = toks[-1]
            st["hln_tok"] = lntoks[-1]
            st["lnlast"] = lntoks[-1]
            prev_b1 = st["b1tok"]
            mm_par = {0: None, 1: None}
            for hc in range(22):
                sz = 128 if hc < 21 else 42
                bi = hc % 2
                t_ca = k.dma(SY, w1ab[bi][:, :, 0:sz],
                             w1_b[l].rearrange("(c p) d -> p c d", p=128)
                             [:, :, hc * 128:hc * 128 + sz],
                             wait=[mm_par[bi], ctok], inc="wl%d" % bi)
                t_cg = k.dma(SY, w1gb[bi][:, :, 0:sz],
                             w1_b[l].rearrange("(c p) d -> p c d", p=128)
                             [:, :, 2730 + hc * 128:2730 + hc * 128 + sz],
                             wait=[mm_par[bi], ctok], inc="wg%d" % bi)
                t_b = k.dma(SY, rows[0:1, 0:sz],
                            b1r[l:l + 1, hc * 128:hc * 128 + sz],
                            wait=[prev_b1, st["lnlast"], st["b2tok"]], inc="bl")
                t_b2 = k.dma(SY, rows[0:1, 128:128 + sz],
                             b1r[l:l + 1, 2730 + hc * 128:2730 + hc * 128 + sz],
                             wait=[], inc="bl")
                t_bc = k.op(S, lambda e: e.copy(out=b1sb[:], in_=rows[0:1, 0:256]),
                            wait=[t_b, t_b2])
                prev_b1 = t_bc
                st["b1tok"] = t_bc
                tma = tmg = None
                for c in range(DC):
                    tma = k.op(T, lambda e, c=c, sz=sz, bi=bi: e.matmul(
                        P[4][0:sz, :], w1ab[bi][:, c, 0:sz], hln[:, c, :],
                        start=(c == 0), stop=False),
                        wait=[t_ca, st["hln_tok"], st["geread"]
                              if "geread" in st else None], inc=None)
                tma = k.op(T, lambda e, sz=sz: e.matmul(
                    P[4][0:sz, :], b1sb[0:1, 0:sz], ones_rbf[:],
                    start=False, stop=True), wait=[t_bc], inc="mm")
                for c in range(DC):
                    tmg = k.op(T, lambda e, c=c, sz=sz, bi=bi: e.matmul(
                        P[6][0:sz, :], w1gb[bi][:, c, 0:sz], hln[:, c, :],
                        start=(c == 0), stop=False),
                        wait=[t_cg] if c == 0 else [], inc=None)
                tmg = k.op(T, lambda e, sz=sz: e.matmul(
                    P[6][0:sz, :], b1sb[0:1, 128:128 + sz], ones_rbf[:],
                    start=False, stop=True), wait=[t_bc], inc="mm")
                mm_par[bi] = tmg
                t_gl = k.op(S, lambda e, sz=sz: e.activation(
                    out=gg[0:sz, :], in_=P[6][0:sz, :], func=AF.Gelu),
                    wait=[tmg])
                t_eo = k.op(V, lambda e, hc=hc, sz=sz: e.tensor_mul(
                    eo[0:sz, hc, :], P[4][0:sz, :], gg[0:sz, :]),
                    wait=[t_gl, tma])
                st["eo_toks"][hc] = t_eo
                st["geread"] = t_eo
            w2mms = [None] * 8
            w2_par = {0: None, 1: None}
            for hc in range(22):
                sz = 128 if hc < 21 else 42
                for half in range(2):
                    dcl = slice(half * 512, half * 512 + 512)
                    bi = (hc * 2 + half) % 2
                    t_c2 = k.dma(SY, w2sb[bi][0:sz, :],
                                 w2_b[l][hc * 128:hc * 128 + sz, dcl],
                                 wait=[w2_par[bi], ctok],
                                 inc="wl0" if bi == 0 else "wl1")
                    for dc4 in range(4):
                        dc = half * 4 + dc4
                        w2mms[dc] = k.op(T, lambda e, hc=hc, dc=dc, dc4=dc4,
                                         sz=sz, bi=bi: e.matmul(
                            P[dc][:, :], w2sb[bi][0:sz, dc4 * 128:dc4 * 128 + 128],
                            eo[0:sz, hc, :], start=(hc == 0), stop=False),
                            wait=[t_c2, st["eo_toks"][hc]] +
                                 ([st["yread"] if "yread" in st else None,
                                   st["h1_tok"], st["geread"]]
                                  if hc == 0 else []),
                            inc="mm" if dc4 == 3 else None)
                        if dc4 == 3:
                            w2_par[bi] = w2mms[dc]
            for dc in range(DC):
                w2mms[dc] = k.op(T, lambda e, dc=dc: e.matmul(
                    P[dc][:, :], b2b[0:1, dc * 128:dc * 128 + 128],
                    ones_rbf[:], start=False, stop=True),
                    wait=[st["b2tok"]], inc="mm")
            st["ffnmm"] = w2mms[7]
            if sink == "h1":
                for dc in range(DC):
                    if w == 1:
                        t_h = k.op(V, lambda e, dc=dc: e.tensor_add(
                            h1buf[:, dc, 0:1], xmid[:, dc, 512:513],
                            P[dc][:, 511:512]), wait=[w2mms[dc]])
                    else:
                        b = 512 * (w - 2) + 1
                        t_h = k.op(V, lambda e, dc=dc, b=b: e.tensor_add(
                            h1buf[:, dc, b:b + 512], xmid[:, dc, 1:513],
                            P[dc][:, :]), wait=[w2mms[dc]])
                    st["h1_tok"] = t_h
                st["w2mm"] = w2mms[7]
            else:
                for dc in range(DC):
                    t_t = k.op(V, lambda e, dc=dc: e.tensor_add(
                        t1s[:], P[dc][:, :], xmid[:, dc, 1:513]),
                        wait=[w2mms[dc], st["ystore"]])
                    t_x = k.dma(SY, qh[:], fm_view(xc)[:, dc, w0:w0 + 512],
                                wait=[st.get("yread"), st["ystore"]],
                                inc="ld")
                    t_y = k.op(V, lambda e: e.scalar_tensor_tensor(
                        out=qh[:], in0=t1s[:], scalar=0.5, in1=qh[:],
                        op0=ALU.mult, op1=ALU.add), wait=[t_x, t_t])
                    st["yread"] = t_y
                    st["ystore"] = k.dma(
                        SY, fm_view(y)[:, dc, w0 - 2049:w0 - 2049 + 512], qh[:],
                        wait=[t_y], inc="st")
                st["w2mm"] = w2mms[7]

        def emit_layer(l, h_src, sink):
            wfirst = 0 if l == 0 else 2
            sfirst = wfirst + 1
            ffirst = sfirst if sink == "h1" else sfirst + 1
            st["wtok"] = load_layer_weights(l, [st["qkvmm"], st["womm"]])
            tb2 = None
            for hf in range(2):
                t_l2r = k.dma(SY, gg[0:1, 0:512],
                              lrows[l, 2:3].rearrange("(o a) d -> o (a d)", o=1)
                              [0:1, hf * 512:hf * 512 + 512],
                              wait=[st["ffnmm"], st["b1tok"], st["lnlast"],
                                    st["qkvmm"], st["sig"], tb2], inc="bl")
                tb2 = k.op(S, lambda e, hf=hf: e.copy(
                    out=b2b[0:1, hf * 512:hf * 512 + 512], in_=gg[0:1, 0:512]),
                    wait=[t_l2r])
            st["b2tok"] = tb2
            t_z = k.op(G, lambda e: e.memset(xmid[:, :, 0:1], 0.0),
                       wait=[st["ffnmm"], st["hln_tok"], t_con,
                             st["xmid_tok"], st["evac"][7]])
            st["xmid_tok"] = t_z
            if l == 0:
                t_z0 = k.op(G, lambda e: e.memset(h0r[:, :, 0:1], 0.0),
                            wait=[st["qkvmm"], t_con])
                st["roll"] = t_z0
            for w in range(wfirst, 8):
                w0 = 1 + 512 * w
                if h_src == "x":
                    toks = emit_h0_from_x(w0)
                    st["h0_tok"] = toks[-1]
                else:
                    st["h0_tok"] = st["h1_tok"]
                emit_qkv(l, w, w0)
                if w >= sfirst:
                    emit_attn(l, w, w0, sfirst)
                    emit_wo(l, w, w0)
                    if w >= ffirst:
                        emit_ffn(l, w, w0, sink)
                    t_r = k.op(V, lambda e: e.tensor_copy(
                        xmid[:, :, 0:1], xmid[:, :, 512:513]),
                        wait=[st["evac"][7], st["hln_tok"], st["ffnmm"]])
                    if w + 1 == 4:
                        t_r = k.op(V, lambda e: e.tensor_scalar(
                            out=xmid[:, :, 0:1], in0=xmid[:, :, 0:1],
                            scalar1=hss[:, 0:1], scalar2=None, op0=ALU.mult),
                            wait=[t_r])
                    st["xmid_tok"] = t_r
                if h_src == "x":
                    st["roll"] = k.op(V, lambda e: e.tensor_copy(
                        h0r[:, :, 0:1], h0r[:, :, 512:513]),
                        wait=[st["qkvmm"], st["h0_tok"]])

        emit_layer(0, "x", "h1")
        emit_layer(1, "h1", "y")


def build_l0ffn():
    nc = bass.Bass("TRN2", target_bir_lowering=False, debug=False)
    xc = nc.dram_tensor("xc", [DM, 4097], F32, kind="ExternalInput").ap()
    wqkv = nc.dram_tensor("wqkv", [2, DM, 1536], F32, kind="ExternalInput").ap()
    wo = nc.dram_tensor("wo", [2, 512, DM], F32, kind="ExternalInput").ap()
    w1 = nc.dram_tensor("w1", [2, DM, 5460], F32, kind="ExternalInput").ap()
    w2 = nc.dram_tensor("w2", [2, 2730, DM], F32, kind="ExternalInput").ap()
    lrows = nc.dram_tensor("lrows", [2, 3, DM], F32, kind="ExternalInput").ap()
    b1r = nc.dram_tensor("b1r", [2, 5460], F32, kind="ExternalInput").ap()
    qkcol = nc.dram_tensor("qkcol", [2, 2, 128], F32, kind="ExternalInput").ap()
    lgbi = nc.dram_tensor("lgbi", [128, DC, 4], F32, kind="ExternalInput").ap()
    Cq = nc.dram_tensor("Cq", [128, 4097], F32, kind="ExternalInput").ap()
    Sq = nc.dram_tensor("Sq", [128, 4097], F32, kind="ExternalInput").ap()
    Ck = nc.dram_tensor("Ck", [128, 4097], F32, kind="ExternalInput").ap()
    Sk = nc.dram_tensor("Sk", [128, 4097], F32, kind="ExternalInput").ap()
    Psw = nc.dram_tensor("Psw", [128, 128], F32, kind="ExternalInput").ap()
    E8i = nc.dram_tensor("E8i", [128, 4, 8], F32, kind="ExternalInput").ap()
    E8Ti = nc.dram_tensor("E8Ti", [8, 4, 128], F32, kind="ExternalInput").ap()
    eyef = nc.dram_tensor("eyef", [128, 128], F32, kind="ExternalInput").ap()
    mbk = nc.dram_tensor("mbk", [128, 4, 1024], BF16, kind="ExternalInput").ap()
    hscol = nc.dram_tensor("hscol", [128, 2], F32, kind="ExternalInput").ap()
    y = nc.dram_tensor("y", [DM, OWN], F32, kind="ExternalOutput").ap()
    wqkv_b = nc.dram_tensor("wqkv_b", [2, DM, 1536], BF16).ap()
    wo_b = nc.dram_tensor("wo_b", [2, 512, DM], BF16).ap()
    w1_b = nc.dram_tensor("w1_b", [2, DM, 5460], BF16).ap()
    w2_b = nc.dram_tensor("w2_b", [2, 2730, DM], BF16).ap()
    stack = ExitStack()
    with stack:
        k = Sched(nc, stack)
        with ExitStack() as cs:
            ctok = emit_bf16_cache(nc, k, cs, [
                (wqkv, wqkv_b, 2 * DM * 1536), (wo, wo_b, 2 * 512 * DM),
                (w1, w1_b, 2 * DM * 5460), (w2, w2_b, 2 * 2730 * DM)], tag="c")
        with ExitStack() as ps:
            emit_l0ffn(nc, k, ps, xc, wqkv, wo, w1, w2, lrows, b1r, qkcol, lgbi,
                       Cq, Sq, Ck, Sk, Psw, E8i, E8Ti, eyef, mbk, hscol, y,
                       wqkv_b=wqkv_b, wo_b=wo_b, w1_b=w1_b, w2_b=w2_b,
                       ctok=ctok)
        k.raw_wait("sync", "st", k.cnt["st"])
        k.finalize()
    return nc


def _l0ffn_host_consts(core):
    r, h = core // 2, core % 2
    s = h * OWN
    g = np.arange(-2049, 2048, dtype=np.float64) + s
    gc = np.clip(g, 0, None)
    half = np.arange(0, 64, 2, dtype=np.float64)
    inv_freq = 1.0 / 10000.0 ** (half / 64.0)
    freqs = gc[:, None] * inv_freq[None, :]
    base = (half + 0.4 * 64.0) / (1.4 * 64.0)
    power = (gc - 4096 // 2) / float(512 // 2)
    scale = base[None, :] ** power[:, None]
    cos = np.cos(freqs)
    sin = np.sin(freqs)
    cq = (cos * scale).astype(np.float32)
    sq_ = (sin * scale).astype(np.float32)
    ck = (cos / scale).astype(np.float32)
    sk = (sin / scale).astype(np.float32)

    def tab(a):
        full = np.concatenate([a, a], axis=1)
        return np.ascontiguousarray(np.tile(full.T, (2, 1)).astype(np.float32))

    Psw = np.zeros((128, 128), np.float32)
    for m in range(128):
        if m % 64 < 32:
            Psw[m + 32, m] = -1.0
        else:
            Psw[m - 32, m] = 1.0
    E8 = np.zeros((128, 4, 8), np.float32)
    E8T = np.zeros((8, 4, 128), np.float32)
    for oc in range(4):
        E8[0:64, oc, 2 * oc] = 1.0
        E8[64:128, oc, 2 * oc + 1] = 1.0
        E8T[2 * oc, oc, 0:64] = 1.0
        E8T[2 * oc + 1, oc, 64:128] = 1.0
    mb = np.full((128, 4, 1024), -30000.0, np.float32)
    for j in range(4):
        for p in range(128):
            a = 128 * j + p
            mb[p, j, a:a + 513] = 0.0
    hs = np.zeros((128, 2), np.float32)
    hs[:, 0] = 1.0 if h == 1 else 0.0
    hs[:, 1] = 0.0 if h == 1 else -30000.0
    return {
        "Cq": tab(cq), "Sq": tab(sq_), "Ck": tab(ck), "Sk": tab(sk),
        "Psw": Psw, "E8i": E8, "E8Ti": E8T, "eyef": _EYE,
        "mbk": mb.astype(ml_dtypes.bfloat16),
        "hscol": hs,
    }


def run_l0ffn_launch(nc_f, x, W):
    wqkv = np.asarray(W["L0_ffn_wqkv"])
    wov = np.asarray(W["L0_ffn_wo"])
    w1v = np.asarray(W["L0_ffn_w1"])
    w2v = np.asarray(W["L0_ffn_w2"])
    lrows = np.stack([np.asarray(W["L0_ffn_ln_g"]), np.asarray(W["L0_ffn_ln_b"]),
                      np.asarray(W["L0_ffn_b2"])], axis=1).astype(np.float32)
    lgbi = np.stack([np.asarray(W["L0_ffn_ln_g"])[0].reshape(DC, 128).T,
                     np.asarray(W["L0_ffn_ln_g"])[1].reshape(DC, 128).T,
                     np.asarray(W["L0_ffn_ln_b"])[0].reshape(DC, 128).T,
                     np.asarray(W["L0_ffn_ln_b"])[1].reshape(DC, 128).T],
                    axis=2).astype(np.float32)
    b1v = np.asarray(W["L0_ffn_b1"]).astype(np.float32)
    qs = np.asarray(W["L0_ffn_q_scale"])
    ks = np.asarray(W["L0_ffn_k_scale"])
    qkcol = np.stack([np.stack([np.tile(qs[l], 2), np.tile(ks[l], 2)])
                      for l in range(2)]).astype(np.float32)
    base = {"wqkv": wqkv, "wo": wov, "w1": w1v, "w2": w2v,
            "lrows": lrows, "b1r": b1v, "qkcol": qkcol,
            "lgbi": np.ascontiguousarray(lgbi)}
    in_maps = []
    for c in range(NCORES):
        r, h = c // 2, c % 2
        s = h * OWN
        xrT = x[r].T
        ctx = np.zeros((DM, 4097), np.float32)
        lo = s - 2049
        src_lo = max(lo, 0)
        ctx[:, src_lo - lo:] = xrT[:, src_lo:s + OWN]
        m = dict(base, xc=ctx)
        m.update(_l0ffn_host_consts(c))
        in_maps.append(m)
    res = run_bass_kernel_spmd(nc_f, in_maps, core_ids=list(range(NCORES)))
    return _assemble_fm([np.asarray(res.results[c]["y"]) for c in range(NCORES)])



def emit_bf16_cache(nc, k, stack, pairs, tag="c", waitfor=None):
    S, V, G, SY = "scalar", "vector", "gpsimd", "sync"
    sfx = _uniq()
    CW = 4096
    bf = [stack.enter_context(nc.sbuf_tensor(f"cbf{i}{sfx}", [128, CW], F32))
          for i in range(2)]
    bb = [stack.enter_context(nc.sbuf_tensor(f"cbb{i}{sfx}", [128, CW], BF16))
          for i in range(2)]
    for s in [tag + "l0", tag + "l1", tag + "s0", tag + "s1"]:
        k.sem(s, dma=True)
    k.sem(tag + "fin")
    last = {0: waitfor, 1: waitfor}
    tok = None
    i = 0
    for (srcap, dstap, ne) in pairs:
        nx = ne // 128
        for off in range(0, nx, CW):
            w = min(CW, nx - off)
            bi = i % 2
            t_l = k.dma(SY, bf[bi][:, 0:w],
                        _flat2d(srcap, ne)[:, off:off + w],
                        wait=[last[bi]], inc=tag + "l%d" % bi)
            eng = G if i % 2 == 0 else V
            t_c = k.op(eng, lambda e, bi=bi, w=w: e.tensor_copy(
                bb[bi][:, 0:w], bf[bi][:, 0:w]), wait=[t_l, waitfor])
            tok = k.dma(SY, _flat2d(dstap, ne)[:, off:off + w], bb[bi][:, 0:w],
                        wait=[t_c], inc=tag + "s%d" % bi)
            last[bi] = tok
            i += 1
    return k.op(SY, lambda e: e.nop(), wait=[last[0], last[1]],
                inc=tag + "fin")


def _flat2d(ap, ne):
    spec = " ".join("d%d" % i for i in range(len(ap.shape)))
    flat = ap.rearrange("%s -> (%s)" % (spec, spec))
    return flat.rearrange("(p x) -> p x", p=128)


RG_PAIRS = [[0, 1], [2, 3], [4, 5], [6, 7]]


def build_fused(rg=None):
    rg = rg or RG_PAIRS
    nc = bass.Bass("TRN2", target_bir_lowering=False, debug=False)
    dt = nc.dram_tensor
    xc0 = dt("xc0", [DM, 4097], F32, kind="ExternalInput").ap()
    f_wqkv = dt("f_wqkv", [2, DM, 1536], F32, kind="ExternalInput").ap()
    f_wo = dt("f_wo", [2, 512, DM], F32, kind="ExternalInput").ap()
    f_w1 = dt("f_w1", [2, DM, 5460], F32, kind="ExternalInput").ap()
    f_w2 = dt("f_w2", [2, 2730, DM], F32, kind="ExternalInput").ap()
    f_lrows = dt("f_lrows", [2, 3, DM], F32, kind="ExternalInput").ap()
    f_b1r = dt("f_b1r", [2, 5460], F32, kind="ExternalInput").ap()
    f_qkcol = dt("f_qkcol", [2, 2, 128], F32, kind="ExternalInput").ap()
    f_lgbi = dt("f_lgbi", [128, DC, 4], F32, kind="ExternalInput").ap()
    f_Cq = dt("f_Cq", [128, 4097], F32, kind="ExternalInput").ap()
    f_Sq = dt("f_Sq", [128, 4097], F32, kind="ExternalInput").ap()
    f_Ck = dt("f_Ck", [128, 4097], F32, kind="ExternalInput").ap()
    f_Sk = dt("f_Sk", [128, 4097], F32, kind="ExternalInput").ap()
    f_Psw = dt("f_Psw", [128, 128], F32, kind="ExternalInput").ap()
    f_E8i = dt("f_E8i", [128, 4, 8], F32, kind="ExternalInput").ap()
    f_E8Ti = dt("f_E8Ti", [8, 4, 128], F32, kind="ExternalInput").ap()
    f_eyef = dt("f_eyef", [128, 128], F32, kind="ExternalInput").ap()
    f_mbk = dt("f_mbk", [128, 4, 1024], BF16, kind="ExternalInput").ap()
    f_hscol = dt("f_hscol", [128, 2], F32, kind="ExternalInput").ap()
    a_Wq = dt("a_Wq", [DM, DM], F32, kind="ExternalInput").ap()
    a_Wk = dt("a_Wk", [DM, DM], F32, kind="ExternalInput").ap()
    a_Wv = dt("a_Wv", [DM, DM], F32, kind="ExternalInput").ap()
    a_Wo = dt("a_Wo", [DM, DM], F32, kind="ExternalInput").ap()
    a_brows = dt("a_brows", [4, DM], F32, kind="ExternalInput").ap()
    a_tcol = dt("a_tcol", [128, 16], F32, kind="ExternalInput").ap()
    m_rwT = dt("m_rwT", [DM, 8], F32, kind="ExternalInput").ap()
    m_eb = dt("m_eb", [8], F32, kind="ExternalInput").ap()
    m_w1 = dt("m_w1", [8, DM, 2048], F32, kind="ExternalInput").ap()
    m_w3 = dt("m_w3", [8, DM, 2048], F32, kind="ExternalInput").ap()
    m_w2 = dt("m_w2", [8, 2048, DM], F32, kind="ExternalInput").ap()
    m_sw1 = dt("m_sw1", [DM, 2048], F32, kind="ExternalInput").ap()
    m_sw3 = dt("m_sw3", [DM, 2048], F32, kind="ExternalInput").ap()
    m_sw2 = dt("m_sw2", [2048, DM], F32, kind="ExternalInput").ap()
    t_Wq = dt("t_Wq", [DM, DM], F32, kind="ExternalInput").ap()
    t_Wk = dt("t_Wk", [DM, DM], F32, kind="ExternalInput").ap()
    t_Wv = dt("t_Wv", [DM, DM], F32, kind="ExternalInput").ap()
    t_Wo = dt("t_Wo", [DM, DM], F32, kind="ExternalInput").ap()
    t_vecs = dt("t_vecs", [13, DM], F32, kind="ExternalInput").ap()
    yout = dt("yout", [DM, OWN], F32, kind="ExternalOutput").ap()
    xb = [dt(f"xb{i}", [DM, OWN], F32).ap() for i in range(1, 6)]
    ag2 = dt("ag2", [2 * DM, OWN], F32).ap()
    ag4 = dt("ag4", [2 * DM, OWN], F32).ap()
    ctxb = dt("ctxb", [DM, 4097], F32).ap()
    cwscr = dt("cwscr", [8, OWN], F32).ap()

    fc_wqkv = dt("fc_wqkv", [2, DM, 1536], BF16).ap()
    fc_wo = dt("fc_wo", [2, 512, DM], BF16).ap()
    fc_w1 = dt("fc_w1", [2, DM, 5460], BF16).ap()
    fc_w2 = dt("fc_w2", [2, 2730, DM], BF16).ap()
    mc_w1 = dt("mc_w1", [8, DM, 2048], BF16).ap()
    mc_w3 = dt("mc_w3", [8, DM, 2048], BF16).ap()
    mc_w2 = dt("mc_w2", [8, 2048, DM], BF16).ap()
    mc_sw1 = dt("mc_sw1", [DM, 2048], BF16).ap()
    mc_sw3 = dt("mc_sw3", [DM, 2048], BF16).ap()
    mc_sw2 = dt("mc_sw2", [2048, DM], BF16).ap()
    stack = ExitStack()
    with stack:
        k = Sched(nc, stack)
        SY, V, G = "sync", "vector", "gpsimd"
        k.sem("ccA")
        k.sem("ccB")
        with ExitStack() as cs:
            ctok = emit_bf16_cache(nc, k, cs, [
                (f_wqkv, fc_wqkv, 2 * DM * 1536), (f_wo, fc_wo, 2 * 512 * DM),
                (f_w1, fc_w1, 2 * DM * 5460), (f_w2, fc_w2, 2 * 2730 * DM)],
                tag="c")
        with ExitStack() as cs:
            mtok = emit_bf16_cache(nc, k, cs, [
                (m_w1, mc_w1, 8 * DM * 2048), (m_w3, mc_w3, 8 * DM * 2048),
                (m_w2, mc_w2, 8 * 2048 * DM), (m_sw1, mc_sw1, DM * 2048),
                (m_sw3, mc_sw3, DM * 2048), (m_sw2, mc_sw2, 2048 * DM)],
                tag="m", waitfor=ctok)

        def mk_rowread(ag_ap, cctok):
            lo = ag_ap[0:DM, :].rearrange("(c p) t -> p c t", p=128)
            hi = ag_ap[DM:2 * DM, :].rearrange("(c p) t -> p c t", p=128)

            def rr(dst, t0, n, wait, slow=False):
                t1 = min(n, max(0, OWN - t0))
                tok = None
                if t1 > 0:
                    tok = k.dma(SY, dst[:, :, 0:t1], lo[:, :, t0:t0 + t1],
                                wait=list(wait) + [cctok], inc="ld",
                                slow=slow or t1 < 8)
                if t1 < n:
                    tok = k.dma(SY, dst[:, :, t1:n],
                                hi[:, :, t0 + t1 - OWN:t0 + n - OWN],
                                wait=list(wait) + [cctok], inc="ld",
                                slow=slow or (n - t1) < 8)
                return tok
            return rr

        with ExitStack() as ps:
            emit_l0ffn(nc, k, ps, xc0, f_wqkv, f_wo, f_w1, f_w2, f_lrows,
                       f_b1r, f_qkcol, f_lgbi, f_Cq, f_Sq, f_Ck, f_Sk, f_Psw,
                       f_E8i, f_E8Ti, f_eyef, f_mbk, f_hscol, xb[0],
                       wqkv_b=fc_wqkv, wo_b=fc_wo, w1_b=fc_w1, w2_b=fc_w2,
                       ctok=ctok)
        k.barrier()
        tcc = k.op(G, lambda e: e.collective_compute(
            "AllGather", ALU.bypass, replica_groups=rg,
            ins=[xb[0].opt()], outs=[ag2.opt()]), inc="ccA")
        with ExitStack() as ps:
            emit_l0attn(nc, k, ps, xb[0], xb[0], a_Wq, a_Wk, a_Wv, a_Wo,
                        a_brows, a_tcol, f_eyef, xb[1],
                        row_read=mk_rowread(ag2, tcc))
        k.barrier()
        with ExitStack() as ps:
            cb = ps.enter_context(nc.sbuf_tensor("cb", [128, DC, 512], F32))
            hs2 = ps.enter_context(nc.sbuf_tensor("hs2", [128, 2], F32))
            t_h = k.dma(SY, hs2[:], f_hscol, inc="ld")
            t_z = k.op(G, lambda e: e.memset(cb[:, :, 0:1], 0.0))
            t_z2 = k.dma(SY, fm_view(ctxb)[:, :, 0:1], cb[:, :, 0:1],
                         wait=[t_z], inc="st", slow=True)
            prev = t_z2
            aglo = ag2[0:DM, :].rearrange("(c p) t -> p c t", p=128)
            for j in range(4):
                sl = slice(j * 512, j * 512 + 512)
                t_l = k.dma(SY, cb[:], aglo[:, :, sl], wait=[prev, tcc],
                            inc="ld")
                t_m = k.op(V, lambda e: e.tensor_scalar(
                    out=cb[:], in0=cb[:], scalar1=hs2[:, 0:1], scalar2=None,
                    op0=ALU.mult), wait=[t_l, t_h])
                prev = k.dma(SY, fm_view(ctxb)[:, :, 1 + j * 512:1 + j * 512 + 512],
                             cb[:], wait=[t_m], inc="st")
            for j in range(4):
                sl = slice(j * 512, j * 512 + 512)
                t_l = k.dma(SY, cb[:], fm_view(xb[1])[:, :, sl], wait=[prev],
                            inc="ld")
                prev = k.dma(SY, fm_view(ctxb)[:, :, 2049 + j * 512:
                                               2049 + j * 512 + 512],
                             cb[:], wait=[t_l], inc="st")
        k.barrier()
        with ExitStack() as ps:
            emit_l0ffn(nc, k, ps, ctxb, f_wqkv, f_wo, f_w1, f_w2, f_lrows,
                       f_b1r, f_qkcol, f_lgbi, f_Cq, f_Sq, f_Ck, f_Sk, f_Psw,
                       f_E8i, f_E8Ti, f_eyef, f_mbk, f_hscol, xb[2],
                       wqkv_b=fc_wqkv, wo_b=fc_wo, w1_b=fc_w1, w2_b=fc_w2,
                       ctok=ctok)
        k.barrier()
        with ExitStack() as ps:
            emit_moe(nc, k, ps, xb[2], m_rwT, m_eb, m_w1, m_w3, m_w2, m_sw1,
                     m_sw3, m_sw2, f_eyef, xb[3], cwscr, w1_b=mc_w1, w3_b=mc_w3,
                     w2_b=mc_w2, sw1_b=mc_sw1, sw3_b=mc_sw3, sw2_b=mc_sw2,
                     ctok=mtok)
        k.barrier()
        tcc2 = k.op(G, lambda e: e.collective_compute(
            "AllGather", ALU.bypass, replica_groups=rg,
            ins=[xb[3].opt()], outs=[ag4.opt()]), inc="ccB")
        with ExitStack() as ps:
            emit_l1attn(nc, k, ps, xb[3], xb[3], t_Wq, t_Wk, t_Wv, t_Wo,
                        t_vecs, f_eyef, xb[4], row_read=mk_rowread(ag4, tcc2))
        k.barrier()
        with ExitStack() as ps:
            emit_moe(nc, k, ps, xb[4], m_rwT, m_eb, m_w1, m_w3, m_w2, m_sw1,
                     m_sw3, m_sw2, f_eyef, yout, cwscr, w1_b=mc_w1, w3_b=mc_w3,
                     w2_b=mc_w2, sw1_b=mc_sw1, sw3_b=mc_sw3, sw2_b=mc_sw2,
                     ctok=mtok)
        k.raw_wait("sync", "st", k.cnt["st"])
        k.finalize()
    return nc


def run_fused(nc_z, x, W):
    shards = _shards_fm(x)
    base = {
        "f_wqkv": np.asarray(W["L0_ffn_wqkv"]), "f_wo": np.asarray(W["L0_ffn_wo"]),
        "f_w1": np.asarray(W["L0_ffn_w1"]), "f_w2": np.asarray(W["L0_ffn_w2"]),
        "f_b1r": np.asarray(W["L0_ffn_b1"]).astype(np.float32),
        "a_Wq": np.asarray(W["L0_attn_Wq"]), "a_Wk": np.asarray(W["L0_attn_Wk"]),
        "a_Wv": np.asarray(W["L0_attn_Wv"]), "a_Wo": np.asarray(W["L0_attn_Wo"]),
        "m_rwT": np.ascontiguousarray(np.asarray(W["L1_ffn_router_w"]).T),
        "m_eb": np.asarray(W["L1_ffn_experts_bias"]),
        "m_w1": np.asarray(W["L1_ffn_w1"]), "m_w3": np.asarray(W["L1_ffn_w3"]),
        "m_w2": np.asarray(W["L1_ffn_w2"]),
        "m_sw1": np.asarray(W["L1_ffn_sw1"])[0],
        "m_sw3": np.asarray(W["L1_ffn_sw3"])[0],
        "m_sw2": np.asarray(W["L1_ffn_sw2"])[0],
        "t_Wq": np.asarray(W["L1_attn_Wq"]), "t_Wk": np.asarray(W["L1_attn_Wk"]),
        "t_Wv": np.asarray(W["L1_attn_Wv"]), "t_Wo": np.asarray(W["L1_attn_Wo"]),
    }
    base["f_lrows"] = np.stack([np.asarray(W["L0_ffn_ln_g"]),
                                np.asarray(W["L0_ffn_ln_b"]),
                                np.asarray(W["L0_ffn_b2"])], axis=1).astype(np.float32)
    qs, ks = np.asarray(W["L0_ffn_q_scale"]), np.asarray(W["L0_ffn_k_scale"])
    base["f_qkcol"] = np.stack([np.stack([np.tile(qs[l], 2), np.tile(ks[l], 2)])
                                for l in range(2)]).astype(np.float32)
    base["f_lgbi"] = np.ascontiguousarray(np.stack(
        [np.asarray(W["L0_ffn_ln_g"])[0].reshape(DC, 128).T,
         np.asarray(W["L0_ffn_ln_g"])[1].reshape(DC, 128).T,
         np.asarray(W["L0_ffn_ln_b"])[0].reshape(DC, 128).T,
         np.asarray(W["L0_ffn_ln_b"])[1].reshape(DC, 128).T], axis=2)).astype(np.float32)
    base["a_brows"] = np.stack([np.asarray(W["L0_attn_bq"]),
                                np.asarray(W["L0_attn_bk"]),
                                np.asarray(W["L0_attn_bv"]),
                                np.asarray(W["L0_attn_bo"])]).astype(np.float32)
    base["t_vecs"] = np.stack([
        W["L1_attn_gq"], W["L1_attn_bq"], W["L1_attn_mq"], W["L1_attn_vq"],
        W["L1_attn_gk"], W["L1_attn_bk"], W["L1_attn_mk"], W["L1_attn_vk"],
        W["L1_attn_gv"], W["L1_attn_bv"], W["L1_attn_mv"], W["L1_attn_vv"],
        W["L1_attn_bo"]]).astype(np.float32)
    in_maps = []
    for c in range(NCORES):
        r, h = c // 2, c % 2
        s = h * OWN
        xrT = x[r].T
        ctx = np.zeros((DM, 4097), np.float32)
        lo = s - 2049
        src_lo = max(lo, 0)
        ctx[:, src_lo - lo:] = xrT[:, src_lo:s + OWN]
        m = dict(base, xc0=ctx)
        cst = _l0ffn_host_consts(c)
        m.update({"f_" + kk: vv for kk, vv in cst.items()})
        tcolv = (s + np.arange(OWN).reshape(16, 128).T).astype(np.float32)
        m["a_tcol"] = np.ascontiguousarray(tcolv)
        in_maps.append(m)
    res = run_bass_kernel_spmd(nc_z, in_maps, core_ids=list(range(NCORES)))
    return _assemble_fm([np.asarray(res.results[c]["yout"]) for c in range(NCORES)])


_NC_CACHE = {}


def _get_nc(name, builder):
    if name not in _NC_CACHE:
        _NC_CACHE[name] = builder()
    return _NC_CACHE[name]


def kernel(**inputs):
    x = np.asarray(inputs["L0_attn_x"], np.float32)
    nc_z = _get_nc("fused", build_fused)
    return run_fused(nc_z, x, inputs)
